# version 02
# baseline (speedup 1.0000x reference)
import numpy as np
import ml_dtypes
import concourse.bass as bass
import concourse.mybir as mybir
from concourse.bass_utils import run_bass_kernel_spmd

F32 = mybir.dt.float32
BF16 = mybir.dt.bfloat16
U8 = mybir.dt.uint8
AF = mybir.ActivationFunctionType
ALU = mybir.AluOpType
AX = mybir.AxisListType

B = 4
T = 2048
D = 1024
DC = D // 128
NCORES = 8

ENGS = ("sync", "tensor", "vector", "scalar", "gpsimd")

STAGES = ["sl1", "sl2", "sl3", "sl4", "sl5", "sl6", "sl7", "sl8"]

DEBUG = False
SL6_LIMIT = 99
SL6_SUB = 99

NOW = ()


def dbg_f(ctx, slot, tile, waits=()):
    if getattr(ctx, "DBGF", None) is not None:
        p, f = tile.shape[0], int(np.prod(tile.shape[1:]))
        ctx.em.sdma(ctx.DBGF[slot][0:p, 0:f], tile, waits=waits)


def dbg_b(ctx, slot, tile, waits=()):
    if getattr(ctx, "DBGB", None) is not None:
        p, f = tile.shape[0], int(np.prod(tile.shape[1:]))
        ctx.em.sdma(ctx.DBGB[slot][0:p, 0:f], tile, waits=waits)


class Em:
    def __init__(self, nc):
        self.nc = nc
        self.q = {e: [] for e in ENGS}
        self.sems = {}
        self.cnt = {}
        for name in ("S", "G", "T", "V", "C", "P", "CC"):
            self.sems[name] = nc.alloc_semaphore("sem_" + name)
            self.cnt[name] = 0

    def op(self, eng, build, sem=None, waits=(), inc=1):
        h = self.sems
        for w in waits:
            if w is None:
                continue
            sn, sv = w
            if sv and sv > 0:
                self.q[eng].append(lambda e, hh=h[sn], sv=sv: e.wait_ge(hh, sv))
        if sem in ("V", "C", "P") and self.cnt[sem] > 0:
            self.q[eng].append(
                lambda e, hh=h[sem], sv=self.cnt[sem]: e.wait_ge(hh, sv))
        if sem is None:
            self.q[eng].append(lambda e, b=build: b(e))
            return None
        self.cnt[sem] += inc
        self.q[eng].append(
            lambda e, b=build, hh=h[sem], k=inc: b(e).then_inc(hh, k))
        return (sem, self.cnt[sem])

    def barrier(self):
        snap = [(k, v) for k, v in self.cnt.items() if v > 0]
        for eng in ENGS:
            for sn, sv in snap:
                self.q[eng].append(
                    lambda e, hh=self.sems[sn], sv=sv: e.wait_ge(hh, sv))

    def t(self, build, waits=()):
        return self.op("tensor", build, "T", waits)

    def v(self, build, waits=()):
        return self.op("vector", build, "V", waits)

    def c(self, build, waits=()):
        return self.op("scalar", build, "C", waits)

    def p(self, build, waits=()):
        return self.op("gpsimd", build, "P", waits)

    def sdma(self, out, in_, waits=(), **kw):
        return self.op("sync", lambda e: e.dma_start(out=out, in_=in_, **kw),
                       "S", waits, inc=16)

    def gdma(self, out, in_, waits=(), **kw):
        return self.op("gpsimd", lambda e: e.dma_start(out=out, in_=in_, **kw),
                       "G", waits, inc=16)


class Arena:
    _uid = 0

    def __init__(self, nc, name, nbytes):
        self.nc = nc
        self.h = nc.alloc_sbuf_tensor(name, [128, nbytes], U8)
        self.base = nc.lookup_mloc(self.h).addr
        self.off = 0
        self.peak = 0
        self.nbytes = nbytes

    def tile(self, shape, dtype):
        per_part = int(np.prod(shape[1:])) * mybir.dt.size(dtype)
        off = (self.off + 31) & ~31
        assert off + per_part <= self.nbytes, \
            f"arena overflow: {off}+{per_part} > {self.nbytes}"
        self.off = off + per_part
        self.peak = max(self.peak, self.off)
        Arena._uid += 1
        h = self.nc.alloc_sbuf_tensor_at(
            f"a{Arena._uid}", list(shape), dtype, offset=self.base + off)
        return h.ap()

    def tiles(self, n, shape, dtype):
        return [self.tile(shape, dtype) for _ in range(n)]

    def mark(self):
        return self.off

    def reset(self, off):
        self.off = off


def _pp(bvec, pad_to=None):
    b = np.asarray(bvec, np.float32).reshape(-1)
    if pad_to is not None and b.size < pad_to:
        b = np.concatenate([b, np.zeros(pad_to - b.size, np.float32)])
    n = b.size
    assert n % 128 == 0
    return np.ascontiguousarray(b.reshape(n // 128, 128).T)


def ln_x_norm(ctx, src, xn, rows_out=None):
    em, ar = ctx.em, ctx.ar
    P4a, P4b = ctx.P4a, ctx.P4b
    mk = ar.mark()
    XF = ar.tiles(2, [128, T], F32)
    XB = ar.tiles(DC, [128, T], BF16)
    SQ = ar.tiles(2, [128, T], BF16)
    ROWS = ar.tile([1, 3 * T], F32)
    ROWm = ROWS[:, 0:T]
    ROWmsq = ROWS[:, T:2 * T]
    ROWvar = ROWS[:, 2 * T:3 * T]
    ROWsd = ROWS[:, T:2 * T]
    ROWr = ROWS[:, 2 * T:3 * T]
    ROWSB = ar.tile([1, 2 * T], BF16)
    ROWrb = ROWSB[:, 0:T]
    ROWmrb = ROWSB[:, T:2 * T]
    RBB = ar.tile([128, T], BF16)
    MRB = ar.tile([128, T], BF16)

    ld = [None, None]
    sq_guard = [None, None]
    cast_done = [None] * DC
    last_mm = None
    for dc in range(DC):
        r = dc % 2
        ld[r] = em.sdma(XF[r], src(dc),
                        waits=[cast_done[dc - 2]] if dc >= 2 else ())
        cast_done[dc] = em.c(
            lambda e, o=XB[dc], i=XF[r]: e.activation(o, i, AF.Copy),
            waits=[ld[r]])
        sq = em.v(lambda e, o=SQ[r], i=XB[dc]: e.tensor_tensor(o, i, i, ALU.mult),
                  waits=[cast_done[dc], sq_guard[r]])
        for ns in range(4):
            em.t(lambda e, o=P4b[0:1, 512 * ns:512 * (ns + 1)], i=XB[dc],
                 ns=ns, st=(dc == 0), sp=(dc == DC - 1):
                 e.matmul(o, ctx.ONESC, i[:, 512 * ns:512 * (ns + 1)],
                          start=st, stop=sp),
                 waits=[cast_done[dc]] if ns == 0 else ())
        for ns in range(4):
            last_mm = em.t(lambda e, o=P4a[0:1, 512 * ns:512 * (ns + 1)], i=SQ[r],
                           ns=ns, st=(dc == 0), sp=(dc == DC - 1):
                           e.matmul(o, ctx.ONESC, i[:, 512 * ns:512 * (ns + 1)],
                                    start=st, stop=sp),
                           waits=[sq] if ns == 0 else ())
        sq_guard[r] = last_mm

    v1 = em.v(lambda e: e.tensor_scalar(ROWm, P4b[0:1, :], 1.0 / D, None, ALU.mult),
              waits=[last_mm])
    em.v(lambda e: e.tensor_tensor(ROWmsq, ROWm, ROWm, ALU.mult))
    v3 = em.v(lambda e: e.scalar_tensor_tensor(ROWvar, P4a[0:1, :], 1.0 / D,
                                               ROWmsq, ALU.mult, ALU.subtract))
    c1 = em.c(lambda e: e.activation(ROWsd, ROWvar, AF.Sqrt, bias=1e-5),
              waits=[v3])
    em.v(lambda e: e.reciprocal(ROWr, ROWsd), waits=[c1])
    v4 = em.v(lambda e: e.tensor_copy(ROWrb, ROWr))
    v5 = em.v(lambda e: e.tensor_tensor(ROWmrb, ROWm, ROWr, ALU.mult))
    bc_r = None
    for ns in range(4):
        bc_r = em.t(lambda e, ns=ns: e.matmul(
            P4a[:, 512 * ns:512 * (ns + 1)], ctx.ONESR,
            ROWrb[:, 512 * ns:512 * (ns + 1)], start=True, stop=True),
            waits=[v4, v1] if ns == 0 else ())
    bc_mr = None
    for ns in range(4):
        bc_mr = em.t(lambda e, ns=ns: e.matmul(
            P4b[:, 512 * ns:512 * (ns + 1)], ctx.ONESR,
            ROWmrb[:, 512 * ns:512 * (ns + 1)], start=True, stop=True),
            waits=[v5] if ns == 0 else ())
    e1 = em.c(lambda e: e.activation(RBB, P4a[:, :], AF.Copy), waits=[bc_r])
    e2 = em.c(lambda e: e.activation(MRB, P4b[:, :], AF.Copy), waits=[bc_mr])
    if rows_out is not None:
        em.v(lambda e, o=rows_out[0]: e.tensor_copy(o, ROWm))
        em.v(lambda e, o=rows_out[1]: e.tensor_copy(o, ROWr))
    vdone = None
    for dc in range(DC):
        em.v(lambda e, o=xn[dc], a=XB[dc]: e.tensor_tensor(o, a, RBB, ALU.mult),
             waits=[e1, e2] if dc == 0 else ())
        vdone = em.v(lambda e, o=xn[dc]: e.tensor_tensor(o, o, MRB, ALU.subtract))
    ar.reset(mk)
    return vdone


def load_w_cast(ctx, hbm_ap, kdim, mdim, waits=()):
    kc = kdim // 128
    w = ctx.ar.tile([128, kc * mdim], BF16)
    src = hbm_ap.rearrange("(kc p) m -> p kc m", p=128)
    dst = w.rearrange("p (kc m) -> p kc m", kc=kc)
    g = ctx.em.gdma(dst, src, waits=waits)
    return w, g


def stage_sl5(ctx, src_of, dst_of):
    em, ar = ctx.em, ctx.ar
    P4a, P4b = ctx.P4a, ctx.P4b
    P4a_bf = ctx.P4a_bf
    stage_mk = ar.mark()

    BIAS = ar.tile([128, 32], F32)
    em.sdma(BIAS[:, 0:8], ctx.inp["L1_attn_a_bq_pp"])
    em.sdma(BIAS[:, 8:16], ctx.inp["L1_attn_a_bk_pp"])
    em.sdma(BIAS[:, 16:24], ctx.inp["L1_attn_a_bv_pp"])
    em.sdma(BIAS[:, 24:32], ctx.inp["L1_attn_a_bo_pp"])
    em.barrier()

    for b in range(B):
        mkb = ar.mark()
        XNOT = ar.tiles(DC, [128, T], BF16)

        ln_x_norm(ctx, lambda dc: src_of(b, dc), XNOT)
        em.barrier()

        QT = ar.tiles(DC, [128, T], BF16)
        KT = ar.tiles(DC, [128, T], BF16)
        VT = ar.tiles(16, [128, D], BF16)

        mkp = ar.mark()
        WQ, _ = load_w_cast(ctx, ctx.inp["L1_attn_a_wq"], D, D)
        WK, _ = load_w_cast(ctx, ctx.inp["L1_attn_a_wk"], D, D)
        WV, gv = load_w_cast(ctx, ctx.inp["L1_attn_a_wv"], D, D)
        pg = {"a": None, "b": None}
        for wi, (Wt, bofs, OUTT) in enumerate(
                ((WQ, 0, QT), (WK, 8, KT))):
            for cc in range(DC):
                key = "a" if cc % 2 == 0 else "b"
                ps = P4a if key == "a" else P4b
                last = None
                for ns in range(4):
                    for dc in range(DC):
                        last = em.t(
                            lambda e, o=ps[:, 512 * ns:512 * (ns + 1)],
                            l=Wt[:, dc * D + 128 * cc: dc * D + 128 * cc + 128],
                            r=XNOT[dc][:, 512 * ns:512 * (ns + 1)],
                            st=(dc == 0), sp=(dc == DC - 1):
                            e.matmul(o, l, r, start=st, stop=sp),
                            waits=[pg[key], gv] if (ns == 0 and dc == 0) else ())
                pg[key] = em.c(
                    lambda e, o=OUTT[cc], i=ps, bb=BIAS[:, bofs + cc:bofs + cc + 1]:
                    e.activation(o, i, AF.Identity, bias=bb),
                    waits=[last])
        for tc in range(16):
            key = "a" if tc % 2 == 0 else "b"
            ps = (P4a if key == "a" else P4b)[:, 0:1024]
            last = None
            for ns in range(2):
                for dc in range(DC):
                    last = em.t(
                        lambda e, o=ps[:, 512 * ns:512 * (ns + 1)],
                        l=XNOT[dc][:, 128 * tc:128 * (tc + 1)],
                        r=WV[:, dc * D + 512 * ns: dc * D + 512 * ns + 512],
                        st=(dc == 0), sp=(dc == DC - 1):
                        e.matmul(o, l, r, start=st, stop=sp),
                        waits=[pg[key]] if (ns == 0 and dc == 0) else ())
            pg[key] = em.c(lambda e, o=VT[tc], i=ps: e.activation(o, i, AF.Copy),
                           waits=[last])
        em.barrier()

        ar.reset(mkp)
        SM = ar.tiles(2, [128, T], F32)
        MK = ar.tiles(2, [128, T], U8)
        ATTN = ar.tiles(4, [128, T], BF16)
        ATTS = ar.tile([128, 16 * 512], BF16)
        NM = ar.tile([128, 32], F32)
        RS = ar.tile([128, 16], F32)
        RR = ar.tile([128, 16], F32)
        OT = XNOT

        sm_guard = [None, None]
        mk_guard = [None, None]
        attn_guard = [None] * 4
        ev_guard = None
        pg = {"a": None, "b": None}
        for sl in range(4):
            softmax_done = [None] * 4
            for i in range(4):
                tc = 4 * sl + i
                key = "a" if i % 2 == 0 else "b"
                ps = P4a if key == "a" else P4b
                r2 = i % 2
                ml = em.sdma(MK[r2], ctx.inp["mask5"][tc], waits=[mk_guard[r2]])
                pmem = em.p(lambda e, o=SM[r2]: e.memset(o, -3.2e10),
                            waits=[sm_guard[r2]])
                last = None
                for ns in range(4):
                    for cc in range(DC):
                        last = em.t(
                            lambda e, o=ps[:, 512 * ns:512 * (ns + 1)],
                            l=QT[cc][:, 128 * tc:128 * (tc + 1)],
                            r=KT[cc][:, 512 * ns:512 * (ns + 1)],
                            st=(cc == 0), sp=(cc == DC - 1):
                            e.matmul(o, l, r, start=st, stop=sp),
                            waits=[pg[key]] if (ns == 0 and cc == 0) else ())
                mk_guard[r2] = em.v(
                    lambda e, o=SM[r2], mk=MK[r2], i_=ps:
                    e.copy_predicated(o, mk, i_),
                    waits=[last, pmem, ml])
                pg[key] = mk_guard[r2]
                vm = em.v(lambda e, o=NM[:, tc:tc + 1], i_=SM[r2]:
                          e.reduce_max(o, i_, axis=AX.X, negate=True))
                vm2 = em.c(lambda e, o=NM[:, 16 + tc:17 + tc], i_=NM[:, tc:tc + 1]:
                           e.mul(o, i_, 1.0 / 32.0), waits=[vm])
                cx = em.c(lambda e, o=ATTN[i], i_=SM[r2], tc=tc:
                          e.activation(o, i_, AF.Exp, scale=1.0 / 32.0,
                                       bias=NM[:, 16 + tc:17 + tc],
                                       accum_out=RS[:, tc:tc + 1]),
                          waits=[vm2, attn_guard[i]])
                sm_guard[r2] = cx
                vr = em.v(lambda e, o=RR[:, tc:tc + 1], i_=RS[:, tc:tc + 1]:
                          e.reciprocal(o, i_), waits=[cx])
                softmax_done[i] = em.v(
                    lambda e, o=ATTN[i], tc=tc:
                    e.tensor_scalar(o, o, RR[:, tc:tc + 1], None, ALU.mult),
                    waits=[vr])
            for sc in range(16):
                lastt = None
                for i in range(4):
                    lastt = em.t(
                        lambda e, o=P4a_bf[:, 128 * i:128 * (i + 1)],
                        i_=ATTN[i][:, 128 * sc:128 * (sc + 1)]:
                        e.transpose(o, i_, ctx.IDENT),
                        waits=([softmax_done[i], ev_guard, pg["a"]]
                               if i == 0 else [softmax_done[i]]))
                ev_guard = em.v(
                    lambda e, o=ATTS[:, 512 * sc:512 * (sc + 1)],
                    i_=P4a_bf[:, 0:512]:
                    e.tensor_copy(o, i_),
                    waits=[lastt])
                attn_guard = [lastt] * 4 if sc == 15 else attn_guard
            lastv = None
            for cc in range(DC):
                half = P4a if cc < 4 else P4b
                psb = half[:, 512 * (cc % 4):512 * (cc % 4 + 1)]
                for sc in range(16):
                    lastv = em.t(
                        lambda e, o=psb, l=VT[sc][:, 128 * cc:128 * (cc + 1)],
                        r=ATTS[:, 512 * sc:512 * (sc + 1)],
                        st=(sc == 0), sp=(sc == 15):
                        e.matmul(o, l, r, start=st, stop=sp),
                        waits=[ev_guard, pg["a"], pg["b"]]
                        if (sc == 0 and cc == 0) else ())
            ce = None
            for cc in range(DC):
                half = P4a if cc < 4 else P4b
                psb = half[:, 512 * (cc % 4):512 * (cc % 4 + 1)]
                ce = em.c(lambda e, o=OT[cc][:, 512 * sl:512 * (sl + 1)], i_=psb,
                          bb=BIAS[:, 16 + cc:16 + cc + 1]:
                          e.activation(o, i_, AF.Identity, bias=bb),
                          waits=[lastv] if cc == 0 else ())
            pg["a"] = ce
            pg["b"] = ce
            ev_guard = None
        em.barrier()

        ar.reset(mkp)
        WO, gw = load_w_cast(ctx, ctx.inp["L1_attn_a_wo"], D, D)
        XF = ar.tiles(2, [128, T], F32)
        XNEW = ar.tiles(2, [128, T], F32)
        st_guard = [None, None]
        xf_guard = [None, None]
        pgv = {"a": None, "b": None}
        for dc in range(DC):
            r = dc % 2
            key = "a" if dc % 2 == 0 else "b"
            ps = P4a if key == "a" else P4b
            ldx = em.sdma(XF[r], src_of(b, dc), waits=[xf_guard[r]])
            last = None
            for ns in range(4):
                for cc in range(DC):
                    last = em.t(
                        lambda e, o=ps[:, 512 * ns:512 * (ns + 1)],
                        l=WO[:, cc * D + 128 * dc: cc * D + 128 * dc + 128],
                        r=OT[cc][:, 512 * ns:512 * (ns + 1)],
                        st=(cc == 0), sp=(cc == DC - 1):
                        e.matmul(o, l, r, start=st, stop=sp),
                        waits=[pgv[key], gw] if (ns == 0 and cc == 0) else ())
            vres = em.v(lambda e, o=XNEW[r], i_=ps,
                        bb=BIAS[:, 24 + dc:24 + dc + 1], x=XF[r]:
                        e.scalar_tensor_tensor(o, i_, bb, x, ALU.add, ALU.add),
                        waits=[last, ldx, st_guard[r]])
            pgv[key] = vres
            xf_guard[r] = vres
            st_guard[r] = em.sdma(dst_of(b, dc), XNEW[r], waits=[vres])
        em.barrier()
        ar.reset(mkb)
    ar.reset(stage_mk)


def stage_sl1(ctx, src_of, dst_of):
    em, ar = ctx.em, ctx.ar
    P4a, P4b = ctx.P4a, ctx.P4b
    P4a_bf = ctx.P4a_bf
    stage_mk = ar.mark()
    D4 = 4096

    BIN = ar.tile([64, 64], F32)
    em.sdma(BIN, ctx.inp["L0_b_in_64"])
    BOUT = ar.tile([128, 8], F32)
    em.sdma(BOUT, ctx.inp["L0_b_out_pp"])
    em.barrier()

    for b in range(B):
        mkb = ar.mark()
        XN = ar.tiles(DC, [128, T], BF16)
        ln_x_norm(ctx, lambda dc: src_of(b, dc), XN)
        em.barrier()
        WIN, gwin = load_w_cast(ctx, ctx.inp["L0_W_in"], D, D4)
        WOUT, gwout = load_w_cast(ctx, ctx.inp["L0_W_out"], D, D)

        mkh = ar.mark()
        QT_ = ar.tile([64, T], BF16)
        KT_ = ar.tile([64, T], BF16)
        VT_ = ar.tile([64, T], BF16)
        GT_ = ar.tile([64, T], BF16)
        QN2 = ar.tile([64, T], BF16)
        SQX = ar.tile([64, T], BF16)
        Q2ROW = ar.tile([1, T], BF16)
        K2S = ar.tile([128, 16], F32)
        Q2B = ar.tile([128, T], BF16)
        VTM = ar.tile([128, 16, 68], BF16)
        WT = ar.tiles(2, [128, T], BF16)
        W1F = ar.tile([128, T], F32)
        W2F = ar.tile([128, T], F32)
        RS1a = W1F[0:1, :]
        RS1b = W2F[0:1, :]
        RSRB = Q2ROW
        HH = ar.tile([64, T], BF16)

        pmemset = em.p(lambda e: e.memset(VTM, 1.0))
        hg = {"q": None, "k": None, "v": None, "g": None, "hh": None,
              "pa": None, "pb": None}
        for h in range(16):
            for qi, (OUTT, gk) in enumerate(((QT_, "q"), (KT_, "k"),
                                             (VT_, "v"), (GT_, "g"))):
                key = "pa" if qi % 2 == 0 else "pb"
                ps = P4a if qi % 2 == 0 else P4b
                last = None
                for ns in range(4):
                    for dc in range(DC):
                        base = dc * D4 + 256 * h + 64 * qi
                        last = em.t(
                            lambda e, o=ps[0:64, 512 * ns:512 * (ns + 1)],
                            l=WIN[:, base:base + 64],
                            r_=XN[dc][:, 512 * ns:512 * (ns + 1)],
                            st=(dc == 0), sp=(dc == DC - 1):
                            e.matmul(o, l, r_, start=st, stop=sp),
                            waits=([gwin, hg[key], hg[gk]]
                                   if (ns == 0 and dc == 0) else ()))
                hg[gk] = em.c(
                    lambda e, o=OUTT, i_=ps[0:64, :],
                    bb=BIN[:, 4 * h + qi:4 * h + qi + 1]:
                    e.activation(o, i_, AF.Identity, bias=bb),
                    waits=[last])
                hg[key] = hg[gk]
            vq2 = em.v(lambda e, o=QN2, i_=QT_:
                       e.tensor_scalar(o, i_, -2.0, None, ALU.mult),
                       waits=[hg["q"]])
            vsqq = em.v(lambda e, o=SQX, i_=QT_:
                        e.tensor_tensor(o, i_, i_, ALU.mult))
            lastq = None
            for ns in range(4):
                lastq = em.t(lambda e, o=P4a[0:1, 512 * ns:512 * (ns + 1)],
                             i_=SQX, ns=ns:
                             e.matmul(o, ctx.ONESC[0:64, :],
                                      i_[:, 512 * ns:512 * (ns + 1)],
                                      start=True, stop=True),
                             waits=[vsqq, hg["pa"]] if ns == 0 else ())
            eq2 = em.v(lambda e, o=Q2ROW, i_=P4a[0:1, :]:
                       e.tensor_copy(o, i_), waits=[lastq])
            vsqk = em.v(lambda e, o=SQX, i_=KT_:
                        e.tensor_tensor(o, i_, i_, ALU.mult),
                        waits=[hg["k"], eq2])
            lastk = None
            for sc in range(16):
                lastk = em.t(lambda e, o=P4a[0:128, 16 + sc:17 + sc],
                             l=SQX[:, 128 * sc:128 * (sc + 1)]:
                             e.matmul(o, l, ctx.ONESC[0:64, :],
                                      start=True, stop=True),
                             waits=[vsqk, eq2] if sc == 0 else ())
            ek2 = em.v(lambda e, o=K2S, i_=P4a[:, 16:32]:
                       e.tensor_copy(o, i_), waits=[lastk])
            lastq2b = None
            for ns in range(4):
                lastq2b = em.t(lambda e, ns=ns: e.matmul(
                    P4a[:, 512 * ns:512 * (ns + 1)], ctx.ONESR,
                    Q2ROW[:, 512 * ns:512 * (ns + 1)], start=True, stop=True),
                    waits=[ek2] if ns == 0 else ())
            eq2b = em.c(lambda e, o=Q2B, i_=P4a: e.activation(o, i_, AF.Copy),
                        waits=[lastq2b])

            lastvt = None
            for sc in range(16):
                lastvt = em.t(lambda e, o=P4a_bf[:, 2048 + 64 * sc:
                                               2048 + 64 * (sc + 1)],
                              i_=VT_[:, 128 * sc:128 * (sc + 1)]:
                              e.transpose(o, i_, ctx.IDENT[0:64, 0:64]),
                              waits=([hg["v"], pmemset, hg["hh"], eq2b]
                                     if sc == 0 else ()))
            evtm = em.v(lambda e, o=VTM[:, :, 0:64],
                        i_=P4a_bf[:, 2048:3072].rearrange(
                            "p (s c) -> p s c", c=64):
                        e.tensor_copy(o, i_), waits=[lastvt])
            pg_d2 = None
            wt_guard = [None, None]
            w1_guard = None
            w2_guard = None
            lastA = None
            for sc in range(16):
                rwt = sc % 2
                lastd = None
                for ns in range(4):
                    lastd = em.t(
                        lambda e, o=P4a[:, 512 * ns:512 * (ns + 1)],
                        l=KT_[:, 128 * sc:128 * (sc + 1)],
                        r_=QN2[:, 512 * ns:512 * (ns + 1)]:
                        e.matmul(o, l, r_, start=True, stop=True),
                        waits=([vq2, eq2b, evtm, pg_d2]
                               if (sc == 0 and ns == 0) else
                               ([pg_d2] if ns == 0 else ())))
                vd2 = em.v(lambda e, o=W1F, i_=P4a, k2=K2S[:, sc:sc + 1]:
                           e.scalar_tensor_tensor(o, i_, k2, Q2B,
                                                  ALU.add, ALU.add),
                           waits=[lastd, w1_guard])
                pg_d2 = vd2
                pr = em.c(lambda e, o=W2F, i_=W1F:
                          e.activation(o, i_, AF.Relu),
                          waits=[vd2, w2_guard])
                cs = em.c(lambda e, o=W2F, i_=W2F:
                          e.activation(o, i_, AF.Sqrt), waits=[pr])
                cq = em.c(lambda e, o=W1F, i_=W2F:
                          e.activation(o, i_, AF.Square, bias=1e-4))
                w2_guard = cq
                vrcp = em.v(lambda e, o=WT[rwt], i_=W1F:
                            e.reciprocal(o, i_), waits=[cq, wt_guard[rwt]])
                w1_guard = vrcp

                for ns in range(4):
                    lastA = em.t(
                        lambda e, o=P4b[0:65, 512 * ns:512 * (ns + 1)],
                        l=VTM[:, sc, 0:65].squeeze(),
                        r_=WT[rwt][:, 512 * ns:512 * (ns + 1)],
                        st=(sc == 0), sp=(sc == 15):
                        e.matmul(o, l, r_, start=st, stop=sp),
                        waits=([vrcp, hg["pb"]] if sc == 0 and ns == 0
                               else ([vrcp] if ns == 0 else ())))
                wt_guard[rwt] = lastA
            v1 = em.v(lambda e: e.tensor_scalar(RS1a, P4b[64:65, :],
                                                1e-4, None, ALU.add),
                      waits=[lastA, w1_guard])
            em.v(lambda e: e.reciprocal(RS1b, RS1a))
            vrb = em.v(lambda e: e.tensor_copy(RSRB, RS1b))
            lastrb = None
            for ns in range(4):
                lastrb = em.t(lambda e, ns=ns: e.matmul(
                    P4a[0:64, 512 * ns:512 * (ns + 1)], ctx.ONESR[:, 0:64],
                    RSRB[:, 512 * ns:512 * (ns + 1)], start=True, stop=True),
                    waits=[vrb, pg_d2] if ns == 0 else ())
            vh1 = em.v(lambda e, o=W2F[0:64, :]: e.tensor_tensor(
                o, P4b[0:64, :], GT_, ALU.mult),
                waits=[lastrb, hg["g"], w2_guard])
            vh2 = em.v(lambda e, o=HH, i_=W2F[0:64, :]:
                       e.tensor_tensor(o, i_, P4a[0:64, :], ALU.mult),
                       waits=[hg.get("sh")])
            sh = em.sdma(ctx.HS[h], HH, waits=[vh2])
            if b == 0 and h < 12:
                dbg_b(ctx, h, HH, waits=[vh2])
            hg["hh"] = vh2
            hg["pa"] = vh2
            hg["pb"] = vh1
            hg["q"] = vh2
            hg["k"] = vh2
            hg["v"] = vh2
            hg["g"] = vh2
            w1_guard = vh2
            w2_guard = vh2
            hg["sh"] = sh
        em.barrier()

        ar.reset(mkh)
        HTL = ar.tiles(DC, [128, T], BF16)
        XF = ar.tiles(2, [128, T], F32)
        XNEW = ar.tiles(2, [128, T], F32)
        hl = None
        for cc in range(DC):
            hl = em.sdma(HTL[cc], ctx.HS.rearrange("h p t -> (h p) t")
                         [128 * cc:128 * (cc + 1), :])
        st_guard = [None, None]
        xf_guard = [None, None]
        pgo = {"a": None, "b": None}
        for dc in range(DC):
            r = dc % 2
            key = "a" if dc % 2 == 0 else "b"
            ps = P4a if key == "a" else P4b
            ldx = em.sdma(XF[r], src_of(b, dc), waits=[xf_guard[r]])
            last = None
            for ns in range(4):
                for cc in range(DC):
                    last = em.t(
                        lambda e, o=ps[:, 512 * ns:512 * (ns + 1)],
                        l=WOUT[:, cc * D + 128 * dc: cc * D + 128 * dc + 128],
                        r_=HTL[cc][:, 512 * ns:512 * (ns + 1)],
                        st=(cc == 0), sp=(cc == DC - 1):
                        e.matmul(o, l, r_, start=st, stop=sp),
                        waits=([pgo[key], gwout, hl]
                               if (ns == 0 and cc == 0) else ()))
            vres = em.v(lambda e, o=XNEW[r], i_=ps,
                        bb=BOUT[:, dc:dc + 1], x=XF[r]:
                        e.scalar_tensor_tensor(o, i_, bb, x, ALU.add, ALU.add),
                        waits=[last, ldx, st_guard[r]])
            pgo[key] = vres
            xf_guard[r] = vres
            st_guard[r] = em.sdma(dst_of(b, dc), XNEW[r], waits=[vres])
        em.barrier()
        ar.reset(mkb)
    ar.reset(stage_mk)


def stage_sl3(ctx, src_of, dst_of):
    em, ar = ctx.em, ctx.ar
    P4a, P4b = ctx.P4a, ctx.P4b
    stage_mk = ar.mark()
    RWC = ar.tile([128, 8], F32)
    RB1 = ar.tile([1, 1], F32)
    em.sdma(RWC, ctx.inp["router_w_pp"])
    em.sdma(RB1, ctx.inp["router_b_11"])
    RWCB = ar.tile([128, 8], BF16)
    em.barrier()
    em.v(lambda e: e.tensor_copy(RWCB, RWC))
    ONESROW = ar.tile([1, T], BF16)
    em.v(lambda e: e.memset(ONESROW, 1.0))
    em.barrier()

    W1 = ar.tile([1, 16], F32)
    WLOC = ar.tile([128, 16], BF16)
    W128 = ar.tile([128, 2], F32)
    W128B = ar.tile([128, 1], BF16)
    em.v(lambda e: e.tensor_copy(WLOC[:, 0:8], RWC))
    em.v(lambda e: e.tensor_tensor(WLOC[:, 8:16], RWC, WLOC[:, 0:8],
                                   ALU.subtract))
    em.v(lambda e: e.reduce_sum(W128[:, 0:1], RWC, axis=AX.X))
    vW = em.v(lambda e: e.tensor_copy(W128B, W128[:, 0:1]))
    tW = em.t(lambda e: e.matmul(P4a[0:1, 0:1], ctx.ONESC, W128B,
                                 start=True, stop=True), waits=[vW])
    em.v(lambda e: e.tensor_copy(W1[0:1, 0:1], P4a[0:1, 0:1]), waits=[tW])
    em.barrier()
    for b in range(B):
        mkb = ar.mark()
        XN = ar.tiles(DC, [128, T], BF16)
        MROW = ar.tile([1, T], F32)
        RROW = ar.tile([1, T], F32)
        ln_x_norm(ctx, lambda dc: src_of(b, dc), XN, rows_out=(MROW, RROW))
        em.barrier()

        ROWS2 = ar.tile([1, 4 * T], F32)
        RW = ROWS2[:, 0:T]
        SELR = ROWS2[:, T:2 * T]
        RWB = ar.tile([1, 6 * T], BF16)
        RWHI = RWB[:, 0:T]
        RWLO = RWB[:, T:2 * T]
        NRWHI = RWB[:, 2 * T:3 * T]
        NRWLO = RWB[:, 3 * T:4 * T]
        SELB_R = RWB[:, 4 * T:5 * T]
        GT = ar.tiles(2, [128, T], BF16)
        SELB = ar.tile([128, T], BF16)
        TMP = ar.tiles(2, [128, T], F32)
        XF = ar.tiles(2, [128, T], F32)
        XNEW = ar.tiles(2, [128, T], F32)

        XH = ar.tiles(2, [128, T], BF16)
        XLO = ar.tiles(2, [128, T], BF16)
        XFR = ar.tiles(2, [128, T], F32)
        last = None
        prev = [None, None]
        for dc in range(DC):
            rr = dc % 2
            ldx = em.sdma(XFR[rr], src_of(b, dc), waits=[prev[rr]])
            ch = em.c(lambda e, o=XH[rr], i_=XFR[rr]: e.activation(o, i_, AF.Copy),
                      waits=[ldx])
            vl = em.v(lambda e, o=XLO[rr], a=XFR[rr], bb=XH[rr]:
                      e.tensor_tensor(o, a, bb, ALU.subtract), waits=[ch])
            for ns in range(4):
                lo, hi = 512 * ns, 512 * (ns + 1)
                for k, (lw, xt) in enumerate((
                        (WLOC[:, dc:dc + 1], XH[rr]),
                        (WLOC[:, 8 + dc:8 + dc + 1], XH[rr]),
                        (WLOC[:, dc:dc + 1], XLO[rr]))):
                    last = em.t(lambda e, o=P4b[0:1, lo:hi], l=lw,
                                r_=xt[:, lo:hi],
                                st=(dc == 0 and k == 0),
                                sp=(dc == DC - 1 and k == 2):
                                e.matmul(o, l, r_, start=st, stop=sp),
                                waits=[vl] if (ns == 0 and k == 0) else ())
            prev[rr] = last
        vt = em.v(lambda e: e.tensor_scalar(SELR, MROW, W1[0:1, 0:1], None,
                                            ALU.mult), waits=[last])
        em.v(lambda e: e.tensor_tensor(RW, P4b[0:1, :], SELR, ALU.subtract))
        em.v(lambda e: e.tensor_tensor(RW, RW, RROW, ALU.mult))
        em.v(lambda e: e.tensor_scalar(RW, RW, RB1[0:1, 0:1], None, ALU.add))
        em.v(lambda e: e.tensor_copy(RWHI, RW))
        em.v(lambda e: e.scalar_tensor_tensor(ROWS2[:, 2 * T:3 * T], RW, 1.0,
                                              RWHI, ALU.mult, ALU.subtract))
        em.v(lambda e: e.tensor_copy(RWLO, ROWS2[:, 2 * T:3 * T]))
        em.v(lambda e: e.tensor_scalar(NRWHI, RWHI, -1.0, None, ALU.mult))
        vsp = em.v(lambda e: e.tensor_scalar(NRWLO, RWLO, -1.0, None, ALU.mult))
        cs_last = None
        sg_guard = [None, None]
        pg_sign = None
        for ic in range(16):
            r = ic % 2
            lastd = None
            for ns in range(4):
                lo, hi = 512 * ns, 512 * (ns + 1)
                ops = [(NRWHI[:, 128 * ic:128 * (ic + 1)], ONESROW[:, lo:hi]),
                       (NRWLO[:, 128 * ic:128 * (ic + 1)], ONESROW[:, lo:hi]),
                       (ctx.ONESR, RWHI[:, lo:hi]),
                       (ctx.ONESR, RWLO[:, lo:hi])]
                for k, (lh, rh) in enumerate(ops):
                    lastd = em.t(lambda e, o=P4a[:, lo:hi], l=lh, r_=rh,
                                 st=(k == 0), sp=(k == 3):
                                 e.matmul(o, l, r_, start=st, stop=sp),
                                 waits=([vsp, pg_sign]
                                        if (ns == 0 and k == 0) else ()))
            cg = em.c(lambda e, o=GT[r], i_=P4a: e.activation(o, i_, AF.Sign),
                      waits=[lastd, sg_guard[r]])
            pg_sign = cg
            for ns in range(4):
                cs_last = em.t(lambda e, o=P4b[0:1, 512 * ns:512 * (ns + 1)],
                               i=GT[r], ns=ns, st=(ic == 0), sp=(ic == 15):
                               e.matmul(o, ctx.ONESC,
                                        i[:, 512 * ns:512 * (ns + 1)],
                                        start=st, stop=sp),
                               waits=[cg] if ns == 0 else ())
            sg_guard[r] = cs_last
        c2 = em.c(lambda e: e.activation(SELR, P4b[0:1, :], AF.Sign, bias=-0.5),
                  waits=[cs_last])
        vsel = em.v(lambda e: e.tensor_scalar(SELB_R, SELR, 0.5, 0.5,
                                              ALU.mult, ALU.add), waits=[c2])
        bsel = None
        for ns in range(4):
            bsel = em.t(lambda e, ns=ns: e.matmul(
                P4a[:, 512 * ns:512 * (ns + 1)], ctx.ONESR,
                SELB_R[:, 512 * ns:512 * (ns + 1)], start=True, stop=True),
                waits=[vsel] if ns == 0 else ())
        esel = em.c(lambda e: e.activation(SELB, P4a, AF.Copy), waits=[bsel])
        st_guard = [None, None]
        for dc in range(DC):
            r = dc % 2
            ldx = em.sdma(XF[r], src_of(b, dc), waits=[st_guard[r]])
            v1 = em.v(lambda e, o=TMP[r], a=XN[dc][:, ::-1], s=SELB:
                      e.tensor_tensor(o, a, s, ALU.mult),
                      waits=[esel, st_guard[r]])
            v2 = em.v(lambda e, o=XNEW[r], a=XF[r], bb=TMP[r]:
                      e.tensor_tensor(o, a, bb, ALU.add), waits=[ldx])
            st_guard[r] = em.sdma(dst_of(b, dc), XNEW[r], waits=[v2])
        em.barrier()
        ar.reset(mkb)
    ar.reset(stage_mk)


def transpose_tiles(ctx, A, B, nb, na, waits=(), psum=None):
    em = ctx.em
    ps = psum if psum is not None else ctx.P4b_bf
    evs = []
    ev_guard = None
    for j in range(nb):
        lastt = None
        for i in range(na):
            lastt = em.t(
                lambda e, o=ps[:, 128 * i:128 * (i + 1)],
                i_=A[i][:, 128 * j:128 * (j + 1)]:
                e.transpose(o, i_, ctx.IDENT),
                waits=(list(waits) + [ev_guard]) if i == 0 else ())
        ev_guard = em.v(lambda e, o=B[j], i_=ps[:, 0:128 * na]:
                        e.tensor_copy(o, i_), waits=[lastt])
        evs.append(ev_guard)
    return evs


def stage_sl2(ctx, src_of, dst_of, tag):
    em, ar, nc = ctx.em, ctx.ar, ctx.nc
    P4a, P4b = ctx.P4a, ctx.P4b
    stage_mk = ar.mark()
    NE, PS_, H = 64, 32, 1024
    E2 = NE * PS_

    PHIWT, gphi = load_w_cast(ctx, ctx.inp["phi_wT"], D, E2)
    PB = ar.tile([128, E2], BF16)
    em.gdma(PB, ctx.inp["phi_b_bc"])
    em.barrier()

    for b in range(B):
        mkb = ar.mark()
        XN = ar.tiles(DC, [128, T], BF16)
        ln_x_norm(ctx, lambda dc: src_of(b, dc), XN)
        em.barrier()

        DT = ar.tiles(16, [128, E2], BF16)
        XNT = ar.tiles(16, [128, D], BF16)
        LG = ar.tiles(1, [128, E2], F32) * 2
        EX = ar.tiles(1, [128, E2], F32) * 2
        RD_ = ar.tile([128, NE], F32)
        RC_ = ar.tile([128, PS_], F32)
        SRED = LG[0]
        CSUM = ar.tile([128, 16 * NE], F32)
        CB = ar.tiles(2, [128, E2], BF16)

        transpose_tiles(ctx, XN, XNT, 16, DC)
        em.barrier()

        pga = None
        lg_guard = [None, None]
        cb_guard = [None, None]
        v7 = None
        for tc in range(16):
            r = 0
            last = None
            for ns in range(4):
                for dc in range(DC):
                    last = em.t(
                        lambda e, o=P4a[:, 512 * ns:512 * (ns + 1)],
                        l=XN[dc][:, 128 * tc:128 * (tc + 1)],
                        r_=PHIWT[:, dc * E2 + 512 * ns: dc * E2 + 512 * ns + 512],
                        st=(dc == 0), sp=(dc == DC - 1):
                        e.matmul(o, l, r_, start=st, stop=sp),
                        waits=[pga, gphi] if (ns == 0 and dc == 0) else ())
            v0 = em.v(lambda e, o=LG[r], i_=P4a:
                      e.scalar_tensor_tensor(o, i_, 1.0, PB, ALU.mult, ALU.add),
                      waits=[last, lg_guard[r]])
            pga = v0
            cx = em.c(lambda e, o=EX[r], i_=LG[r]: e.activation(o, i_, AF.Exp),
                      waits=[v0])
            lg_guard[r] = cx
            v1 = em.v(lambda e, o=RD_[:, :, None],
                      i_=EX[r].rearrange("q (n p) -> q n p", p=PS_):
                      e.reduce_sum(o, i_, axis=AX.X), waits=[cx])
            em.v(lambda e, o=RD_, i_=RD_: e.reciprocal(o, i_))
            em.v(lambda e, o=DT[tc].rearrange("q (n p) -> q n p", p=PS_),
                 i_=EX[r].rearrange("q (n p) -> q n p", p=PS_),
                 s=RD_[:, :, None].broadcast_to([128, NE, PS_]):
                 e.tensor_tensor(o, i_, s, ALU.mult))
            em.v(lambda e, o=SRED[:, 0:1024], a=EX[r][:, 0:1024],
                 bb=EX[r][:, 1024:2048]: e.tensor_tensor(o, a, bb, ALU.add))
            em.v(lambda e, o=SRED[:, 1024:1536], a=SRED[:, 0:512],
                 bb=SRED[:, 512:1024]: e.tensor_tensor(o, a, bb, ALU.add))
            em.v(lambda e, o=SRED[:, 1536:1792], a=SRED[:, 1024:1280],
                 bb=SRED[:, 1280:1536]: e.tensor_tensor(o, a, bb, ALU.add))
            em.v(lambda e, o=SRED[:, 1792:1920], a=SRED[:, 1536:1664],
                 bb=SRED[:, 1664:1792]: e.tensor_tensor(o, a, bb, ALU.add))
            em.v(lambda e, o=SRED[:, 1920:1984], a=SRED[:, 1792:1856],
                 bb=SRED[:, 1856:1920]: e.tensor_tensor(o, a, bb, ALU.add))
            em.v(lambda e, o=RC_, a=SRED[:, 1920:1952],
                 bb=SRED[:, 1952:1984]: e.tensor_tensor(o, a, bb, ALU.add))
            em.v(lambda e, o=RC_, i_=RC_: e.reciprocal(o, i_))
            em.v(lambda e, o=SRED[:, 0:32], i_=RC_: e.tensor_copy(o, i_))
            wdt = 32
            while wdt < E2:
                em.v(lambda e, o=SRED[:, wdt:2 * wdt], i_=SRED[:, 0:wdt]:
                     e.tensor_copy(o, i_))
                wdt *= 2
            rb = tc % 2
            v6 = em.v(lambda e, o=CB[rb], i_=EX[r], s=SRED[:, 0:E2]:
                      e.tensor_tensor(o, i_, s, ALU.mult), waits=[cb_guard[rb]])
            v7 = em.v(lambda e, o=CSUM[:, NE * tc:NE * (tc + 1)][:, :, None],
                      i_=CB[rb].rearrange("q (n p) -> q n p", p=PS_):
                      e.reduce_sum(o, i_, axis=AX.X))
            cb_guard[rb] = em.sdma(ctx.CS[b, tc], CB[rb], waits=[v7])
            if b == 0 and tc == 0:
                dbg_b(ctx, 1, DT[0], waits=[v7])
                dbg_b(ctx, 2, CB[rb], waits=[v7])
                dbg_f(ctx, 2, EX[r], waits=[v7])
                dbg_f(ctx, 3, SRED[:, 0:T], waits=[v7])
                dbg_f(ctx, 4, RC_, waits=[v7])
        em.sdma(ctx.CSUMS[b], CSUM, waits=[v7])
        if b == 0:
            dbg_b(ctx, 0, XN[0], waits=[v7])
            dbg_b(ctx, 3, XNT[0][:, 0:D], waits=[v7])
        em.barrier()

        XEV = ar.tiles(2, [128, D], F32)
        xev_guard = [None, None]
        for npc in range(16):
            r = npc % 2
            ps = (P4a if r == 0 else P4b)[:, 0:1024]
            last = None
            for tc in range(16):
                for ns in range(2):
                    last = em.t(
                        lambda e, o=ps[:, 512 * ns:512 * (ns + 1)],
                        l=DT[tc][:, 128 * npc:128 * (npc + 1)],
                        r_=XNT[tc][:, 512 * ns:512 * (ns + 1)],
                        st=(tc == 0), sp=(tc == 15):
                        e.matmul(o, l, r_, start=st, stop=sp),
                        waits=[xev_guard[r]] if (tc == 0 and ns == 0) else ())
            ve = em.v(lambda e, o=XEV[r], i_=ps: e.tensor_copy(o, i_),
                      waits=[last])
            xev_guard[r] = em.gdma(
                ctx.CCIN[128 * npc:128 * (npc + 1), :], XEV[r],
                waits=[ve], **({} if b == 0 else dict(accum_op=ALU.add)))
            if b == B - 1 and npc == 0:
                dbg_f(ctx, 0, XEV[r][:, 0:D], waits=[ve])
        em.barrier()
        ar.reset(mkb)

    em.op("gpsimd", lambda e: e.collective_compute(
        "AllReduce", ALU.add, replica_groups=[list(range(NCORES))],
        ins=[ctx.CCIN.opt()], outs=[ctx.CCOUT.opt()]), "CC", waits=())
    em.barrier()

    ar.reset(stage_mk)
    YTIL = ar.tiles(16, [128, H], BF16)
    ytil_bytes = ar.mark()
    XTF = ar.tiles(2, [128, D], F32)
    XSQ = ar.tiles(2, [128, D], F32)
    XTT = ar.tiles(2, [128, D], BF16)
    XTD = ar.tiles(2, [128, DC * 128], BF16)
    ROWC = ar.tile([128, 16], F32)
    EW = ar.tiles(2, [128, DC * H], BF16)
    ew_guard = [None, None]
    psum_guard = [None, None]
    xg = [None, None]
    xtd_guard = [None, None]
    for npc in range(16):
        r = npc % 2
        ld = em.sdma(XTF[r], ctx.CCOUT[128 * npc:128 * (npc + 1), :],
                     waits=[xg[r]])
        vc = em.v(lambda e, o=XTF[r]: e.tensor_scalar(
            o, o, -33000.0, 65000.0, ALU.max, ALU.min), waits=[ld])
        vs = em.v(lambda e, o=ROWC[:, 0:1], i_=XTF[r]:
                  e.reduce_sum(o, i_, axis=AX.X), waits=[vc])
        cm = em.c(lambda e, o=ROWC[:, 1:2], i_=ROWC[:, 0:1]:
                  e.mul(o, i_, 1.0 / D), waits=[vs])
        csq = em.c(lambda e, o=XSQ[r], i_=XTF[r]:
                   e.activation(o, i_, AF.Square), waits=[vc])
        vq = em.v(lambda e, o=ROWC[:, 2:3], i_=XSQ[r]:
                  e.reduce_sum(o, i_, axis=AX.X), waits=[csq])
        cq = em.c(lambda e, o=ROWC[:, 3:4], i_=ROWC[:, 2:3]:
                  e.mul(o, i_, 1.0 / D), waits=[vq])
        cmsq = em.c(lambda e, o=ROWC[:, 4:5], i_=ROWC[:, 1:2]:
                    e.activation(o, i_, AF.Square))
        cvar = em.c(lambda e, o=ROWC[:, 5:6], i_=ROWC[:, 4:5],
                    bb=ROWC[:, 3:4]:
                    e.activation(o, i_, AF.Identity, scale=-1.0, bias=bb))
        csd = em.c(lambda e, o=ROWC[:, 6:7], i_=ROWC[:, 5:6]:
                   e.activation(o, i_, AF.Sqrt, bias=1e-5))
        vr = em.v(lambda e, o=ROWC[:, 7:8], i_=ROWC[:, 6:7]:
                  e.reciprocal(o, i_), waits=[csd])
        va = em.v(lambda e, o=XTF[r], i_=XTF[r], m=ROWC[:, 1:2],
                  rr=ROWC[:, 7:8]:
                  e.tensor_scalar(o, i_, m, rr, ALU.subtract, ALU.mult),
                  waits=[vr])
        ct = em.c(lambda e, o=XTT[r], i_=XTF[r]:
                  e.activation(o, i_, AF.Tanh), waits=[va])
        xg[r] = ct
        if npc == 0:
            dbg_f(ctx, 1, XTF[r][:, 0:D], waits=[va])
            dbg_b(ctx, 4, XTT[r][:, 0:D], waits=[ct])
        lastt = None
        for dcc in range(DC):
            lastt = em.t(lambda e, o=ctx.P4b_bf[:, 128 * dcc:128 * (dcc + 1)],
                         i_=XTT[r][:, 128 * dcc:128 * (dcc + 1)]:
                         e.transpose(o, i_, ctx.IDENT),
                         waits=[ct, xtd_guard[r]] if dcc == 0 else ())
        evx = em.v(lambda e, o=XTD[r], i_=ctx.P4b_bf[:, 0:DC * 128]:
                   e.tensor_copy(o, i_), waits=[lastt])
        xtd_guard[r] = evx
        for ei in range(4):
            n = 4 * npc + ei
            rw = n % 2
            gw = em.gdma(EW[rw].rearrange("p (kc m) -> p kc m", kc=DC),
                         ctx.inp["expert_w"][n].rearrange(
                             "(kc p) m -> p kc m", p=128),
                         waits=[ew_guard[rw]])
            pr = ei % 2
            psv = P4a[0:32, 1024 * pr:1024 * pr + 1024]
            last = None
            for dcc in range(DC):
                for ns in range(2):
                    last = em.t(
                        lambda e, o=psv[:, 512 * ns:512 * (ns + 1)],
                        l=XTD[r][:, 128 * dcc + 32 * ei:128 * dcc + 32 * ei + 32],
                        r_=EW[rw][:, dcc * H + 512 * ns:dcc * H + 512 * ns + 512],
                        st=(dcc == 0), sp=(dcc == DC - 1):
                        e.matmul(o, l, r_, start=st, stop=sp),
                        waits=([evx, gw, psum_guard[pr]]
                               if (dcc == 0 and ns == 0) else ()))
            ev = em.c(
                lambda e, o=YTIL[npc][32 * ei:32 * (ei + 1), :], i_=psv:
                e.activation(o, i_, AF.Copy), waits=[last])
            psum_guard[pr] = ev
            ew_guard[rw] = ev
    em.barrier()

    dbg_b(ctx, 5, YTIL[0][:, 0:H])
    ar.reset(ytil_bytes)
    EB = ar.tile([64, H], BF16)
    em.gdma(EB, ctx.inp["expert_b"])
    CLD = ar.tiles(16, [128, E2], BF16)
    CTT = ar.tiles(16, [128, T], BF16)
    CSL = ar.tile([128, 16 * 64], F32)
    CSB = ar.tile([128, 16 * 64], BF16)
    CSUMT = ar.tile([64, T], BF16)
    XF = ar.tiles(1, [128, T], F32)
    XNEW = ar.tiles(1, [128, T], F32)
    prev_b_done = None
    for b in range(B):
        lds = []
        for tc in range(16):
            lds.append(em.sdma(CLD[tc], ctx.CS[b, tc], waits=[prev_b_done]))
        ldc = em.sdma(CSL, ctx.CSUMS[b], waits=[prev_b_done])
        vcs = em.v(lambda e, o=CSB, i_=CSL: e.tensor_copy(o, i_), waits=[ldc])
        lastt = None
        for tc in range(16):
            lastt = em.t(lambda e, o=ctx.P4b_bf[0:64, 128 * tc:128 * (tc + 1)],
                         i_=CSB[:, 64 * tc:64 * (tc + 1)]:
                         e.transpose(o, i_, ctx.IDENT),
                         waits=[vcs] if tc == 0 else ())
        vct = em.v(lambda e, o=CSUMT, i_=ctx.P4b_bf[0:64, 0:T]:
                   e.tensor_copy(o, i_), waits=[lastt])
        transpose_tiles(ctx, CLD, CTT, 16, 16, waits=[lds[-1], vct])
        em.barrier()
        if b == 0:
            dbg_b(ctx, 6, CTT[0])
            dbg_b(ctx, 7, CSUMT[0:64, :])
        pg2 = [None, None]
        st_guard = [None, None]
        xf_guard = [None, None]
        for hc in range(DC):
            r = 0
            ps = P4a if hc % 2 == 0 else P4b
            ldx = em.sdma(XF[r], src_of(b, hc), waits=[xf_guard[r]])
            last = None
            for ns in range(4):
                last = em.t(lambda e, o=ps[:, 512 * ns:512 * (ns + 1)],
                            l=EB[:, 128 * hc:128 * (hc + 1)],
                            r_=CSUMT[:, 512 * ns:512 * (ns + 1)]:
                            e.matmul(o, l, r_, start=True, stop=False),
                            waits=[pg2[r]] if ns == 0 else ())
                for npc in range(16):
                    last = em.t(
                        lambda e, o=ps[:, 512 * ns:512 * (ns + 1)],
                        l=YTIL[npc][:, 128 * hc:128 * (hc + 1)],
                        r_=CTT[npc][:, 512 * ns:512 * (ns + 1)],
                        sp=(npc == 15):
                        e.matmul(o, l, r_, start=False, stop=sp))
            vres = em.v(lambda e, o=XNEW[r], i_=ps, x=XF[r]:
                        e.scalar_tensor_tensor(o, i_, 1.0, x, ALU.mult,
                                               ALU.add),
                        waits=[last, ldx, st_guard[r]])
            pg2[r] = vres
            xf_guard[r] = vres
            st_guard[r] = em.sdma(dst_of(b, hc), XNEW[r], waits=[vres])
        prev_b_done = st_guard[0]
        em.barrier()
    ar.reset(stage_mk)


def ln_tiles(ctx, xg, xl, lg_col, lb_col):
    em, ar = ctx.em, ctx.ar
    P4a, P4b = ctx.P4a, ctx.P4b
    mk = ar.mark()
    SQ = ar.tiles(2, [128, T], BF16)
    ROWS = ar.tile([1, 3 * T], F32)
    ROWm = ROWS[:, 0:T]
    ROWmsq = ROWS[:, T:2 * T]
    ROWvar = ROWS[:, 2 * T:3 * T]
    ROWsd = ROWS[:, T:2 * T]
    ROWr = ROWS[:, 2 * T:3 * T]
    ROWSB = ar.tile([1, 2 * T], BF16)
    ROWrb = ROWSB[:, 0:T]
    ROWmrb = ROWSB[:, T:2 * T]
    RBB = ar.tile([128, T], BF16)
    MRB = ar.tile([128, T], BF16)
    sq_guard = [None, None]
    last_mm = None
    for dc in range(DC):
        r = dc % 2
        sq = em.v(lambda e, o=SQ[r], i=xg[dc]:
                  e.tensor_tensor(o, i, i, ALU.mult), waits=[sq_guard[r]])
        for ns in range(4):
            em.t(lambda e, o=P4b[0:1, 512 * ns:512 * (ns + 1)], i=xg[dc],
                 ns=ns, st=(dc == 0), sp=(dc == DC - 1):
                 e.matmul(o, ctx.ONESC, i[:, 512 * ns:512 * (ns + 1)],
                          start=st, stop=sp))
        for ns in range(4):
            last_mm = em.t(lambda e, o=P4a[0:1, 512 * ns:512 * (ns + 1)],
                           i=SQ[r], ns=ns, st=(dc == 0), sp=(dc == DC - 1):
                           e.matmul(o, ctx.ONESC,
                                    i[:, 512 * ns:512 * (ns + 1)],
                                    start=st, stop=sp),
                           waits=[sq] if ns == 0 else ())
        sq_guard[r] = last_mm
    v1 = em.v(lambda e: e.tensor_scalar(ROWm, P4b[0:1, :], 1.0 / D, None,
                                        ALU.mult), waits=[last_mm])
    em.v(lambda e: e.tensor_tensor(ROWmsq, ROWm, ROWm, ALU.mult))
    v3 = em.v(lambda e: e.scalar_tensor_tensor(ROWvar, P4a[0:1, :], 1.0 / D,
                                               ROWmsq, ALU.mult, ALU.subtract))
    c1 = em.c(lambda e: e.activation(ROWsd, ROWvar, AF.Sqrt, bias=1e-5),
              waits=[v3])
    em.v(lambda e: e.reciprocal(ROWr, ROWsd), waits=[c1])
    v4 = em.v(lambda e: e.tensor_copy(ROWrb, ROWr))
    v5 = em.v(lambda e: e.tensor_tensor(ROWmrb, ROWm, ROWr, ALU.mult))
    bc_r = None
    for ns in range(4):
        bc_r = em.t(lambda e, ns=ns: e.matmul(
            P4a[:, 512 * ns:512 * (ns + 1)], ctx.ONESR,
            ROWrb[:, 512 * ns:512 * (ns + 1)], start=True, stop=True),
            waits=[v4, v1] if ns == 0 else ())
    bc_mr = None
    for ns in range(4):
        bc_mr = em.t(lambda e, ns=ns: e.matmul(
            P4b[:, 512 * ns:512 * (ns + 1)], ctx.ONESR,
            ROWmrb[:, 512 * ns:512 * (ns + 1)], start=True, stop=True),
            waits=[v5] if ns == 0 else ())
    e1 = em.c(lambda e: e.activation(RBB, P4a[:, :], AF.Copy), waits=[bc_r])
    e2 = em.c(lambda e: e.activation(MRB, P4b[:, :], AF.Copy), waits=[bc_mr])
    vdone = None
    for dc in range(DC):
        em.v(lambda e, o=xl[dc], a=xg[dc]:
             e.tensor_tensor(o, a, RBB, ALU.mult),
             waits=[e1, e2] if dc == 0 else ())
        em.v(lambda e, o=xl[dc]: e.tensor_tensor(o, o, MRB, ALU.subtract))
        vdone = em.v(lambda e, o=xl[dc], lg=lg_col(dc), lb=lb_col(dc):
                     e.tensor_scalar(o, o, lg, lb, ALU.mult, ALU.add))
    ar.reset(mk)
    return vdone


def ln_x_norm_f32(ctx, src, xn):
    return ln_x_norm(ctx, src, xn)


def stage_sl6(ctx, src_of, dst_of, tag):
    em, ar = ctx.em, ctx.ar
    P4a, P4b = ctx.P4a, ctx.P4b
    P4a_bf = ctx.P4a_bf
    stage_mk = ar.mark()
    IN2 = 512
    HID = 2730
    HIDP = 2816
    MC = 22

    H8ALL = ar.tile([128, 32], BF16)
    em.sdma(H8ALL, ctx.inp["f_h8all"])
    RP = ar.tile([128, 128], BF16)
    em.sdma(RP, ctx.inp["f_rperm"])
    MSK = ar.tile([128, 2, 4, 1024], U8)
    em.sdma(MSK[:, 0], ctx.inp["mask6"].rearrange("c p k -> p c k"))
    em.sdma(MSK[:, 1], ctx.inp["mask6w0"].rearrange("c p k -> p c k"))
    QSKS = ar.tile([128, 8], F32)
    em.sdma(QSKS, ctx.inp["f_qsks_pp"])
    LNG = ar.tile([128, 16], F32)
    em.sdma(LNG, ctx.inp["f_lng_pp"])
    LNB = ar.tile([128, 16], F32)
    em.sdma(LNB, ctx.inp["f_lnb_pp"])
    em.barrier()

    for b in range(B):
        mkb = ar.mark()
        XLOC = ar.tiles(DC, [128, T], F32)
        ln_x_norm(ctx, lambda dc: src_of(b, dc), XLOC)
        em.barrier()

        for l in range(2):
            LIM = SL6_LIMIT
            mkl = ar.mark()
            TQ1 = ar.tile([128, T], BF16)
            TQ2 = ar.tile([128, T], BF16)
            TK1 = ar.tile([128, T], BF16)
            TK2 = ar.tile([128, T], BF16)

            QTT = ar.tiles(4, [128, T], BF16)
            KTT = ar.tiles(4, [128, T], BF16)
            VT6 = ar.tiles(16, [128, IN2], BF16)
            RNQT = ar.tile([128, 16 * 8], F32)
            OH = ar.tile([64, T], BF16)

            mkq = ar.mark()
            TBASE = ar.tile([128, 4 * T], BF16)
            gtb = em.gdma(TBASE, ctx.inp["f_tabs"])
            tdone = None
            for (tt, tb, col) in ((TQ1, 0, 0), (TQ2, 1, 1), (TK1, 2, 2),
                                  (TK2, 3, 3)):
                tdone = em.v(lambda e, o=tt, i_=TBASE[:, tb * T:(tb + 1) * T],
                             s=QSKS[:, 4 * l + col:4 * l + col + 1]:
                             e.tensor_scalar(o, i_, s, None, ALU.mult),
                             waits=[gtb] if tb == 0 else ())
            em.barrier()
            ar.reset(mkq)
            XQ = ar.tiles(DC, [128, T], BF16)
            WQKV, gq = load_w_cast(ctx, ctx.inp["f_wqkv"][l], D, 1536)
            sg = None
            for dc in range(DC):
                if dc < 4:
                    sg = em.v(lambda e, o=XQ[dc], i_=XLOC[dc]:
                              e.tensor_copy(o, i_))
                else:
                    em.v(lambda e, o=XQ[dc][:, 0:2], i_=XQ[dc][:, 0:2]:
                         e.memset(o, 0.0))
                    sg = em.v(lambda e, o=XQ[dc][:, 1:T],
                              i_=XLOC[dc][:, 0:T - 1]:
                              e.tensor_copy(o, i_))
            em.barrier()
            pg = {"a": None, "b": None}
            for which, OUTL, cofs in (("q", QTT, 0), ("k", KTT, 512)):
                for cc in range(4):
                    key = "a" if cc % 2 == 0 else "b"
                    ps = P4a if key == "a" else P4b
                    last = None
                    for ns in range(4):
                        for dc in range(DC):
                            base = dc * 1536 + cofs + 128 * cc
                            last = em.t(
                                lambda e, o=ps[:, 512 * ns:512 * (ns + 1)],
                                l_=WQKV[:, base:base + 128],
                                r_=XQ[dc][:, 512 * ns:512 * (ns + 1)],
                                st=(dc == 0), sp=(dc == DC - 1):
                                e.matmul(o, l_, r_, start=st, stop=sp),
                                waits=([gq, pg[key]]
                                       if (ns == 0 and dc == 0) else ()))
                    pg[key] = em.c(lambda e, o=OUTL[cc], i_=ps:
                                   e.activation(o, i_, AF.Copy),
                                   waits=[last])
            for tc in range(16):
                key = "a" if tc % 2 == 0 else "b"
                ps = (P4a if key == "a" else P4b)[:, 0:512]
                last = None
                for dc in range(DC):
                    base = dc * 1536 + 1024
                    last = em.t(
                        lambda e, o=ps,
                        l_=XQ[dc][:, 128 * tc:128 * (tc + 1)],
                        r_=WQKV[:, base:base + 512],
                        st=(dc == 0), sp=(dc == DC - 1):
                        e.matmul(o, l_, r_, start=st, stop=sp),
                        waits=[pg[key]] if dc == 0 else ())
                pg[key] = em.c(lambda e, o=VT6[tc], i_=ps:
                               e.activation(o, i_, AF.Copy), waits=[last])
            if b == 0 and l == 0:
                dbg_b(ctx, 3, QTT[0], waits=[pg["a"], pg["b"]])
                dbg_b(ctx, 8, VT6[0], waits=[pg["a"], pg["b"]])
            em.barrier()
            ar.reset(mkq)

            if LIM < 2:
                ar.reset(mkl)
                continue
            TMP6 = ar.tiles(2, [128, T], BF16)
            SQ6 = TMP6[0]
            RN8 = ar.tile([8, T], F32)
            RN8B = ar.tile([8, T], BF16)
            RNKB = ar.tiles(4, [128, T], BF16)
            SM6 = ar.tiles(1, [128, 1024], F32) * 2
            ATTN6 = ar.tiles(4, [128, 1024], BF16)
            ATTS6 = ar.tile([128, 8 * 512], BF16)
            NM6 = ar.tile([128, 16], F32)
            RS6 = ar.tile([128, 8], F32)
            RR6 = ar.tile([128, 8], F32)
            RNROW2 = ar.tiles(1, [1, T], BF16) * 2
            rnrow_guard = [None, None]
            rn_guard = None
            for side, OUTL in (("q", QTT), ("k", KTT)):
                lasts = None
                sqg = None
                for j in range(4):
                    sqg = em.v(lambda e, o=SQ6, i_=OUTL[j]:
                               e.tensor_tensor(o, i_, i_, ALU.mult),
                               waits=[lasts])
                    for ns in range(4):
                        lasts = em.t(
                            lambda e, o=P4a[0:8, 512 * ns:512 * (ns + 1)],
                            l_=H8ALL[:, 8 * j:8 * (j + 1)], i_=SQ6, ns=ns,
                            st=(j == 0), sp=(j == 3):
                            e.matmul(o, l_, i_[:, 512 * ns:512 * (ns + 1)],
                                     start=st, stop=sp),
                            waits=[sqg] if ns == 0 else ())
                scl = 1.0 / 64.0 if side == "q" else 1.0
                clamp = 1e-12 / 8.0 if side == "q" else 1e-12
                c1 = em.c(lambda e, o=RN8, i_=P4a[0:8, :],
                          s=scl: e.activation(o, i_, AF.Sqrt, scale=s),
                          waits=[lasts, rn_guard])
                v1 = em.v(lambda e, o=RN8, cl=clamp:
                          e.tensor_scalar(o, o, cl, None, ALU.max),
                          waits=[c1])
                vr = em.v(lambda e, o=RN8:
                          e.reciprocal(o, o))
                if SL6_SUB < 2:
                    continue
                if side == "q":
                    lastt = None
                    for tc in range(16):
                        lastt = em.t(
                            lambda e, o=ctx.P4b[0:128, 8 * tc:8 * (tc + 1)],
                            i_=RN8[0:8, 128 * tc:128 * (tc + 1)]:
                            e.transpose(o, i_, ctx.IDENTF[0:8, 0:8]),
                            waits=[vr, pg["b"]] if tc == 0 else ())
                    pg["b"] = em.v(lambda e, o=RNQT, i_=ctx.P4b[:, 0:128]:
                                   e.tensor_copy(o, i_), waits=[lastt])
                    rn_guard = pg["b"]
                elif SL6_SUB < 3:
                    continue
                else:
                    vb = em.v(lambda e, o=RN8B, i_=RN8:
                              e.tensor_copy(o, i_), waits=[vr])
                    grn = em.sdma(ctx.RNS, RN8B, waits=[vb])
                    lrn = [None] * 4
                    for j in range(4):
                        lrnA = em.sdma(
                            RNROW2[0], ctx.RNS[2 * j:2 * j + 1, :],
                            waits=[grn, rnrow_guard[0]])
                        lastt = None
                        for ns in range(4):
                            lastt = em.t(
                                lambda e, o=P4b[:, 512 * ns:512 * (ns + 1)],
                                r_=RNROW2[0][:, 512 * ns:512 * (ns + 1)]:
                                e.matmul(o, ctx.INDA, r_, start=True,
                                         stop=False),
                                waits=[lrnA, pg["b"]] if ns == 0 else ())
                        lrnB = em.sdma(
                            RNROW2[0], ctx.RNS[2 * j + 1:2 * j + 2, :],
                            waits=[lastt])
                        for ns in range(4):
                            lastt = em.t(
                                lambda e, o=P4b[:, 512 * ns:512 * (ns + 1)],
                                r_=RNROW2[0][:, 512 * ns:512 * (ns + 1)]:
                                e.matmul(o, ctx.INDB, r_, start=False,
                                         stop=True),
                                waits=[lrnB] if ns == 0 else ())
                        pg["b"] = em.c(lambda e, o=RNKB[j], i_=P4b:
                                       e.activation(o, i_, AF.Copy),
                                       waits=[lastt])
                        rnrow_guard[0] = lastt
            if LIM < 3:
                ar.reset(mkl)
                continue
            rotg = None
            for side, OUTL, T1, T2 in (("q", QTT, TQ1, TQ2),
                                       ("k", KTT, TK1, TK2)):
                for j in range(4):
                    lastr = None
                    for ns in range(4):
                        lastr = em.t(
                            lambda e, o=P4a[:, 512 * ns:512 * (ns + 1)],
                            i_=OUTL[j][:, 512 * ns:512 * (ns + 1)]:
                            e.matmul(o, RP, i_, start=True, stop=True),
                            waits=[pg["a"], rotg] if ns == 0 else ())
                    v1_ = em.v(lambda e, o=TMP6[0], i_=P4a, t2=T2:
                               e.tensor_tensor(o, i_, t2, ALU.mult),
                               waits=[lastr])
                    pg["a"] = v1_
                    v2_ = em.v(lambda e, o=TMP6[1], i_=OUTL[j], t1=T1:
                               e.tensor_tensor(o, i_, t1, ALU.mult))
                    if side == "q":
                        rotg = em.v(lambda e, o=OUTL[j], a=TMP6[1],
                                    bb=TMP6[0]:
                                    e.tensor_tensor(o, a, bb, ALU.add))
                    else:
                        v3_ = em.v(lambda e, o=TMP6[0], a=TMP6[1],
                                   bb=TMP6[0]:
                                   e.tensor_tensor(o, a, bb, ALU.add))
                        rotg = em.v(lambda e, o=OUTL[j], a=TMP6[0],
                                    s=RNKB[j]:
                                    e.tensor_tensor(o, a, s, ALU.mult))
            if b == 0 and l == 0:
                dbg_b(ctx, 4, QTT[0], waits=[rotg])
                dbg_b(ctx, 5, KTT[0], waits=[rotg])
                dbg_b(ctx, 6, RNKB[0], waits=[rotg])
                dbg_f(ctx, 3, RNQT, waits=[rotg])
            em.barrier()

            if LIM < 4:
                ar.reset(mkl)
                continue
            hh_guard = None
            pg = {"a": None, "b": None}
            attn_guard = [None] * 4
            sm_guard = [None, None]
            ev_guard = None
            for h in range(8):
                tile_i, off = h // 2, 64 * (h % 2)
                Qh = QTT[tile_i][off:off + 64, :]
                Kh = KTT[tile_i][off:off + 64, :]
                for w in range(4):
                    kcl = 0 if w > 0 else 512
                    klo = 512 * w - 512 + (512 if w == 0 else 0)
                    nks = 2 if w > 0 else 1
                    softmax_done = [None] * 4
                    for i in range(4):
                        tc = 4 * w + i
                        key = "a" if i % 2 == 0 else "b"
                        ps = P4a if key == "a" else P4b
                        r2 = 0
                        pmem = em.p(lambda e, o=SM6[r2]:
                                    e.memset(o, -3.4e38),
                                    waits=[pg[key], sm_guard[r2]])
                        last = None
                        for ns in range(nks):
                            last = em.t(
                                lambda e,
                                o=ps[:, kcl + 512 * ns:kcl + 512 * (ns + 1)],
                                l_=Qh[:, 128 * tc:128 * (tc + 1)],
                                r_=Kh[:, klo + 512 * ns:klo + 512 * (ns + 1)]:
                                e.matmul(o, l_, r_, start=True, stop=True),
                                waits=[pg[key]] if ns == 0 else ())
                        vcp = em.v(lambda e, o=SM6[r2],
                                   mk=MSK[:, 1 if w == 0 else 0, i],
                                   i_=ps[:, 0:1024]:
                                   e.copy_predicated(o, mk, i_),
                                   waits=[last, pmem])
                        pg[key] = vcp
                        vm = em.v(lambda e, o=NM6[:, tc:tc + 1], i_=SM6[r2]:
                                  e.reduce_max(o, i_, axis=AX.X, negate=True))
                        rncol = RNQT[:, 8 * tc + h:8 * tc + h + 1]
                        cb = em.c(lambda e, o=NM6[:, tc:tc + 1],
                                  i_=NM6[:, tc:tc + 1], s=rncol:
                                  e.activation(o, i_, AF.Identity, scale=s),
                                  waits=[vm])
                        cx = em.c(lambda e, o=ATTN6[i], i_=SM6[r2], ii=i,
                                  s=rncol, tc=tc:
                                  e.activation(o, i_, AF.Exp, scale=s,
                                               bias=NM6[:, tc:tc + 1],
                                               accum_out=RS6[:, ii:ii + 1]),
                                  waits=[cb, attn_guard[i]])
                        sm_guard[r2] = cx
                        vr = em.v(lambda e, o=RR6[:, i:i + 1],
                                  i_=RS6[:, i:i + 1]: e.reciprocal(o, i_),
                                  waits=[cx])
                        softmax_done[i] = em.v(
                            lambda e, o=ATTN6[i], ii=i:
                            e.tensor_scalar(o, o, RR6[:, ii:ii + 1], None,
                                            ALU.mult), waits=[vr])
                    ngs = 8 if w > 0 else 4
                    for gsc in range(ngs):
                        lastt = None
                        for i in range(4):
                            lastt = em.t(
                                lambda e,
                                o=P4a_bf[:, 128 * i:128 * (i + 1)],
                                i_=ATTN6[i][:, kcl + 128 * gsc:
                                            kcl + 128 * (gsc + 1)]:
                                e.transpose(o, i_, ctx.IDENT),
                                waits=([softmax_done[i], ev_guard, pg["a"]]
                                       if i == 0 else [softmax_done[i]]))
                        ev_guard = em.v(
                            lambda e, o=ATTS6[:, 512 * gsc:512 * (gsc + 1)],
                            i_=P4a_bf[:, 0:512]:
                            e.tensor_copy(o, i_), waits=[lastt])
                    attn_guard = [lastt] * 4
                    lastv = None
                    for gsc in range(ngs):
                        g = 4 * (w - 1) + kcl // 128 + gsc
                        lastv = em.t(
                            lambda e, o=P4b[0:64, 1024:1536],
                            l_=VT6[g][:, 64 * h:64 * h + 64],
                            r_=ATTS6[:, 512 * gsc:512 * (gsc + 1)],
                            st=(gsc == 0), sp=(gsc == ngs - 1):
                            e.matmul(o, l_, r_, start=st, stop=sp),
                            waits=[ev_guard, pg["b"]] if gsc == 0 else ())
                    ce = em.c(lambda e, o=OH[:, 512 * w:512 * (w + 1)],
                              i_=P4b[0:64, 1024:1536]:
                              e.activation(o, i_, AF.Copy),
                              waits=[lastv, hh_guard])
                    pg["a"] = ce
                    pg["b"] = ce
                    ev_guard = None
                sh = em.sdma(ctx.HS[h], OH, waits=[pg["a"]])
                if b == 0 and l == 0 and h == 0:
                    dbg_b(ctx, 7, OH, waits=[pg["a"]])
                hh_guard = sh
            em.barrier()
            ar.reset(mkq)

            if LIM < 5:
                ar.reset(mkl)
                continue
            if DEBUG and b == 0 and l == 0:
                dbg_f(ctx, 8, XLOC[0])
                em.barrier()
            OL = ar.tiles(4, [128, T], BF16)
            WO6, gwo = load_w_cast(ctx, ctx.inp["f_wo"][l], IN2, D)
            hl = None
            for cc in range(4):
                hl = em.sdma(OL[cc], ctx.HS.rearrange("h p t -> (h p) t")
                             [128 * cc:128 * (cc + 1), :])
            if b == 0 and l == 0:
                dbg_b(ctx, 9, OL[0], waits=[hl])
                dbg_b(ctx, 10, OL[3], waits=[hl])
                dbg_b(ctx, 11, WO6[:, 0:T], waits=[gwo])
                if DEBUG:
                    em.sdma(ctx.DBGB[0], ctx.HS.rearrange("h p t -> (h p) t")
                            [0:128, :])
                    dbg_b(ctx, 1, OL[1], waits=[hl])
                    dbg_b(ctx, 2, OL[2], waits=[hl])
                    em.barrier()
            XWT = ar.tiles(2, [128, T], F32)
            pgo = {"a": None, "b": None}
            xwg = [None, None]
            for dc in range(DC):
                key = "a" if dc % 2 == 0 else "b"
                rr = dc % 2
                ps = P4a if key == "a" else P4b
                last = None
                for ns in range(4):
                    for cc in range(4):
                        last = em.t(
                            lambda e, o=ps[:, 512 * ns:512 * (ns + 1)],
                            l_=WO6[:, cc * D + 128 * dc: cc * D + 128 * dc + 128],
                            r_=OL[cc][:, 512 * ns:512 * (ns + 1)],
                            st=(cc == 0), sp=(cc == 3):
                            e.matmul(o, l_, r_, start=st, stop=sp),
                            waits=([pgo[key], gwo, hl]
                                   if (ns == 0 and cc == 0) else ()))
                v1_ = em.v(
                    lambda e, o=XWT[rr], i_=ps, x_=XLOC[dc]:
                    e.scalar_tensor_tensor(o, i_, 1.0, x_,
                                           ALU.mult, ALU.add),
                    waits=[last, xwg[rr]])
                pgo[key] = v1_
                xwg[rr] = em.v(lambda e, o=XLOC[dc], i_=XWT[rr]:
                               e.tensor_copy(o, i_))
                if DEBUG and b == 0 and l == 0 and dc == 0:
                    em.barrier()
                    dbg_f(ctx, 9, XWT[0])
                    dbg_f(ctx, 10, XLOC[0])
                    em.barrier()
            em.barrier()
            if b == 0 and l == 0:
                dbg_f(ctx, 6, XLOC[0])
                dbg_f(ctx, 7, XLOC[4])
                em.barrier()
            ar.reset(mkl)

            if LIM < 6:
                continue
            mkf = ar.mark()
            XL6 = ar.tiles(DC, [128, T], BF16)
            mkg = ar.mark()
            XG6 = ar.tiles(DC, [128, T], BF16)
            for dc in range(DC):
                if dc < 4:
                    em.v(lambda e, o=XG6[dc], i_=XLOC[dc]:
                         e.tensor_copy(o, i_))
                else:
                    em.v(lambda e, o=XG6[dc][:, 0:2], i_=XG6[dc][:, 0:2]:
                         e.memset(o, 0.0))
                    em.v(lambda e, o=XG6[dc][:, 1:T], i_=XLOC[dc][:, 0:T - 1]:
                         e.tensor_copy(o, i_))
            em.barrier()
            ln_tiles(ctx, XG6, XL6,
                     lambda dc, l=l: LNG[:, 8 * l + dc:8 * l + dc + 1],
                     lambda dc, l=l: LNB[:, 8 * l + dc:8 * l + dc + 1])
            em.barrier()
            ar.reset(mkg)
            H2 = ar.tiles(MC, [128, 1024], BF16)
            GT6 = ar.tile([128, 1024], BF16)
            XW2 = ar.tiles(2, [128, 512], F32)
            W1C = ar.tiles(2, [128, DC * 128], BF16)
            W1G = ar.tiles(2, [128, DC * 128], BF16)
            W2C = ar.tiles(2, [128, D], BF16)
            AT6 = ar.tile([128, 1024], F32)
            B1A = ar.tile([128, MC], F32)
            B1G = ar.tile([128, MC], F32)
            B26 = ar.tile([128, 8], F32)
            em.sdma(B1A, ctx.inp["f_b1a_pp"][l])
            em.sdma(B1G, ctx.inp["f_b1g_pp"][l])
            em.sdma(B26, ctx.inp["f_b2_pp"][l])
            em.barrier()
            for th in range(2):
                tlo = 1024 * th
                wg = [None, None]
                evg = [None, None]
                h2g = None
                for mc in range(MC):
                    rw = mc % 2
                    ga = em.gdma(W1C[rw].rearrange("p (kc m) -> p kc m",
                                                   kc=DC),
                                 ctx.inp["f_w1a"][l]
                                 .rearrange("(kc p) m -> p kc m", p=128)
                                 [:, :, 128 * mc:128 * (mc + 1)],
                                 waits=[wg[rw]])
                    gg = em.gdma(W1G[rw].rearrange("p (kc m) -> p kc m",
                                                   kc=DC),
                                 ctx.inp["f_w1g"][l]
                                 .rearrange("(kc p) m -> p kc m", p=128)
                                 [:, :, 128 * mc:128 * (mc + 1)],
                                 waits=[wg[rw]])
                    lasta = None
                    for ns in range(2):
                        for dc in range(DC):
                            lasta = em.t(
                                lambda e, o=P4a[:, 512 * ns:512 * (ns + 1)],
                                l_=W1C[rw][:, 128 * dc:128 * (dc + 1)],
                                r_=XL6[dc][:, tlo + 512 * ns:
                                           tlo + 512 * (ns + 1)],
                                st=(dc == 0), sp=(dc == DC - 1):
                                e.matmul(o, l_, r_, start=st, stop=sp),
                                waits=([ga, evg[0]]
                                       if (ns == 0 and dc == 0) else ()))
                    lastg = None
                    for ns in range(2):
                        for dc in range(DC):
                            lastg = em.t(
                                lambda e, o=P4b[:, 512 * ns:512 * (ns + 1)],
                                l_=W1G[rw][:, 128 * dc:128 * (dc + 1)],
                                r_=XL6[dc][:, tlo + 512 * ns:
                                           tlo + 512 * (ns + 1)],
                                st=(dc == 0), sp=(dc == DC - 1):
                                e.matmul(o, l_, r_, start=st, stop=sp),
                                waits=([gg, evg[1]]
                                       if (ns == 0 and dc == 0) else ()))
                    ea = em.v(lambda e, o=AT6, i_=P4a[:, 0:1024],
                              bb=B1A[:, mc:mc + 1]:
                              e.tensor_scalar(o, i_, bb, None, ALU.add),
                              waits=[lasta, h2g])
                    evg[0] = ea
                    eg = em.c(lambda e, o=GT6, i_=P4b[:, 0:1024],
                              bb=B1G[:, mc:mc + 1]:
                              e.activation(o, i_, AF.Gelu, bias=bb),
                              waits=[lastg, h2g])
                    evg[1] = eg
                    h2g = em.v(lambda e, o=H2[mc], a=AT6:
                               e.tensor_tensor(o, a, GT6, ALU.mult),
                               waits=[eg])
                    if b == 0 and l == 0 and th == 0 and mc in (0, 21):
                        dbg_b(ctx, 1 if mc == 0 else 2, H2[mc], waits=[h2g])
                        if mc == 0:
                            dbg_b(ctx, 0, XL6[0], waits=[h2g])
                            dbg_f(ctx, 1, AT6, waits=[ea])
                    wg[rw] = lasta if lastg is None else lastg
                for sub in range(2):
                    slo = tlo + 512 * sub
                    w2g = [None, None]
                    lasto = None
                    for mc in range(MC):
                        rw = mc % 2
                        g2 = em.gdma(W2C[rw],
                                     ctx.inp["f_w2p"][l]
                                     [128 * mc:128 * (mc + 1), :],
                                     waits=[w2g[rw]])
                        for dc in range(DC):
                            half = P4a if dc < 4 else P4b
                            psb = half[:, 512 * (dc % 4):512 * (dc % 4) + 512]
                            lasto = em.t(
                                lambda e, o=psb,
                                l_=W2C[rw][:, 128 * dc:128 * (dc + 1)],
                                r_=H2[mc][:, 512 * sub:512 * (sub + 1)],
                                st=(mc == 0), sp=(mc == MC - 1):
                                e.matmul(o, l_, r_, start=st, stop=sp),
                                waits=([g2, h2g, evg[0], evg[1]]
                                       if (mc == 0 and dc == 0) else
                                       ([g2] if dc == 0 else ())))
                        w2g[rw] = lasto
                    vr6 = None
                    for dc in range(DC):
                        half = P4a if dc < 4 else P4b
                        psb = half[:, 512 * (dc % 4):512 * (dc % 4) + 512]
                        em.v(lambda e, o=XW2[dc % 2], i_=psb,
                             bb=B26[:, dc:dc + 1], dc=dc, slo=slo:
                             e.scalar_tensor_tensor(
                                 o, i_, bb, XLOC[dc][:, slo:slo + 512],
                                 ALU.add, ALU.add),
                             waits=[lasto] if dc == 0 else ())
                        vr6 = em.v(lambda e, o=XLOC[dc][:, slo:slo + 512],
                                   i_=XW2[dc % 2]: e.tensor_copy(o, i_))
                    evg = [vr6, vr6]
                    if b == 0 and l == 0 and th == 0 and sub == 0:
                        dbg_f(ctx, 0, XLOC[0], waits=[vr6])
                em.barrier()
            ar.reset(mkf)
        em.barrier()
        if b == 0:
            dbg_f(ctx, 4, XLOC[0])
            dbg_f(ctx, 5, XLOC[4])
        em.barrier()
        XF = ar.tiles(2, [128, T], F32)
        XO = ar.tiles(2, [128, T], F32)
        st_guard = [None, None]
        for dc in range(DC):
            r = dc % 2
            ldx = em.sdma(XF[r], src_of(b, dc), waits=[st_guard[r]])
            va = em.v(lambda e, o=XO[r], a=XLOC[dc], bb=XF[r]:
                      e.tensor_tensor(o, a, bb, ALU.add),
                      waits=[ldx, st_guard[r]])
            st_guard[r] = em.sdma(dst_of(b, dc), XO[r], waits=[va])
        em.barrier()
        ar.reset(mkb)
    ar.reset(stage_mk)


def stage_sl7(ctx, src_of, dst_of):
    em, ar = ctx.em, ctx.ar
    P4a, P4b = ctx.P4a, ctx.P4b
    P4a_bf = ctx.P4a_bf
    stage_mk = ar.mark()
    H4 = 4096

    for b in range(B):
        mkb = ar.mark()
        XN = ar.tiles(DC, [128, T], BF16)
        vd = ln_x_norm(ctx, lambda dc: src_of(b, dc), XN)
        for dc in range(DC):
            em.sdma(ctx.XNS[b, 128 * dc:128 * (dc + 1), :], XN[dc],
                    waits=[vd])
        em.barrier()
        ar.reset(mkb)

    for ti, typ in enumerate(("q", "k", "v")):
        mkt = ar.mark()
        W1, g1 = load_w_cast(ctx, ctx.inp[f"b_{typ}_w1"], D, H4)
        W2, g2 = load_w_cast(ctx, ctx.inp[f"b_{typ}_w2"], H4, D)
        B1 = ar.tile([128, 32], F32)
        em.sdma(B1, ctx.inp[f"b_{typ}_b1_pp"])
        B2 = ar.tile([128, 8], F32)
        em.sdma(B2, ctx.inp[f"b_{typ}_b2_pp"])
        em.barrier()
        XT8 = ar.tiles(2, [128, DC, 512], BF16)
        H1S = ar.tile([128, 32 * 512], BF16)
        QEV = ar.tiles(2, [128, 512], BF16)
        VEV = ar.tiles(2, [128, D], BF16)
        ld_guard = [None, None]
        ev_guard = [None, None]
        vev_guard = [None, None]
        h1_guard = None
        for b in range(B):
            for ts in range(4):
                r = (b * 4 + ts) % 2
                lo = 512 * ts
                ldx = em.sdma(
                    XT8[r], ctx.XNS[b].rearrange("(dc p) t -> p dc t", p=128)
                    [:, :, lo:lo + 512], waits=[ld_guard[r]])
                lasth_all = None
                h1ev = [h1_guard, h1_guard]
                for hc in range(32):
                    key = hc % 2
                    ps = (P4a if key == 0 else P4b)[:, 0:512]
                    last = None
                    for dc in range(DC):
                        last = em.t(
                            lambda e, o=ps,
                            l=W1[:, dc * H4 + 128 * hc: dc * H4 + 128 * hc + 128],
                            r_=XT8[r][:, dc, :].squeeze(),
                            st=(dc == 0), sp=(dc == DC - 1):
                            e.matmul(o, l, r_, start=st, stop=sp),
                            waits=(([ldx, g1] if hc < 2 else [])
                                   + [h1ev[key]] if dc == 0 else ()))
                    lasth_all = em.c(
                        lambda e, o=H1S[:, 512 * hc:512 * (hc + 1)], i_=ps,
                        bb=B1[:, hc:hc + 1]:
                        e.activation(o, i_, AF.Gelu, bias=bb),
                        waits=[last])
                    h1ev[key] = lasth_all
                h1_guard = lasth_all
                if typ != "v":
                    for cc in range(DC):
                        key = cc % 2
                        ps = (P4a if key == 0 else P4b)[:, 512:1024]
                        last = None
                        for hc in range(32):
                            last = em.t(
                                lambda e, o=ps,
                                l=W2[:, hc * D + 128 * cc: hc * D + 128 * cc + 128],
                                r_=H1S[:, 512 * hc:512 * (hc + 1)],
                                st=(hc == 0), sp=(hc == 31):
                                e.matmul(o, l, r_, start=st, stop=sp),
                                waits=([lasth_all, g2, ev_guard[key]]
                                       if hc == 0 else ()))
                        rr = cc % 2
                        ev = em.c(lambda e, o=QEV[rr], i_=ps,
                                  bb=B2[:, cc:cc + 1]:
                                  e.activation(o, i_, AF.Identity, bias=bb),
                                  waits=[last, vev_guard[rr]])
                        ev_guard[key] = ev
                        dst = ctx.QS if typ == "q" else ctx.KS
                        vev_guard[rr] = em.sdma(
                            dst[b, 128 * cc:128 * (cc + 1), lo:lo + 512],
                            QEV[rr], waits=[ev])
                else:
                    for tq in range(4):
                        key = tq % 2
                        ps = (P4a if key == 0 else P4b)[:, 1024:2048]
                        last = None
                        for hc in range(32):
                            for ns in range(2):
                                last = em.t(
                                    lambda e, o=ps[:, 512 * ns:512 * (ns + 1)],
                                    l=H1S[:, 512 * hc + 128 * tq:
                                          512 * hc + 128 * tq + 128],
                                    r_=W2[:, hc * D + 512 * ns:
                                          hc * D + 512 * ns + 512],
                                    st=(hc == 0), sp=(hc == 31):
                                    e.matmul(o, l, r_, start=st, stop=sp),
                                    waits=([lasth_all, g2, ev_guard[key]]
                                           if (hc == 0 and ns == 0) else ()))
                        rr = tq % 2
                        ev = em.c(lambda e, o=VEV[rr], i_=ps:
                                  e.activation(o, i_, AF.Copy),
                                  waits=[last, vev_guard[rr]])
                        ev_guard[key] = ev
                        vev_guard[rr] = em.sdma(
                            ctx.VS[b, lo + 128 * tq:lo + 128 * (tq + 1), :],
                            VEV[rr], waits=[ev])
                ld_guard[r] = lasth_all
        em.barrier()
        ar.reset(mkt)

    PROJW, gp = load_w_cast(ctx, ctx.inp["b_proj_w"], D, D)
    PBJ = ar.tile([128, 8], F32)
    em.sdma(PBJ, ctx.inp["b_proj_b_pp"])
    VB2 = ar.tile([128, 8], F32)
    em.sdma(VB2, ctx.inp["b_v_b2_pp"])
    em.barrier()
    for b in range(B):
        mkb = ar.mark()
        QT = ar.tiles(DC, [128, T], BF16)
        KT = ar.tiles(DC, [128, T], BF16)
        VT = ar.tiles(16, [128, D], BF16)
        lq = None
        for dc in range(DC):
            em.sdma(QT[dc], ctx.QS[b, 128 * dc:128 * (dc + 1), :])
            em.sdma(KT[dc], ctx.KS[b, 128 * dc:128 * (dc + 1), :])
        for tc in range(16):
            lq = em.sdma(VT[tc], ctx.VS[b, 128 * tc:128 * (tc + 1), :])
        ATTN = ar.tiles(4, [128, T], BF16)
        ATTS = ar.tile([128, 16 * 512], BF16)
        NM = ar.tile([128, 64], F32)
        RS = ar.tile([128, 16], F32)
        RR = ar.tile([128, 16], F32)
        HH = ar.tile([64, T], BF16)
        em.barrier()

        hh_guard = None
        pg = {"a": None, "b": None}
        attn_guard = [None] * 4
        ev_guard = None
        for h in range(16):
            tile_i, off = h // 2, 64 * (h % 2)
            Qh = QT[tile_i][off:off + 64, :]
            Kh = KT[tile_i][off:off + 64, :]
            for sl in range(4):
                softmax_done = [None] * 4
                for i in range(4):
                    tc = 4 * sl + i
                    key = "a" if i % 2 == 0 else "b"
                    ps = P4a if key == "a" else P4b
                    last = None
                    for ns in range(4):
                        last = em.t(
                            lambda e, o=ps[:, 512 * ns:512 * (ns + 1)],
                            l=Qh[:, 128 * tc:128 * (tc + 1)],
                            r_=Kh[:, 512 * ns:512 * (ns + 1)]:
                            e.matmul(o, l, r_, start=True, stop=True),
                            waits=[pg[key]] if ns == 0 else ())
                    vm = em.v(lambda e, o=NM[:, tc:tc + 1], i_=ps:
                              e.reduce_max(o, i_, axis=AX.X, negate=True),
                              waits=[last])
                    vm2 = em.c(lambda e, o=NM[:, 16 + tc:17 + tc],
                               i_=NM[:, tc:tc + 1]:
                               e.mul(o, i_, 0.125), waits=[vm])
                    cx = em.c(lambda e, o=ATTN[i], i_=ps, tc=tc, ii=i:
                              e.activation(o, i_, AF.Exp, scale=0.125,
                                           bias=NM[:, 16 + tc:17 + tc],
                                           accum_out=RS[:, ii:ii + 1]),
                              waits=[vm2, attn_guard[i]])
                    pg[key] = cx
                    vr = em.v(lambda e, o=RR[:, i:i + 1],
                              i_=RS[:, i:i + 1]:
                              e.reciprocal(o, i_), waits=[cx])
                    softmax_done[i] = em.v(
                        lambda e, o=ATTN[i], i=i:
                        e.tensor_scalar(o, o, RR[:, i:i + 1], None, ALU.mult),
                        waits=[vr])
                for sc in range(16):
                    lastt = None
                    for i in range(4):
                        lastt = em.t(
                            lambda e, o=P4a_bf[:, 128 * i:128 * (i + 1)],
                            i_=ATTN[i][:, 128 * sc:128 * (sc + 1)]:
                            e.transpose(o, i_, ctx.IDENT),
                            waits=([softmax_done[i], ev_guard, pg["a"]]
                                   if i == 0 else [softmax_done[i]]))
                    ev_guard = em.v(
                        lambda e, o=ATTS[:, 512 * sc:512 * (sc + 1)],
                        i_=P4a_bf[:, 0:512]:
                        e.tensor_copy(o, i_), waits=[lastt])
                    attn_guard = [lastt] * 4 if sc == 15 else attn_guard
                lastv = None
                for sc in range(16):
                    lastv = em.t(
                        lambda e, o=P4b[0:64, 512:1024],
                        l=VT[sc][:, 64 * h:64 * h + 64],
                        r_=ATTS[:, 512 * sc:512 * (sc + 1)],
                        st=(sc == 0), sp=(sc == 15):
                        e.matmul(o, l, r_, start=st, stop=sp),
                        waits=([ev_guard, pg["b"], lq] if sc == 0 else ()))
                ce = em.c(lambda e, o=HH[:, 512 * sl:512 * (sl + 1)],
                          i_=P4b[0:64, 512:1024],
                          bb=VB2[off // 64 if False else 0:1, 0:1]:
                          e.activation(o, i_, AF.Copy), waits=[lastv, hh_guard])
                pg["a"] = ce
                pg["b"] = ce
                ev_guard = None
            sh = em.sdma(ctx.HS[h], HH, waits=[pg["a"]])
            hh_guard = sh
        em.barrier()

        ar.reset(mkb)
        OTL = ar.tiles(DC, [128, T], BF16)
        XF = ar.tiles(2, [128, T], F32)
        XNEW = ar.tiles(2, [128, T], F32)
        hl = None
        for cc in range(DC):
            hl = em.sdma(OTL[cc], ctx.HS.rearrange("h p t -> (h p) t")
                         [128 * cc:128 * (cc + 1), :])
        vb = None
        for cc in range(DC):
            vb = em.v(lambda e, o=OTL[cc], bb=VB2[:, cc:cc + 1]:
                      e.tensor_scalar(o, o, bb, None, ALU.add),
                      waits=[hl] if cc == 0 else ())
        st_guard = [None, None]
        xf_guard = [None, None]
        pgo = {"a": None, "b": None}
        for dc in range(DC):
            r = dc % 2
            key = "a" if dc % 2 == 0 else "b"
            ps = P4a if key == "a" else P4b
            ldx = em.sdma(XF[r], src_of(b, dc), waits=[xf_guard[r]])
            last = None
            for ns in range(4):
                for cc in range(DC):
                    last = em.t(
                        lambda e, o=ps[:, 512 * ns:512 * (ns + 1)],
                        l=PROJW[:, cc * D + 128 * dc: cc * D + 128 * dc + 128],
                        r_=OTL[cc][:, 512 * ns:512 * (ns + 1)],
                        st=(cc == 0), sp=(cc == DC - 1):
                        e.matmul(o, l, r_, start=st, stop=sp),
                        waits=([pgo[key], gp, vb]
                               if (ns == 0 and cc == 0) else ()))
            vres = em.v(lambda e, o=XNEW[r], i_=ps,
                        bb=PBJ[:, dc:dc + 1], x=XF[r]:
                        e.scalar_tensor_tensor(o, i_, bb, x, ALU.add, ALU.add),
                        waits=[last, ldx, st_guard[r]])
            pgo[key] = vres
            xf_guard[r] = vres
            st_guard[r] = em.sdma(dst_of(b, dc), XNEW[r], waits=[vres])
        em.barrier()
        ar.reset(mkb)
    ar.reset(stage_mk)


def stage_emit(ctx, src_of):
    em, ar = ctx.em, ctx.ar
    mk = ar.mark()
    TL = ar.tiles(2, [128, T], F32)
    g = [None, None]
    for b in range(B):
        for dc in range(DC):
            r = (b * DC + dc) % 2
            ld = em.sdma(TL[r], src_of(b, dc), waits=[g[r]])
            g[r] = em.sdma(ctx.OUT[b, 128 * dc:128 * (dc + 1), :], TL[r],
                           waits=[ld])
    em.barrier()
    ar.reset(mk)


def build_nc(stages):
    nc = bass.Bass("TRN2", target_bir_lowering=False)
    ctx = Ctx = type("C", (), {})()
    ctx.nc = nc

    inp = {}
    def di(name, shape, dtype=F32):
        inp[name] = nc.dram_tensor(name, list(shape), dtype,
                                   kind="ExternalInput").ap()
    di("Xc", (B, D, T))
    if "sl5" in stages:
        di("L1_attn_a_wq", (D, D)); di("L1_attn_a_wk", (D, D))
        di("L1_attn_a_wv", (D, D)); di("L1_attn_a_wo", (D, D))
        di("L1_attn_a_bq_pp", (128, 8)); di("L1_attn_a_bk_pp", (128, 8))
        di("L1_attn_a_bv_pp", (128, 8)); di("L1_attn_a_bo_pp", (128, 8))
        di("mask5", (16, 128, T), U8)
    if "sl1" in stages:
        di("L0_W_in", (D, 4096))
        di("L0_b_in_64", (64, 64))
        di("L0_W_out", (D, D))
        di("L0_b_out_pp", (128, 8))
        ctx.HS = nc.dram_tensor("HS", [16, 64, T], BF16, kind="Internal").ap()
    if "sl6" in stages or "sl8" in stages:
        di("f_wqkv", (2, D, 1536))
        di("f_wo", (2, 512, D))
        di("f_w1a", (2, D, 2816))
        di("f_w1g", (2, D, 2816))
        di("f_w2p", (2, 2816, D))
        di("f_b1a_pp", (2, 128, 22))
        di("f_b1g_pp", (2, 128, 22))
        di("f_b2_pp", (2, 128, 8))
        di("f_lng_pp", (128, 16))
        di("f_lnb_pp", (128, 16))
        di("f_tabs", (128, 4 * T))
        di("f_qsks_pp", (128, 8))
        di("f_h8all", (128, 32), BF16)
        di("f_sel8", (8, 512), BF16)
        di("f_rperm", (128, 128), BF16)
        di("mask6", (4, 128, 1024), U8)
        di("mask6w0", (4, 128, 1024), U8)
        if "sl1" not in stages and "sl7" not in stages:
            ctx.HS = nc.dram_tensor("HS", [16, 64, T], BF16,
                                    kind="Internal").ap()
        ctx.RNS = nc.dram_tensor("RNS", [8, T], BF16, kind="Internal").ap()
    if "sl7" in stages:
        for typ in ("q", "k", "v"):
            di(f"b_{typ}_w1", (D, 4096))
            di(f"b_{typ}_w2", (4096, D))
            di(f"b_{typ}_b1_pp", (128, 32))
            di(f"b_{typ}_b2_pp", (128, 8))
        di("b_proj_w", (D, D))
        di("b_proj_b_pp", (128, 8))
        ctx.XNS = nc.dram_tensor("XNS", [B, D, T], BF16, kind="Internal").ap()
        ctx.QS = nc.dram_tensor("QS", [B, D, T], BF16, kind="Internal").ap()
        ctx.KS = nc.dram_tensor("KS", [B, D, T], BF16, kind="Internal").ap()
        ctx.VS = nc.dram_tensor("VS", [B, T, D], BF16, kind="Internal").ap()
        if "sl1" not in stages:
            ctx.HS = nc.dram_tensor("HS", [16, 64, T], BF16,
                                    kind="Internal").ap()
    if "sl3" in stages:
        di("router_w_pp", (128, 8))
        di("router_b_11", (1, 1))
    if "sl2" in stages or "sl4" in stages:
        di("phi_wT", (D, 2048))
        di("phi_b_bc", (128, 2048))
        di("expert_w", (64, D, 1024))
        di("expert_b", (64, 1024))
        ctx.CS = nc.dram_tensor("CS", [B, 16, 128, 2048], BF16,
                                kind="Internal").ap()
        ctx.CSUMS = nc.dram_tensor("CSUMS", [B, 128, 16 * 64], F32,
                                   kind="Internal").ap()
        ctx.CCIN = nc.dram_tensor("CCIN", [2048, D], F32, kind="Internal").ap()
        ctx.CCOUT = nc.dram_tensor("CCOUT", [2048, D], F32,
                                   kind="Internal").ap()
    ctx.inp = inp

    ctx.OUT = nc.dram_tensor("OUT", [B, D, T], F32, kind="ExternalOutput").ap()
    ctx.S = nc.dram_tensor("S", [B, D, T], F32, kind="Internal").ap()
    ctx.DBGF = ctx.DBGB = None
    if DEBUG:
        ctx.DBGF = nc.dram_tensor("DBGF", [12, 128, T], F32,
                                  kind="ExternalOutput").ap()
        ctx.DBGB = nc.dram_tensor("DBGB", [12, 128, T], BF16,
                                  kind="ExternalOutput").ap()

    ident = nc.inline_tensor(np.eye(128, dtype=ml_dtypes.bfloat16),
                             name="identc").ap()
    identf = nc.inline_tensor(np.eye(128, dtype=np.float32),
                              name="identfc").ap()
    inda_np = np.zeros((1, 128), dtype=ml_dtypes.bfloat16); inda_np[0, 0:64] = 1
    indb_np = np.zeros((1, 128), dtype=ml_dtypes.bfloat16); indb_np[0, 64:128] = 1
    inda = nc.inline_tensor(inda_np, name="indac").ap()
    indb = nc.inline_tensor(indb_np, name="indbc").ap()
    onesc = nc.inline_tensor(np.ones((128, 1), dtype=ml_dtypes.bfloat16),
                             name="onescc").ap()
    onesr = nc.inline_tensor(np.ones((1, 128), dtype=ml_dtypes.bfloat16),
                             name="onesrc").ap()

    em = Em(nc)
    ctx.em = em
    ar = Arena(nc, "arena", 205 * 1024)
    ctx.ar = ar
    ctx.P4a = nc.alloc_psum_tensor("p4a", [128, 2048], F32).ap()
    ctx.P4b = nc.alloc_psum_tensor("p4b", [128, 2048], F32).ap()
    ctx.P4a_bf = ctx.P4a.bitcast(BF16)
    ctx.P4b_bf = ctx.P4b.bitcast(BF16)

    ctx.IDENT = ar.tile([128, 128], BF16)
    ctx.IDENTF = ar.tile([128, 128], F32)
    ctx.ONESC = ar.tile([128, 1], BF16)
    ctx.ONESR = ar.tile([1, 128], BF16)
    ctx.INDA = ar.tile([1, 128], BF16)
    ctx.INDB = ar.tile([1, 128], BF16)
    em.sdma(ctx.IDENT, ident)
    em.sdma(ctx.IDENTF, identf)
    em.sdma(ctx.INDA, inda)
    em.sdma(ctx.INDB, indb)
    em.sdma(ctx.ONESC, onesc)
    em.sdma(ctx.ONESR, onesr)
    cvals = [1e-5, 1e-4, 1.0, 0.5, -0.5, 2.0, 1.0 / 32.0, 1.0 / 1024.0,
             8.0, 0.125, -2.0, -1.0]
    carr = np.tile(np.asarray(cvals, np.float32)[None, :], (128, 1))
    cbuf = nc.inline_tensor(carr, name="constpool").ap()
    CON = ar.tile([128, len(cvals)], F32)
    em.sdma(CON, cbuf)
    for i, v in enumerate(cvals):
        nc.const_aps.register(F32, float(v), CON[:, i:i + 1])
    ctx.CON = CON
    em.barrier()

    state = {"first": True}
    def src_of(b, dc):
        if state["first"]:
            return inp["Xc"][b, 128 * dc:128 * (dc + 1), :]
        return ctx.S[b, 128 * dc:128 * (dc + 1), :]
    def dst_of(b, dc):
        return ctx.S[b, 128 * dc:128 * (dc + 1), :]

    impl = {
        "sl5": lambda c, s, d: stage_sl5(c, s, d),
        "sl2": lambda c, s, d: stage_sl2(c, s, d, "sl2"),
        "sl4": lambda c, s, d: stage_sl2(c, s, d, "sl4"),
        "sl3": lambda c, s, d: stage_sl3(c, s, d),
        "sl1": lambda c, s, d: stage_sl1(c, s, d),
        "sl7": lambda c, s, d: stage_sl7(c, s, d),
        "sl6": lambda c, s, d: stage_sl6(c, s, d, "sl6"),
        "sl8": lambda c, s, d: stage_sl6(c, s, d, "sl8"),
    }
    for st in stages:
        if st in impl:
            impl[st](ctx, src_of, dst_of)
            state["first"] = False
    if state["first"]:
        stage_emit(ctx, lambda b, dc: inp["Xc"][b, 128 * dc:128 * (dc + 1), :])
    else:
        stage_emit(ctx, lambda b, dc: ctx.S[b, 128 * dc:128 * (dc + 1), :])

    with nc.allow_low_precision(
            reason="bf16 matmul operands sanctioned by problem tolerance"):
        with nc.Block() as blk:
            @blk.sync
            def _(e):
                for f in em.q["sync"]:
                    f(e)
            @blk.tensor
            def _(e):
                for f in em.q["tensor"]:
                    f(e)
            @blk.vector
            def _(e):
                for f in em.q["vector"]:
                    f(e)
            @blk.scalar
            def _(e):
                for f in em.q["scalar"]:
                    f(e)
            @blk.gpsimd
            def _(e):
                for f in em.q["gpsimd"]:
                    f(e)
    return nc


def host_inputs(inputs, stages):
    X = np.asarray(inputs["L0_attn_a_X"], np.float32)
    Xc = np.ascontiguousarray(
        X.reshape(NCORES, B, T, D).transpose(0, 1, 3, 2))
    shared = {}
    if "sl5" in stages:
        shared.update({
            "L1_attn_a_wq": np.ascontiguousarray(inputs["L1_attn_a_wq"], np.float32),
            "L1_attn_a_wk": np.ascontiguousarray(inputs["L1_attn_a_wk"], np.float32),
            "L1_attn_a_wv": np.ascontiguousarray(inputs["L1_attn_a_wv"], np.float32),
            "L1_attn_a_wo": np.ascontiguousarray(inputs["L1_attn_a_wo"], np.float32),
            "L1_attn_a_bq_pp": _pp(inputs["L1_attn_a_bq"]),
            "L1_attn_a_bk_pp": _pp(inputs["L1_attn_a_bk"]),
            "L1_attn_a_bv_pp": _pp(inputs["L1_attn_a_bv"]),
            "L1_attn_a_bo_pp": _pp(inputs["L1_attn_a_bo"]),
        })
        row = np.arange(128)[:, None]
        col = np.arange(T)[None, :]
        m5 = np.stack([(col >= 128 * tc + row + 2).astype(np.uint8)
                       for tc in range(16)])
        shared["mask5"] = m5
    if "sl1" in stages:
        shared["L0_W_in"] = np.ascontiguousarray(inputs["L0_attn_a_W_in"],
                                                 dtype=np.float32)
        shared["L0_b_in_64"] = np.ascontiguousarray(
            np.asarray(inputs["L0_attn_a_b_in"], np.float32)
            .reshape(16, 4, 64).transpose(2, 0, 1).reshape(64, 64))
        shared["L0_W_out"] = np.ascontiguousarray(inputs["L0_attn_a_W_out"],
                                                  dtype=np.float32)
        shared["L0_b_out_pp"] = _pp(inputs["L0_attn_a_b_out"])
    if "sl6" in stages or "sl8" in stages:
        w1 = np.asarray(inputs["L1_ffn_w1"], np.float32)
        w2 = np.asarray(inputs["L1_ffn_w2"], np.float32)
        b1 = np.asarray(inputs["L1_ffn_b1"], np.float32)
        HID = 2730
        w1a = np.zeros((2, D, 2816), np.float32)
        w1g = np.zeros((2, D, 2816), np.float32)
        w1a[:, :, :HID] = w1[:, :, :HID]
        w1g[:, :, :HID] = w1[:, :, HID:]
        w2p = np.zeros((2, 2816, D), np.float32)
        w2p[:, :HID, :] = w2
        shared["f_wqkv"] = np.ascontiguousarray(inputs["L1_ffn_wqkv"],
                                                dtype=np.float32)
        shared["f_wo"] = np.ascontiguousarray(inputs["L1_ffn_wo"],
                                              dtype=np.float32)
        shared["f_w1a"] = w1a
        shared["f_w1g"] = w1g
        shared["f_w2p"] = w2p
        shared["f_b1a_pp"] = np.stack([_pp(b1[l, :HID], pad_to=2816)
                                       for l in range(2)])
        shared["f_b1g_pp"] = np.stack([_pp(b1[l, HID:], pad_to=2816)
                                       for l in range(2)])
        shared["f_b2_pp"] = np.stack([_pp(np.asarray(inputs["L1_ffn_b2"])[l])
                                      for l in range(2)])
        shared["f_lng_pp"] = np.concatenate(
            [_pp(np.asarray(inputs["L1_ffn_ln_g"])[l]) for l in range(2)], axis=1)
        shared["f_lnb_pp"] = np.concatenate(
            [_pp(np.asarray(inputs["L1_ffn_ln_b"])[l]) for l in range(2)], axis=1)
        half_idx = np.arange(0, 64, 2, dtype=np.float64)
        inv_freq = 1.0 / 10000.0 ** (half_idx / 64)
        tt = np.arange(T, dtype=np.float64)
        freqs = np.concatenate([tt[:, None] * inv_freq[None, :]] * 2, -1)
        basep = (half_idx + 0.4 * 64) / (1.4 * 64)
        power = (tt - T // 2) / 256.0
        scale = np.concatenate([basep[None, :] ** power[:, None]] * 2, -1)
        cosf, sinf = np.cos(freqs), np.sin(freqs)
        tabs = [cosf * scale, sinf * scale, cosf / scale, sinf / scale]
        shared["f_tabs"] = np.ascontiguousarray(np.concatenate(
            [np.tile(x.T, (2, 1)) for x in tabs], axis=1).astype(np.float32))
        qs = np.asarray(inputs["L1_ffn_q_scale"], np.float32)
        ks = np.asarray(inputs["L1_ffn_k_scale"], np.float32)
        cols = []
        for l in range(2):
            for vec in (qs[l], np.roll(qs[l], -32), ks[l], np.roll(ks[l], -32)):
                cols.append(np.tile(vec, 2))
        shared["f_qsks_pp"] = np.ascontiguousarray(np.stack(cols, axis=1))
        H8 = np.zeros((128, 32), np.float32)
        SEL = np.zeros((8, 512), np.float32)
        for j in range(4):
            H8[0:64, 8 * j + 2 * j] = 1.0
            H8[64:128, 8 * j + 2 * j + 1] = 1.0
            SEL[2 * j, 128 * j:128 * j + 64] = 1.0
            SEL[2 * j + 1, 128 * j + 64:128 * j + 128] = 1.0
        shared["f_h8all"] = H8.astype(ml_dtypes.bfloat16)
        shared["f_sel8"] = SEL.astype(ml_dtypes.bfloat16)
        Rp = np.zeros((128, 128), np.float32)
        for blk in range(2):
            o = 64 * blk
            for c in range(32):
                Rp[o + c + 32, o + c] = -1.0
            for c in range(32, 64):
                Rp[o + c - 32, o + c] = 1.0
        shared["f_rperm"] = Rp.astype(ml_dtypes.bfloat16)
        row = np.arange(128)[:, None]
        jk = np.arange(1024)[None, :]
        m6 = np.stack([((0 <= 128 * c4 + row - jk + 512)
                        & (128 * c4 + row - jk + 512 <= 512)).astype(np.uint8)
                       for c4 in range(4)])
        shared["mask6"] = m6
        m6w0 = m6.copy()
        m6w0[:, :, 0:512] = 0
        shared["mask6w0"] = m6w0
    if "sl7" in stages:
        for typ in ("q", "k", "v"):
            shared[f"b_{typ}_w1"] = np.ascontiguousarray(
                inputs[f"L1_attn_b_{typ}_w1"], dtype=np.float32)
            shared[f"b_{typ}_w2"] = np.ascontiguousarray(
                inputs[f"L1_attn_b_{typ}_w2"], dtype=np.float32)
            shared[f"b_{typ}_b1_pp"] = _pp(inputs[f"L1_attn_b_{typ}_b1"])
            shared[f"b_{typ}_b2_pp"] = _pp(inputs[f"L1_attn_b_{typ}_b2"])
        shared["b_proj_w"] = np.ascontiguousarray(
            inputs["L1_attn_b_proj_w"], dtype=np.float32)
        shared["b_proj_b_pp"] = _pp(inputs["L1_attn_b_proj_b"])
        shared["b_v_b2_pp"] = _pp(inputs["L1_attn_b_v_b2"])
    if "sl3" in stages:
        shared["router_w_pp"] = _pp(inputs["L0_attn_b_router_w"].reshape(-1))
        shared["router_b_11"] = np.asarray(
            inputs["L0_attn_b_router_b"], np.float32).reshape(1, 1)
    if "sl2" in stages or "sl4" in stages:
        shared["phi_wT"] = np.ascontiguousarray(
            np.asarray(inputs["L0_ffn_phi_w"], np.float32).T)
        shared["phi_b_bc"] = np.ascontiguousarray(np.tile(
            np.asarray(inputs["L0_ffn_phi_b"], np.float32)[None, :], (128, 1)))
        shared["expert_w"] = np.ascontiguousarray(
            inputs["L0_ffn_expert_w"], dtype=np.float32)
        shared["expert_b"] = np.ascontiguousarray(
            inputs["L0_ffn_expert_b"], dtype=np.float32)
    return Xc, shared


def kernel(**inputs):
    stages = list(STAGES)
    Xc, shared = host_inputs(inputs, stages)
    nc = build_nc(stages)
    in_maps = [dict(shared, Xc=np.ascontiguousarray(Xc[c]))
               for c in range(NCORES)]
    res = run_bass_kernel_spmd(nc, in_maps, core_ids=list(range(NCORES)))
    out = np.concatenate(
        [np.asarray(res.results[c]["OUT"]).transpose(0, 2, 1)
         for c in range(NCORES)], axis=0)
    return np.ascontiguousarray(out)


# version 05
# speedup vs baseline: 1.0009x; 1.0009x over previous
import numpy as np
import ml_dtypes
import concourse.bass as bass
import concourse.mybir as mybir
from concourse.bass_utils import run_bass_kernel_spmd

F32 = mybir.dt.float32
BF16 = mybir.dt.bfloat16
U8 = mybir.dt.uint8
AF = mybir.ActivationFunctionType
ALU = mybir.AluOpType
AX = mybir.AxisListType

B = 4
T = 2048
D = 1024
DC = D // 128
NCORES = 8

ENGS = ("sync", "tensor", "vector", "scalar", "gpsimd")

STAGES = ["sl1", "sl2", "sl3", "sl4", "sl5", "sl6", "sl7", "sl8"]

DEBUG = False
SL6_LIMIT = 99
SL6_SUB = 99

NOW = ()


def dbg_f(ctx, slot, tile, waits=()):
    if getattr(ctx, "DBGF", None) is not None:
        p, f = tile.shape[0], int(np.prod(tile.shape[1:]))
        ctx.em.sdma(ctx.DBGF[slot][0:p, 0:f], tile, waits=waits)


def dbg_b(ctx, slot, tile, waits=()):
    if getattr(ctx, "DBGB", None) is not None:
        p, f = tile.shape[0], int(np.prod(tile.shape[1:]))
        ctx.em.sdma(ctx.DBGB[slot][0:p, 0:f], tile, waits=waits)


class Em:
    def __init__(self, nc):
        self.nc = nc
        self.q = {e: [] for e in ENGS}
        self.sems = {}
        self.cnt = {}
        for name in ("S", "G", "T", "V", "C", "P", "CC"):
            self.sems[name] = nc.alloc_semaphore("sem_" + name)
            self.cnt[name] = 0

    def op(self, eng, build, sem=None, waits=(), inc=1):
        h = self.sems
        for w in waits:
            if w is None:
                continue
            sn, sv = w
            if sv and sv > 0:
                self.q[eng].append(lambda e, hh=h[sn], sv=sv: e.wait_ge(hh, sv))
        if sem in ("V", "C", "P") and self.cnt[sem] > 0:
            self.q[eng].append(
                lambda e, hh=h[sem], sv=self.cnt[sem]: e.wait_ge(hh, sv))
        if sem is None:
            self.q[eng].append(lambda e, b=build: b(e))
            return None
        self.cnt[sem] += inc
        self.q[eng].append(
            lambda e, b=build, hh=h[sem], k=inc: b(e).then_inc(hh, k))
        return (sem, self.cnt[sem])

    def barrier(self):
        snap = [(k, v) for k, v in self.cnt.items() if v > 0]
        for eng in ENGS:
            for sn, sv in snap:
                self.q[eng].append(
                    lambda e, hh=self.sems[sn], sv=sv: e.wait_ge(hh, sv))

    def t(self, build, waits=()):
        return self.op("tensor", build, "T", waits)

    def v(self, build, waits=()):
        return self.op("vector", build, "V", waits)

    def c(self, build, waits=()):
        return self.op("scalar", build, "C", waits)

    def p(self, build, waits=()):
        return self.op("gpsimd", build, "P", waits)

    def sdma(self, out, in_, waits=(), **kw):
        return self.op("sync", lambda e: e.dma_start(out=out, in_=in_, **kw),
                       "S", waits, inc=16)

    def gdma(self, out, in_, waits=(), **kw):
        return self.op("gpsimd", lambda e: e.dma_start(out=out, in_=in_, **kw),
                       "G", waits, inc=16)


class Arena:
    _uid = 0

    def __init__(self, nc, name, nbytes):
        self.nc = nc
        self.h = nc.alloc_sbuf_tensor(name, [128, nbytes], U8)
        self.base = nc.lookup_mloc(self.h).addr
        self.off = 0
        self.peak = 0
        self.nbytes = nbytes

    def tile(self, shape, dtype):
        per_part = int(np.prod(shape[1:])) * mybir.dt.size(dtype)
        off = (self.off + 31) & ~31
        assert off + per_part <= self.nbytes, \
            f"arena overflow: {off}+{per_part} > {self.nbytes}"
        self.off = off + per_part
        self.peak = max(self.peak, self.off)
        Arena._uid += 1
        h = self.nc.alloc_sbuf_tensor_at(
            f"a{Arena._uid}", list(shape), dtype, offset=self.base + off)
        return h.ap()

    def tiles(self, n, shape, dtype):
        return [self.tile(shape, dtype) for _ in range(n)]

    def mark(self):
        return self.off

    def reset(self, off):
        self.off = off


def _pp(bvec, pad_to=None):
    b = np.asarray(bvec, np.float32).reshape(-1)
    if pad_to is not None and b.size < pad_to:
        b = np.concatenate([b, np.zeros(pad_to - b.size, np.float32)])
    n = b.size
    assert n % 128 == 0
    return np.ascontiguousarray(b.reshape(n // 128, 128).T)


def ln_x_norm(ctx, src, xn, rows_out=None):
    em, ar = ctx.em, ctx.ar
    P4a, P4b = ctx.P4a, ctx.P4b
    mk = ar.mark()
    XF = ar.tiles(2, [128, T], F32)
    XB = ar.tiles(DC, [128, T], BF16)
    SQ = ar.tiles(2, [128, T], BF16)
    ROWS = ar.tile([1, 3 * T], F32)
    ROWm = ROWS[:, 0:T]
    ROWmsq = ROWS[:, T:2 * T]
    ROWvar = ROWS[:, 2 * T:3 * T]
    ROWsd = ROWS[:, T:2 * T]
    ROWr = ROWS[:, 2 * T:3 * T]
    ROWSB = ar.tile([1, 2 * T], BF16)
    ROWrb = ROWSB[:, 0:T]
    ROWmrb = ROWSB[:, T:2 * T]
    RBB = ar.tile([128, T], BF16)
    MRB = ar.tile([128, T], BF16)

    ld = [None, None]
    sq_guard = [None, None]
    cast_done = [None] * DC
    last_mm = None
    for dc in range(DC):
        r = dc % 2
        ld[r] = em.sdma(XF[r], src(dc),
                        waits=[cast_done[dc - 2]] if dc >= 2 else ())
        cast_done[dc] = em.c(
            lambda e, o=XB[dc], i=XF[r]: e.activation(o, i, AF.Copy),
            waits=[ld[r]])
        sq = em.v(lambda e, o=SQ[r], i=XB[dc]: e.tensor_tensor(o, i, i, ALU.mult),
                  waits=[cast_done[dc], sq_guard[r]])
        for ns in range(4):
            em.t(lambda e, o=P4b[0:1, 512 * ns:512 * (ns + 1)], i=XB[dc],
                 ns=ns, st=(dc == 0), sp=(dc == DC - 1):
                 e.matmul(o, ctx.ONESC, i[:, 512 * ns:512 * (ns + 1)],
                          start=st, stop=sp),
                 waits=[cast_done[dc]] if ns == 0 else ())
        for ns in range(4):
            last_mm = em.t(lambda e, o=P4a[0:1, 512 * ns:512 * (ns + 1)], i=SQ[r],
                           ns=ns, st=(dc == 0), sp=(dc == DC - 1):
                           e.matmul(o, ctx.ONESC, i[:, 512 * ns:512 * (ns + 1)],
                                    start=st, stop=sp),
                           waits=[sq] if ns == 0 else ())
        sq_guard[r] = last_mm

    v1 = em.v(lambda e: e.tensor_scalar(ROWm, P4b[0:1, :], 1.0 / D, None, ALU.mult),
              waits=[last_mm])
    em.v(lambda e: e.tensor_tensor(ROWmsq, ROWm, ROWm, ALU.mult))
    v3 = em.v(lambda e: e.scalar_tensor_tensor(ROWvar, P4a[0:1, :], 1.0 / D,
                                               ROWmsq, ALU.mult, ALU.subtract))
    c1 = em.c(lambda e: e.activation(ROWsd, ROWvar, AF.Sqrt, bias=1e-5),
              waits=[v3])
    cr_ = em.v(lambda e: e.reciprocal(ROWr, ROWsd), waits=[c1])
    v4 = em.v(lambda e: e.tensor_copy(ROWrb, ROWr), waits=[cr_])
    v5 = em.v(lambda e: e.tensor_tensor(ROWmrb, ROWm, ROWr, ALU.mult))
    bc_r = None
    for ns in range(4):
        bc_r = em.t(lambda e, ns=ns: e.matmul(
            P4a[:, 512 * ns:512 * (ns + 1)], ctx.ONESR,
            ROWrb[:, 512 * ns:512 * (ns + 1)], start=True, stop=True),
            waits=[v4, v1] if ns == 0 else ())
    bc_mr = None
    for ns in range(4):
        bc_mr = em.t(lambda e, ns=ns: e.matmul(
            P4b[:, 512 * ns:512 * (ns + 1)], ctx.ONESR,
            ROWmrb[:, 512 * ns:512 * (ns + 1)], start=True, stop=True),
            waits=[v5] if ns == 0 else ())
    e1 = em.c(lambda e: e.activation(RBB, P4a[:, :], AF.Copy), waits=[bc_r])
    e2 = em.c(lambda e: e.activation(MRB, P4b[:, :], AF.Copy), waits=[bc_mr])
    if rows_out is not None:
        em.v(lambda e, o=rows_out[0]: e.tensor_copy(o, ROWm))
        em.v(lambda e, o=rows_out[1]: e.tensor_copy(o, ROWr))
    vdone = None
    for dc in range(DC):
        em.v(lambda e, o=xn[dc], a=XB[dc]: e.tensor_tensor(o, a, RBB, ALU.mult),
             waits=[e1, e2] if dc == 0 else ())
        vdone = em.v(lambda e, o=xn[dc]: e.tensor_tensor(o, o, MRB, ALU.subtract))
    ar.reset(mk)
    return vdone


def load_w_cast(ctx, hbm_ap, kdim, mdim, waits=()):
    kc = kdim // 128
    w = ctx.ar.tile([128, kc * mdim], BF16)
    src = hbm_ap.rearrange("(kc p) m -> p kc m", p=128)
    dst = w.rearrange("p (kc m) -> p kc m", kc=kc)
    g = ctx.em.gdma(dst, src, waits=waits)
    return w, g


def stage_sl5(ctx, src_of, dst_of):
    em, ar = ctx.em, ctx.ar
    P4a, P4b = ctx.P4a, ctx.P4b
    P4a_bf = ctx.P4a_bf
    stage_mk = ar.mark()

    BIAS = ar.tile([128, 32], F32)
    em.sdma(BIAS[:, 0:8], ctx.inp["L1_attn_a_bq_pp"])
    em.sdma(BIAS[:, 8:16], ctx.inp["L1_attn_a_bk_pp"])
    em.sdma(BIAS[:, 16:24], ctx.inp["L1_attn_a_bv_pp"])
    em.sdma(BIAS[:, 24:32], ctx.inp["L1_attn_a_bo_pp"])
    em.barrier()

    for b in range(B):
        mkb = ar.mark()
        XNOT = ar.tiles(DC, [128, T], BF16)

        ln_x_norm(ctx, lambda dc: src_of(b, dc), XNOT)
        em.barrier()

        QT = ar.tiles(DC, [128, T], BF16)
        KT = ar.tiles(DC, [128, T], BF16)
        VT = ar.tiles(16, [128, D], BF16)

        mkp = ar.mark()
        WQ, _ = load_w_cast(ctx, ctx.inp["L1_attn_a_wq"], D, D)
        WK, _ = load_w_cast(ctx, ctx.inp["L1_attn_a_wk"], D, D)
        WV, gv = load_w_cast(ctx, ctx.inp["L1_attn_a_wv"], D, D)
        pg = {"a": None, "b": None}
        for wi, (Wt, bofs, OUTT) in enumerate(
                ((WQ, 0, QT), (WK, 8, KT))):
            for cc in range(DC):
                key = "a" if cc % 2 == 0 else "b"
                ps = P4a if key == "a" else P4b
                last = None
                for ns in range(4):
                    for dc in range(DC):
                        last = em.t(
                            lambda e, o=ps[:, 512 * ns:512 * (ns + 1)],
                            l=Wt[:, dc * D + 128 * cc: dc * D + 128 * cc + 128],
                            r=XNOT[dc][:, 512 * ns:512 * (ns + 1)],
                            st=(dc == 0), sp=(dc == DC - 1):
                            e.matmul(o, l, r, start=st, stop=sp),
                            waits=[pg[key], gv] if (ns == 0 and dc == 0) else ())
                pg[key] = em.c(
                    lambda e, o=OUTT[cc], i=ps, bb=BIAS[:, bofs + cc:bofs + cc + 1]:
                    e.activation(o, i, AF.Identity, bias=bb),
                    waits=[last])
        for tc in range(16):
            key = "a" if tc % 2 == 0 else "b"
            ps = (P4a if key == "a" else P4b)[:, 0:1024]
            last = None
            for ns in range(2):
                for dc in range(DC):
                    last = em.t(
                        lambda e, o=ps[:, 512 * ns:512 * (ns + 1)],
                        l=XNOT[dc][:, 128 * tc:128 * (tc + 1)],
                        r=WV[:, dc * D + 512 * ns: dc * D + 512 * ns + 512],
                        st=(dc == 0), sp=(dc == DC - 1):
                        e.matmul(o, l, r, start=st, stop=sp),
                        waits=[pg[key]] if (ns == 0 and dc == 0) else ())
            pg[key] = em.c(lambda e, o=VT[tc], i=ps: e.activation(o, i, AF.Copy),
                           waits=[last])
        em.barrier()

        ar.reset(mkp)
        SM = ar.tiles(2, [128, T], F32)
        MK = ar.tiles(2, [128, T], U8)
        ATTN = ar.tiles(4, [128, T], BF16)
        ATTS = ar.tile([128, 16 * 512], BF16)
        NM = ar.tile([128, 32], F32)
        RS = ar.tile([128, 16], F32)
        RR = ar.tile([128, 16], F32)
        OT = XNOT

        sm_guard = [None, None]
        mk_guard = [None, None]
        attn_guard = [None] * 4
        ev_guard = None
        pg = {"a": None, "b": None}
        for sl in range(4):
            softmax_done = [None] * 4
            for i in range(4):
                tc = 4 * sl + i
                key = "a" if i % 2 == 0 else "b"
                ps = P4a if key == "a" else P4b
                r2 = i % 2
                ml = em.sdma(MK[r2], ctx.inp["mask5"][tc], waits=[mk_guard[r2]])
                pmem = em.p(lambda e, o=SM[r2]: e.memset(o, -3.2e10),
                            waits=[sm_guard[r2]])
                last = None
                for ns in range(4):
                    for cc in range(DC):
                        last = em.t(
                            lambda e, o=ps[:, 512 * ns:512 * (ns + 1)],
                            l=QT[cc][:, 128 * tc:128 * (tc + 1)],
                            r=KT[cc][:, 512 * ns:512 * (ns + 1)],
                            st=(cc == 0), sp=(cc == DC - 1):
                            e.matmul(o, l, r, start=st, stop=sp),
                            waits=[pg[key]] if (ns == 0 and cc == 0) else ())
                mk_guard[r2] = em.v(
                    lambda e, o=SM[r2], mk=MK[r2], i_=ps:
                    e.copy_predicated(o, mk, i_),
                    waits=[last, pmem, ml])
                pg[key] = mk_guard[r2]
                vm = em.v(lambda e, o=NM[:, tc:tc + 1], i_=SM[r2]:
                          e.reduce_max(o, i_, axis=AX.X, negate=True))
                vm2 = em.c(lambda e, o=NM[:, 16 + tc:17 + tc], i_=NM[:, tc:tc + 1]:
                           e.mul(o, i_, 1.0 / 32.0), waits=[vm])
                cx = em.c(lambda e, o=ATTN[i], i_=SM[r2], tc=tc:
                          e.activation(o, i_, AF.Exp, scale=1.0 / 32.0,
                                       bias=NM[:, 16 + tc:17 + tc],
                                       accum_out=RS[:, tc:tc + 1]),
                          waits=[vm2, attn_guard[i]])
                sm_guard[r2] = cx
                vr = em.v(lambda e, o=RR[:, tc:tc + 1], i_=RS[:, tc:tc + 1]:
                          e.reciprocal(o, i_), waits=[cx])
                softmax_done[i] = em.v(
                    lambda e, o=ATTN[i], tc=tc:
                    e.tensor_scalar(o, o, RR[:, tc:tc + 1], None, ALU.mult),
                    waits=[vr])
            for sc in range(16):
                lastt = None
                for i in range(4):
                    lastt = em.t(
                        lambda e, o=P4a_bf[:, 128 * i:128 * (i + 1)],
                        i_=ATTN[i][:, 128 * sc:128 * (sc + 1)]:
                        e.transpose(o, i_, ctx.IDENT),
                        waits=([softmax_done[i], ev_guard, pg["a"]]
                               if i == 0 else [softmax_done[i]]))
                ev_guard = em.v(
                    lambda e, o=ATTS[:, 512 * sc:512 * (sc + 1)],
                    i_=P4a_bf[:, 0:512]:
                    e.tensor_copy(o, i_),
                    waits=[lastt])
                attn_guard = [lastt] * 4 if sc == 15 else attn_guard
            lastv = None
            for cc in range(DC):
                half = P4a if cc < 4 else P4b
                psb = half[:, 512 * (cc % 4):512 * (cc % 4 + 1)]
                for sc in range(16):
                    lastv = em.t(
                        lambda e, o=psb, l=VT[sc][:, 128 * cc:128 * (cc + 1)],
                        r=ATTS[:, 512 * sc:512 * (sc + 1)],
                        st=(sc == 0), sp=(sc == 15):
                        e.matmul(o, l, r, start=st, stop=sp),
                        waits=[ev_guard, pg["a"], pg["b"]]
                        if (sc == 0 and cc == 0) else ())
            ce = None
            for cc in range(DC):
                half = P4a if cc < 4 else P4b
                psb = half[:, 512 * (cc % 4):512 * (cc % 4 + 1)]
                ce = em.c(lambda e, o=OT[cc][:, 512 * sl:512 * (sl + 1)], i_=psb,
                          bb=BIAS[:, 16 + cc:16 + cc + 1]:
                          e.activation(o, i_, AF.Identity, bias=bb),
                          waits=[lastv] if cc == 0 else ())
            pg["a"] = ce
            pg["b"] = ce
            ev_guard = None
        em.barrier()

        ar.reset(mkp)
        WO, gw = load_w_cast(ctx, ctx.inp["L1_attn_a_wo"], D, D)
        XF = ar.tiles(2, [128, T], F32)
        XNEW = ar.tiles(2, [128, T], F32)
        st_guard = [None, None]
        xf_guard = [None, None]
        pgv = {"a": None, "b": None}
        for dc in range(DC):
            r = dc % 2
            key = "a" if dc % 2 == 0 else "b"
            ps = P4a if key == "a" else P4b
            ldx = em.sdma(XF[r], src_of(b, dc), waits=[xf_guard[r]])
            last = None
            for ns in range(4):
                for cc in range(DC):
                    last = em.t(
                        lambda e, o=ps[:, 512 * ns:512 * (ns + 1)],
                        l=WO[:, cc * D + 128 * dc: cc * D + 128 * dc + 128],
                        r=OT[cc][:, 512 * ns:512 * (ns + 1)],
                        st=(cc == 0), sp=(cc == DC - 1):
                        e.matmul(o, l, r, start=st, stop=sp),
                        waits=[pgv[key], gw] if (ns == 0 and cc == 0) else ())
            vres = em.v(lambda e, o=XNEW[r], i_=ps,
                        bb=BIAS[:, 24 + dc:24 + dc + 1], x=XF[r]:
                        e.scalar_tensor_tensor(o, i_, bb, x, ALU.add, ALU.add),
                        waits=[last, ldx, st_guard[r]])
            pgv[key] = vres
            xf_guard[r] = vres
            st_guard[r] = em.sdma(dst_of(b, dc), XNEW[r], waits=[vres])
        em.barrier()
        ar.reset(mkb)
    ar.reset(stage_mk)


def stage_sl1(ctx, src_of, dst_of):
    em, ar = ctx.em, ctx.ar
    P4a, P4b = ctx.P4a, ctx.P4b
    P4a_bf = ctx.P4a_bf
    stage_mk = ar.mark()
    D4 = 4096

    BIN = ar.tile([64, 64], F32)
    em.sdma(BIN, ctx.inp["L0_b_in_64"])
    BOUT = ar.tile([128, 8], F32)
    em.sdma(BOUT, ctx.inp["L0_b_out_pp"])
    em.barrier()

    for b in range(B):
        mkb = ar.mark()
        XN = ar.tiles(DC, [128, T], BF16)
        ln_x_norm(ctx, lambda dc: src_of(b, dc), XN)
        em.barrier()
        WIN, gwin = load_w_cast(ctx, ctx.inp["L0_W_in"], D, D4)
        WOUT, gwout = load_w_cast(ctx, ctx.inp["L0_W_out"], D, D)

        mkh = ar.mark()
        QT_ = ar.tile([64, T], BF16)
        KT_ = ar.tile([64, T], BF16)
        VT_ = ar.tile([64, T], BF16)
        GT_ = ar.tile([64, T], BF16)
        QN2 = ar.tile([64, T], BF16)
        SQX = ar.tile([64, T], BF16)
        Q2ROW = ar.tile([1, T], BF16)
        K2S = ar.tile([128, 16], F32)
        Q2B = ar.tile([128, T], BF16)
        VTM = ar.tile([128, 16, 68], BF16)
        WT = ar.tiles(2, [128, T], BF16)
        W1F = ar.tile([128, T], F32)
        W2F = ar.tile([128, T], F32)
        RS1a = W1F[0:1, :]
        RS1b = W2F[0:1, :]
        RSRB = Q2ROW
        HH = ar.tile([64, T], BF16)

        pmemset = em.p(lambda e: e.memset(VTM, 1.0))
        hg = {"q": None, "k": None, "v": None, "g": None, "hh": None,
              "pa": None, "pb": None}
        for h in range(16):
            for qi, (OUTT, gk) in enumerate(((QT_, "q"), (KT_, "k"),
                                             (VT_, "v"), (GT_, "g"))):
                key = "pa" if qi % 2 == 0 else "pb"
                ps = P4a if qi % 2 == 0 else P4b
                last = None
                for ns in range(4):
                    for dc in range(DC):
                        base = dc * D4 + 256 * h + 64 * qi
                        last = em.t(
                            lambda e, o=ps[0:64, 512 * ns:512 * (ns + 1)],
                            l=WIN[:, base:base + 64],
                            r_=XN[dc][:, 512 * ns:512 * (ns + 1)],
                            st=(dc == 0), sp=(dc == DC - 1):
                            e.matmul(o, l, r_, start=st, stop=sp),
                            waits=([gwin, hg[key], hg[gk]]
                                   if (ns == 0 and dc == 0) else ()))
                hg[gk] = em.c(
                    lambda e, o=OUTT, i_=ps[0:64, :],
                    bb=BIN[:, 4 * h + qi:4 * h + qi + 1]:
                    e.activation(o, i_, AF.Identity, bias=bb),
                    waits=[last])
                hg[key] = hg[gk]
            vq2 = em.v(lambda e, o=QN2, i_=QT_:
                       e.tensor_scalar(o, i_, -2.0, None, ALU.mult),
                       waits=[hg["q"]])
            vsqq = em.v(lambda e, o=SQX, i_=QT_:
                        e.tensor_tensor(o, i_, i_, ALU.mult))
            lastq = None
            for ns in range(4):
                lastq = em.t(lambda e, o=P4a[0:1, 512 * ns:512 * (ns + 1)],
                             i_=SQX, ns=ns:
                             e.matmul(o, ctx.ONESC[0:64, :],
                                      i_[:, 512 * ns:512 * (ns + 1)],
                                      start=True, stop=True),
                             waits=[vsqq, hg["pa"]] if ns == 0 else ())
            eq2 = em.v(lambda e, o=Q2ROW, i_=P4a[0:1, :]:
                       e.tensor_copy(o, i_), waits=[lastq])
            vsqk = em.v(lambda e, o=SQX, i_=KT_:
                        e.tensor_tensor(o, i_, i_, ALU.mult),
                        waits=[hg["k"], eq2])
            lastk = None
            for sc in range(16):
                lastk = em.t(lambda e, o=P4a[0:128, 16 + sc:17 + sc],
                             l=SQX[:, 128 * sc:128 * (sc + 1)]:
                             e.matmul(o, l, ctx.ONESC[0:64, :],
                                      start=True, stop=True),
                             waits=[vsqk, eq2] if sc == 0 else ())
            ek2 = em.v(lambda e, o=K2S, i_=P4a[:, 16:32]:
                       e.tensor_copy(o, i_), waits=[lastk])
            lastq2b = None
            for ns in range(4):
                lastq2b = em.t(lambda e, ns=ns: e.matmul(
                    P4a[:, 512 * ns:512 * (ns + 1)], ctx.ONESR,
                    Q2ROW[:, 512 * ns:512 * (ns + 1)], start=True, stop=True),
                    waits=[ek2] if ns == 0 else ())
            eq2b = em.c(lambda e, o=Q2B, i_=P4a: e.activation(o, i_, AF.Copy),
                        waits=[lastq2b])

            lastvt = None
            for sc in range(16):
                lastvt = em.t(lambda e, o=P4a_bf[:, 2048 + 64 * sc:
                                               2048 + 64 * (sc + 1)],
                              i_=VT_[:, 128 * sc:128 * (sc + 1)]:
                              e.transpose(o, i_, ctx.IDENT[0:64, 0:64]),
                              waits=([hg["v"], pmemset, hg["hh"], eq2b]
                                     if sc == 0 else ()))
            evtm = em.v(lambda e, o=VTM[:, :, 0:64],
                        i_=P4a_bf[:, 2048:3072].rearrange(
                            "p (s c) -> p s c", c=64):
                        e.tensor_copy(o, i_), waits=[lastvt])
            pg_d2 = None
            wt_guard = [None, None]
            w1_guard = None
            w2_guard = None
            lastA = None
            for sc in range(16):
                rwt = sc % 2
                lastd = None
                for ns in range(4):
                    lastd = em.t(
                        lambda e, o=P4a[:, 512 * ns:512 * (ns + 1)],
                        l=KT_[:, 128 * sc:128 * (sc + 1)],
                        r_=QN2[:, 512 * ns:512 * (ns + 1)]:
                        e.matmul(o, l, r_, start=True, stop=True),
                        waits=([vq2, eq2b, evtm, pg_d2]
                               if (sc == 0 and ns == 0) else
                               ([pg_d2] if ns == 0 else ())))
                vd2 = em.v(lambda e, o=W1F, i_=P4a, k2=K2S[:, sc:sc + 1]:
                           e.scalar_tensor_tensor(o, i_, k2, Q2B,
                                                  ALU.add, ALU.add),
                           waits=[lastd, w1_guard])
                pg_d2 = vd2
                pr = em.c(lambda e, o=W2F, i_=W1F:
                          e.activation(o, i_, AF.Relu),
                          waits=[vd2, w2_guard])
                cs = em.c(lambda e, o=W2F, i_=W2F:
                          e.activation(o, i_, AF.Sqrt), waits=[pr])
                cq = em.c(lambda e, o=W1F, i_=W2F:
                          e.activation(o, i_, AF.Square, bias=1e-4))
                w2_guard = cq
                vrcp = em.v(lambda e, o=WT[rwt], i_=W1F:
                            e.reciprocal(o, i_), waits=[cq, wt_guard[rwt]])
                w1_guard = vrcp

                for ns in range(4):
                    lastA = em.t(
                        lambda e, o=P4b[0:65, 512 * ns:512 * (ns + 1)],
                        l=VTM[:, sc, 0:65].squeeze(),
                        r_=WT[rwt][:, 512 * ns:512 * (ns + 1)],
                        st=(sc == 0), sp=(sc == 15):
                        e.matmul(o, l, r_, start=st, stop=sp),
                        waits=([vrcp, hg["pb"]] if sc == 0 and ns == 0
                               else ([vrcp] if ns == 0 else ())))
                wt_guard[rwt] = lastA
            v1n = em.v(lambda e: e.tensor_scalar(RS1a, P4b[64:65, :],
                                                 1e-4, None, ALU.add),
                       waits=[lastA, w1_guard])
            em.v(lambda e: e.reciprocal(RS1b, RS1a))
            vrb = em.v(lambda e: e.tensor_copy(RSRB, RS1b))
            lastrb = None
            for ns in range(4):
                lastrb = em.t(lambda e, ns=ns: e.matmul(
                    P4a[0:64, 512 * ns:512 * (ns + 1)], ctx.ONESR[:, 0:64],
                    RSRB[:, 512 * ns:512 * (ns + 1)], start=True, stop=True),
                    waits=[vrb, pg_d2] if ns == 0 else ())
            vh1 = em.v(lambda e, o=W2F[0:64, :]: e.tensor_tensor(
                o, P4b[0:64, :], GT_, ALU.mult),
                waits=[lastrb, hg["g"], w2_guard])
            vh2 = em.v(lambda e, o=HH, i_=W2F[0:64, :]:
                       e.tensor_tensor(o, i_, P4a[0:64, :], ALU.mult),
                       waits=[hg.get("sh")])
            sh = em.sdma(ctx.HS[h], HH, waits=[vh2])
            if b == 0 and h < 12:
                dbg_b(ctx, h, HH, waits=[vh2])
            hg["hh"] = vh2
            hg["pa"] = vh2
            hg["pb"] = vh1
            hg["q"] = vh2
            hg["k"] = vh2
            hg["v"] = vh2
            hg["g"] = vh2
            w1_guard = vh2
            w2_guard = vh2
            hg["sh"] = sh
        em.barrier()

        ar.reset(mkh)
        HTL = ar.tiles(DC, [128, T], BF16)
        XF = ar.tiles(2, [128, T], F32)
        XNEW = ar.tiles(2, [128, T], F32)
        hl = None
        for cc in range(DC):
            hl = em.sdma(HTL[cc], ctx.HS.rearrange("h p t -> (h p) t")
                         [128 * cc:128 * (cc + 1), :])
        st_guard = [None, None]
        xf_guard = [None, None]
        pgo = {"a": None, "b": None}
        for dc in range(DC):
            r = dc % 2
            key = "a" if dc % 2 == 0 else "b"
            ps = P4a if key == "a" else P4b
            ldx = em.sdma(XF[r], src_of(b, dc), waits=[xf_guard[r]])
            last = None
            for ns in range(4):
                for cc in range(DC):
                    last = em.t(
                        lambda e, o=ps[:, 512 * ns:512 * (ns + 1)],
                        l=WOUT[:, cc * D + 128 * dc: cc * D + 128 * dc + 128],
                        r_=HTL[cc][:, 512 * ns:512 * (ns + 1)],
                        st=(cc == 0), sp=(cc == DC - 1):
                        e.matmul(o, l, r_, start=st, stop=sp),
                        waits=([pgo[key], gwout, hl]
                               if (ns == 0 and cc == 0) else ()))
            vres = em.v(lambda e, o=XNEW[r], i_=ps,
                        bb=BOUT[:, dc:dc + 1], x=XF[r]:
                        e.scalar_tensor_tensor(o, i_, bb, x, ALU.add, ALU.add),
                        waits=[last, ldx, st_guard[r]])
            pgo[key] = vres
            xf_guard[r] = vres
            st_guard[r] = em.sdma(dst_of(b, dc), XNEW[r], waits=[vres])
        em.barrier()
        ar.reset(mkb)
    ar.reset(stage_mk)


def stage_sl3(ctx, src_of, dst_of):
    em, ar = ctx.em, ctx.ar
    P4a, P4b = ctx.P4a, ctx.P4b
    stage_mk = ar.mark()
    RWC = ar.tile([128, 8], F32)
    RB1 = ar.tile([1, 1], F32)
    em.sdma(RWC, ctx.inp["router_w_pp"])
    em.sdma(RB1, ctx.inp["router_b_11"])
    RWCB = ar.tile([128, 8], BF16)
    em.barrier()
    em.v(lambda e: e.tensor_copy(RWCB, RWC))
    ONESROW = ar.tile([1, T], BF16)
    em.v(lambda e: e.memset(ONESROW, 1.0))
    em.barrier()

    W1 = ar.tile([1, 16], F32)
    WLOC = ar.tile([128, 16], BF16)
    W128 = ar.tile([128, 2], F32)
    W128B = ar.tile([128, 1], BF16)
    em.v(lambda e: e.tensor_copy(WLOC[:, 0:8], RWC))
    em.v(lambda e: e.tensor_tensor(WLOC[:, 8:16], RWC, WLOC[:, 0:8],
                                   ALU.subtract))
    em.v(lambda e: e.reduce_sum(W128[:, 0:1], RWC, axis=AX.X))
    vW = em.v(lambda e: e.tensor_copy(W128B, W128[:, 0:1]))
    tW = em.t(lambda e: e.matmul(P4a[0:1, 0:1], ctx.ONESC, W128B,
                                 start=True, stop=True), waits=[vW])
    em.v(lambda e: e.tensor_copy(W1[0:1, 0:1], P4a[0:1, 0:1]), waits=[tW])
    em.barrier()
    for b in range(B):
        mkb = ar.mark()
        XN = ar.tiles(DC, [128, T], BF16)
        MROW = ar.tile([1, T], F32)
        RROW = ar.tile([1, T], F32)
        ln_x_norm(ctx, lambda dc: src_of(b, dc), XN, rows_out=(MROW, RROW))
        em.barrier()

        ROWS2 = ar.tile([1, 4 * T], F32)
        RW = ROWS2[:, 0:T]
        SELR = ROWS2[:, T:2 * T]
        RWB = ar.tile([1, 6 * T], BF16)
        RWHI = RWB[:, 0:T]
        RWLO = RWB[:, T:2 * T]
        NRWHI = RWB[:, 2 * T:3 * T]
        NRWLO = RWB[:, 3 * T:4 * T]
        SELB_R = RWB[:, 4 * T:5 * T]
        GT = ar.tiles(2, [128, T], BF16)
        SELB = ar.tile([128, T], BF16)
        TMP = ar.tiles(2, [128, T], F32)
        XF = ar.tiles(2, [128, T], F32)
        XNEW = ar.tiles(2, [128, T], F32)

        XH = ar.tiles(2, [128, T], BF16)
        XLO = ar.tiles(2, [128, T], BF16)
        XFR = ar.tiles(2, [128, T], F32)
        last = None
        prev = [None, None]
        for dc in range(DC):
            rr = dc % 2
            ldx = em.sdma(XFR[rr], src_of(b, dc), waits=[prev[rr]])
            ch = em.c(lambda e, o=XH[rr], i_=XFR[rr]: e.activation(o, i_, AF.Copy),
                      waits=[ldx])
            vl = em.v(lambda e, o=XLO[rr], a=XFR[rr], bb=XH[rr]:
                      e.tensor_tensor(o, a, bb, ALU.subtract), waits=[ch])
            for ns in range(4):
                lo, hi = 512 * ns, 512 * (ns + 1)
                for k, (lw, xt) in enumerate((
                        (WLOC[:, dc:dc + 1], XH[rr]),
                        (WLOC[:, 8 + dc:8 + dc + 1], XH[rr]),
                        (WLOC[:, dc:dc + 1], XLO[rr]))):
                    last = em.t(lambda e, o=P4b[0:1, lo:hi], l=lw,
                                r_=xt[:, lo:hi],
                                st=(dc == 0 and k == 0),
                                sp=(dc == DC - 1 and k == 2):
                                e.matmul(o, l, r_, start=st, stop=sp),
                                waits=[vl] if (ns == 0 and k == 0) else ())
            prev[rr] = last
        vt = em.v(lambda e: e.tensor_scalar(SELR, MROW, W1[0:1, 0:1], None,
                                            ALU.mult), waits=[last])
        em.v(lambda e: e.tensor_tensor(RW, P4b[0:1, :], SELR, ALU.subtract))
        em.v(lambda e: e.tensor_tensor(RW, RW, RROW, ALU.mult))
        em.v(lambda e: e.tensor_scalar(RW, RW, RB1[0:1, 0:1], None, ALU.add))
        em.v(lambda e: e.tensor_copy(RWHI, RW))
        em.v(lambda e: e.scalar_tensor_tensor(ROWS2[:, 2 * T:3 * T], RW, 1.0,
                                              RWHI, ALU.mult, ALU.subtract))
        em.v(lambda e: e.tensor_copy(RWLO, ROWS2[:, 2 * T:3 * T]))
        em.v(lambda e: e.tensor_scalar(NRWHI, RWHI, -1.0, None, ALU.mult))
        vsp = em.v(lambda e: e.tensor_scalar(NRWLO, RWLO, -1.0, None, ALU.mult))
        cs_last = None
        sg_guard = [None, None]
        pg_sign = None
        for ic in range(16):
            r = ic % 2
            lastd = None
            for ns in range(4):
                lo, hi = 512 * ns, 512 * (ns + 1)
                ops = [(NRWHI[:, 128 * ic:128 * (ic + 1)], ONESROW[:, lo:hi]),
                       (NRWLO[:, 128 * ic:128 * (ic + 1)], ONESROW[:, lo:hi]),
                       (ctx.ONESR, RWHI[:, lo:hi]),
                       (ctx.ONESR, RWLO[:, lo:hi])]
                for k, (lh, rh) in enumerate(ops):
                    lastd = em.t(lambda e, o=P4a[:, lo:hi], l=lh, r_=rh,
                                 st=(k == 0), sp=(k == 3):
                                 e.matmul(o, l, r_, start=st, stop=sp),
                                 waits=([vsp, pg_sign]
                                        if (ns == 0 and k == 0) else ()))
            cg = em.c(lambda e, o=GT[r], i_=P4a: e.activation(o, i_, AF.Sign),
                      waits=[lastd, sg_guard[r]])
            pg_sign = cg
            for ns in range(4):
                cs_last = em.t(lambda e, o=P4b[0:1, 512 * ns:512 * (ns + 1)],
                               i=GT[r], ns=ns, st=(ic == 0), sp=(ic == 15):
                               e.matmul(o, ctx.ONESC,
                                        i[:, 512 * ns:512 * (ns + 1)],
                                        start=st, stop=sp),
                               waits=[cg] if ns == 0 else ())
            sg_guard[r] = cs_last
        c2 = em.c(lambda e: e.activation(SELR, P4b[0:1, :], AF.Sign, bias=-0.5),
                  waits=[cs_last])
        vsel = em.v(lambda e: e.tensor_scalar(SELB_R, SELR, 0.5, 0.5,
                                              ALU.mult, ALU.add), waits=[c2])
        bsel = None
        for ns in range(4):
            bsel = em.t(lambda e, ns=ns: e.matmul(
                P4a[:, 512 * ns:512 * (ns + 1)], ctx.ONESR,
                SELB_R[:, 512 * ns:512 * (ns + 1)], start=True, stop=True),
                waits=[vsel] if ns == 0 else ())
        esel = em.c(lambda e: e.activation(SELB, P4a, AF.Copy), waits=[bsel])
        st_guard = [None, None]
        for dc in range(DC):
            r = dc % 2
            ldx = em.sdma(XF[r], src_of(b, dc), waits=[st_guard[r]])
            v1 = em.v(lambda e, o=TMP[r], a=XN[dc][:, ::-1], s=SELB:
                      e.tensor_tensor(o, a, s, ALU.mult),
                      waits=[esel, st_guard[r]])
            v2 = em.v(lambda e, o=XNEW[r], a=XF[r], bb=TMP[r]:
                      e.tensor_tensor(o, a, bb, ALU.add), waits=[ldx])
            st_guard[r] = em.sdma(dst_of(b, dc), XNEW[r], waits=[v2])
        em.barrier()
        ar.reset(mkb)
    ar.reset(stage_mk)


def transpose_tiles(ctx, A, B, nb, na, waits=(), psum=None):
    em = ctx.em
    ps = psum if psum is not None else ctx.P4b_bf
    evs = []
    ev_guard = None
    for j in range(nb):
        lastt = None
        for i in range(na):
            lastt = em.t(
                lambda e, o=ps[:, 128 * i:128 * (i + 1)],
                i_=A[i][:, 128 * j:128 * (j + 1)]:
                e.transpose(o, i_, ctx.IDENT),
                waits=(list(waits) + [ev_guard]) if i == 0 else ())
        ev_guard = em.v(lambda e, o=B[j], i_=ps[:, 0:128 * na]:
                        e.tensor_copy(o, i_), waits=[lastt])
        evs.append(ev_guard)
    return evs


def stage_sl2(ctx, src_of, dst_of, tag):
    em, ar, nc = ctx.em, ctx.ar, ctx.nc
    P4a, P4b = ctx.P4a, ctx.P4b
    stage_mk = ar.mark()
    NE, PS_, H = 64, 32, 1024
    E2 = NE * PS_

    PHIWT, gphi = load_w_cast(ctx, ctx.inp["phi_wT"], D, E2)
    PB = ar.tile([128, E2], BF16)
    em.gdma(PB, ctx.inp["phi_b_bc"])
    em.barrier()

    for b in range(B):
        mkb = ar.mark()
        XN = ar.tiles(DC, [128, T], BF16)
        ln_x_norm(ctx, lambda dc: src_of(b, dc), XN)
        em.barrier()

        DT = ar.tiles(16, [128, E2], BF16)
        XNT = ar.tiles(16, [128, D], BF16)
        LG = ar.tiles(1, [128, E2], F32) * 2
        EX = ar.tiles(1, [128, E2], F32) * 2
        RD_ = ar.tile([128, NE], F32)
        RC_ = ar.tile([128, PS_], F32)
        SRED = LG[0]
        CSUM = ar.tile([128, 16 * NE], F32)
        CB = ar.tiles(2, [128, E2], BF16)

        transpose_tiles(ctx, XN, XNT, 16, DC)
        em.barrier()

        pga = None
        lg_guard = [None, None]
        cb_guard = [None, None]
        v7 = None
        rdg = None
        for tc in range(16):
            r = 0
            last = None
            for ns in range(4):
                for dc in range(DC):
                    last = em.t(
                        lambda e, o=P4a[:, 512 * ns:512 * (ns + 1)],
                        l=XN[dc][:, 128 * tc:128 * (tc + 1)],
                        r_=PHIWT[:, dc * E2 + 512 * ns: dc * E2 + 512 * ns + 512],
                        st=(dc == 0), sp=(dc == DC - 1):
                        e.matmul(o, l, r_, start=st, stop=sp),
                        waits=[pga, gphi] if (ns == 0 and dc == 0) else ())
            v0 = em.v(lambda e, o=LG[r], i_=P4a:
                      e.scalar_tensor_tensor(o, i_, 1.0, PB, ALU.mult, ALU.add),
                      waits=[last, lg_guard[r]])
            pga = v0
            cx = em.c(lambda e, o=EX[r], i_=LG[r]: e.activation(o, i_, AF.Exp),
                      waits=[v0])
            lg_guard[r] = cx
            v1 = em.v(lambda e, o=RD_[:, :, None],
                      i_=EX[r].rearrange("q (n p) -> q n p", p=PS_):
                      e.reduce_sum(o, i_, axis=AX.X), waits=[cx, rdg])
            rdg = em.v(lambda e, o=RD_, i_=RD_:
                       e.reciprocal(o, i_), waits=[v1])
            em.v(lambda e, o=DT[tc].rearrange("q (n p) -> q n p", p=PS_),
                 i_=EX[r].rearrange("q (n p) -> q n p", p=PS_),
                 s=RD_[:, :, None].broadcast_to([128, NE, PS_]):
                 e.tensor_tensor(o, i_, s, ALU.mult), waits=[rdg])
            em.v(lambda e, o=SRED[:, 0:1024], a=EX[r][:, 0:1024],
                 bb=EX[r][:, 1024:2048]: e.tensor_tensor(o, a, bb, ALU.add))
            em.v(lambda e, o=SRED[:, 1024:1536], a=SRED[:, 0:512],
                 bb=SRED[:, 512:1024]: e.tensor_tensor(o, a, bb, ALU.add))
            em.v(lambda e, o=SRED[:, 1536:1792], a=SRED[:, 1024:1280],
                 bb=SRED[:, 1280:1536]: e.tensor_tensor(o, a, bb, ALU.add))
            em.v(lambda e, o=SRED[:, 1792:1920], a=SRED[:, 1536:1664],
                 bb=SRED[:, 1664:1792]: e.tensor_tensor(o, a, bb, ALU.add))
            em.v(lambda e, o=SRED[:, 1920:1984], a=SRED[:, 1792:1856],
                 bb=SRED[:, 1856:1920]: e.tensor_tensor(o, a, bb, ALU.add))
            vtr = em.v(lambda e, o=RC_, a=SRED[:, 1920:1952],
                       bb=SRED[:, 1952:1984]: e.tensor_tensor(o, a, bb,
                                                              ALU.add))
            crc2 = em.v(lambda e, o=RC_, i_=RC_:
                        e.reciprocal(o, i_), waits=[vtr])
            em.v(lambda e, o=SRED[:, 0:32], i_=RC_: e.tensor_copy(o, i_),
                 waits=[crc2])
            wdt = 32
            while wdt < E2:
                em.v(lambda e, o=SRED[:, wdt:2 * wdt], i_=SRED[:, 0:wdt]:
                     e.tensor_copy(o, i_))
                wdt *= 2
            rb = tc % 2
            v6 = em.v(lambda e, o=CB[rb], i_=EX[r], s=SRED[:, 0:E2]:
                      e.tensor_tensor(o, i_, s, ALU.mult), waits=[cb_guard[rb]])
            v7 = em.v(lambda e, o=CSUM[:, NE * tc:NE * (tc + 1)][:, :, None],
                      i_=CB[rb].rearrange("q (n p) -> q n p", p=PS_):
                      e.reduce_sum(o, i_, axis=AX.X))
            cb_guard[rb] = em.sdma(ctx.CS[b, tc], CB[rb], waits=[v7])
            if b == 0 and tc == 0:
                dbg_b(ctx, 1, DT[0], waits=[v7])
                dbg_b(ctx, 2, CB[rb], waits=[v7])
                dbg_f(ctx, 2, EX[r], waits=[v7])
                dbg_f(ctx, 3, SRED[:, 0:T], waits=[v7])
                dbg_f(ctx, 4, RC_, waits=[v7])
        em.sdma(ctx.CSUMS[b], CSUM, waits=[v7])
        if b == 0:
            dbg_b(ctx, 0, XN[0], waits=[v7])
            dbg_b(ctx, 3, XNT[0][:, 0:D], waits=[v7])
        em.barrier()

        XEV = ar.tiles(2, [128, D], F32)
        xev_guard = [None, None]
        for npc in range(16):
            r = npc % 2
            ps = (P4a if r == 0 else P4b)[:, 0:1024]
            last = None
            for tc in range(16):
                for ns in range(2):
                    last = em.t(
                        lambda e, o=ps[:, 512 * ns:512 * (ns + 1)],
                        l=DT[tc][:, 128 * npc:128 * (npc + 1)],
                        r_=XNT[tc][:, 512 * ns:512 * (ns + 1)],
                        st=(tc == 0), sp=(tc == 15):
                        e.matmul(o, l, r_, start=st, stop=sp),
                        waits=[xev_guard[r]] if (tc == 0 and ns == 0) else ())
            ve = em.v(lambda e, o=XEV[r], i_=ps: e.tensor_copy(o, i_),
                      waits=[last])
            xev_guard[r] = em.gdma(
                ctx.CCIN[128 * npc:128 * (npc + 1), :], XEV[r],
                waits=[ve], **({} if b == 0 else dict(accum_op=ALU.add)))
            if b == B - 1 and npc == 0:
                dbg_f(ctx, 0, XEV[r][:, 0:D], waits=[ve])
        em.barrier()
        ar.reset(mkb)

    em.op("gpsimd", lambda e: e.collective_compute(
        "AllReduce", ALU.add, replica_groups=[list(range(NCORES))],
        ins=[ctx.CCIN.opt()], outs=[ctx.CCOUT.opt()]), "CC", waits=())
    em.barrier()

    ar.reset(stage_mk)
    YTIL = ar.tiles(16, [128, H], BF16)
    ytil_bytes = ar.mark()
    XTF = ar.tiles(2, [128, D], F32)
    XSQ = ar.tiles(2, [128, D], F32)
    XTT = ar.tiles(2, [128, D], BF16)
    XTD = ar.tiles(2, [128, DC * 128], BF16)
    ROWC = ar.tile([128, 16], F32)
    EW = ar.tiles(2, [128, DC * H], BF16)
    ew_guard = [None, None]
    psum_guard = [None, None]
    xg = [None, None]
    xtd_guard = [None, None]
    for npc in range(16):
        r = npc % 2
        ld = em.sdma(XTF[r], ctx.CCOUT[128 * npc:128 * (npc + 1), :],
                     waits=[xg[r]])
        vc = em.v(lambda e, o=XTF[r]: e.tensor_scalar(
            o, o, -33000.0, 65000.0, ALU.max, ALU.min), waits=[ld])
        vs = em.v(lambda e, o=ROWC[:, 0:1], i_=XTF[r]:
                  e.reduce_sum(o, i_, axis=AX.X), waits=[vc])
        cm = em.c(lambda e, o=ROWC[:, 1:2], i_=ROWC[:, 0:1]:
                  e.mul(o, i_, 1.0 / D), waits=[vs])
        csq = em.c(lambda e, o=XSQ[r], i_=XTF[r]:
                   e.activation(o, i_, AF.Square), waits=[vc])
        vq = em.v(lambda e, o=ROWC[:, 2:3], i_=XSQ[r]:
                  e.reduce_sum(o, i_, axis=AX.X), waits=[csq])
        cq = em.c(lambda e, o=ROWC[:, 3:4], i_=ROWC[:, 2:3]:
                  e.mul(o, i_, 1.0 / D), waits=[vq])
        cmsq = em.c(lambda e, o=ROWC[:, 4:5], i_=ROWC[:, 1:2]:
                    e.activation(o, i_, AF.Square))
        cvar = em.c(lambda e, o=ROWC[:, 5:6], i_=ROWC[:, 4:5],
                    bb=ROWC[:, 3:4]:
                    e.activation(o, i_, AF.Identity, scale=-1.0, bias=bb))
        csd = em.c(lambda e, o=ROWC[:, 6:7], i_=ROWC[:, 5:6]:
                   e.activation(o, i_, AF.Sqrt, bias=1e-5))
        vr = em.v(lambda e, o=ROWC[:, 7:8], i_=ROWC[:, 6:7]:
                  e.reciprocal(o, i_), waits=[csd])
        va = em.v(lambda e, o=XTF[r], i_=XTF[r], m=ROWC[:, 1:2],
                  rr=ROWC[:, 7:8]:
                  e.tensor_scalar(o, i_, m, rr, ALU.subtract, ALU.mult),
                  waits=[vr])
        ct = em.c(lambda e, o=XTT[r], i_=XTF[r]:
                  e.activation(o, i_, AF.Tanh), waits=[va])
        xg[r] = ct
        if npc == 0:
            dbg_f(ctx, 1, XTF[r][:, 0:D], waits=[va])
            dbg_b(ctx, 4, XTT[r][:, 0:D], waits=[ct])
        lastt = None
        for dcc in range(DC):
            lastt = em.t(lambda e, o=ctx.P4b_bf[:, 128 * dcc:128 * (dcc + 1)],
                         i_=XTT[r][:, 128 * dcc:128 * (dcc + 1)]:
                         e.transpose(o, i_, ctx.IDENT),
                         waits=[ct, xtd_guard[r]] if dcc == 0 else ())
        evx = em.v(lambda e, o=XTD[r], i_=ctx.P4b_bf[:, 0:DC * 128]:
                   e.tensor_copy(o, i_), waits=[lastt])
        xtd_guard[r] = evx
        for ei in range(4):
            n = 4 * npc + ei
            rw = n % 2
            gw = em.gdma(EW[rw].rearrange("p (kc m) -> p kc m", kc=DC),
                         ctx.inp["expert_w"][n].rearrange(
                             "(kc p) m -> p kc m", p=128),
                         waits=[ew_guard[rw]])
            pr = ei % 2
            psv = P4a[0:32, 1024 * pr:1024 * pr + 1024]
            last = None
            for dcc in range(DC):
                for ns in range(2):
                    last = em.t(
                        lambda e, o=psv[:, 512 * ns:512 * (ns + 1)],
                        l=XTD[r][:, 128 * dcc + 32 * ei:128 * dcc + 32 * ei + 32],
                        r_=EW[rw][:, dcc * H + 512 * ns:dcc * H + 512 * ns + 512],
                        st=(dcc == 0), sp=(dcc == DC - 1):
                        e.matmul(o, l, r_, start=st, stop=sp),
                        waits=([evx, gw, psum_guard[pr]]
                               if (dcc == 0 and ns == 0) else ()))
            ev = em.c(
                lambda e, o=YTIL[npc][32 * ei:32 * (ei + 1), :], i_=psv:
                e.activation(o, i_, AF.Copy), waits=[last])
            psum_guard[pr] = ev
            ew_guard[rw] = ev
    em.barrier()

    dbg_b(ctx, 5, YTIL[0][:, 0:H])
    ar.reset(ytil_bytes)
    EB = ar.tile([64, H], BF16)
    em.gdma(EB, ctx.inp["expert_b"])
    CLD = ar.tiles(16, [128, E2], BF16)
    CTT = ar.tiles(16, [128, T], BF16)
    CSL = ar.tile([128, 16 * 64], F32)
    CSB = ar.tile([128, 16 * 64], BF16)
    CSUMT = ar.tile([64, T], BF16)
    XF = ar.tiles(1, [128, T], F32)
    XNEW = ar.tiles(1, [128, T], F32)
    prev_b_done = None
    for b in range(B):
        lds = []
        for tc in range(16):
            lds.append(em.sdma(CLD[tc], ctx.CS[b, tc], waits=[prev_b_done]))
        ldc = em.sdma(CSL, ctx.CSUMS[b], waits=[prev_b_done])
        vcs = em.v(lambda e, o=CSB, i_=CSL: e.tensor_copy(o, i_), waits=[ldc])
        lastt = None
        for tc in range(16):
            lastt = em.t(lambda e, o=ctx.P4b_bf[0:64, 128 * tc:128 * (tc + 1)],
                         i_=CSB[:, 64 * tc:64 * (tc + 1)]:
                         e.transpose(o, i_, ctx.IDENT),
                         waits=[vcs] if tc == 0 else ())
        vct = em.v(lambda e, o=CSUMT, i_=ctx.P4b_bf[0:64, 0:T]:
                   e.tensor_copy(o, i_), waits=[lastt])
        transpose_tiles(ctx, CLD, CTT, 16, 16, waits=[lds[-1], vct])
        em.barrier()
        if b == 0:
            dbg_b(ctx, 6, CTT[0])
            dbg_b(ctx, 7, CSUMT[0:64, :])
        pg2 = [None, None]
        st_guard = [None, None]
        xf_guard = [None, None]
        for hc in range(DC):
            r = 0
            ps = P4a if hc % 2 == 0 else P4b
            ldx = em.sdma(XF[r], src_of(b, hc), waits=[xf_guard[r]])
            last = None
            for ns in range(4):
                last = em.t(lambda e, o=ps[:, 512 * ns:512 * (ns + 1)],
                            l=EB[:, 128 * hc:128 * (hc + 1)],
                            r_=CSUMT[:, 512 * ns:512 * (ns + 1)]:
                            e.matmul(o, l, r_, start=True, stop=False),
                            waits=[pg2[r]] if ns == 0 else ())
                for npc in range(16):
                    last = em.t(
                        lambda e, o=ps[:, 512 * ns:512 * (ns + 1)],
                        l=YTIL[npc][:, 128 * hc:128 * (hc + 1)],
                        r_=CTT[npc][:, 512 * ns:512 * (ns + 1)],
                        sp=(npc == 15):
                        e.matmul(o, l, r_, start=False, stop=sp))
            vres = em.v(lambda e, o=XNEW[r], i_=ps, x=XF[r]:
                        e.scalar_tensor_tensor(o, i_, 1.0, x, ALU.mult,
                                               ALU.add),
                        waits=[last, ldx, st_guard[r]])
            pg2[r] = vres
            xf_guard[r] = vres
            st_guard[r] = em.sdma(dst_of(b, hc), XNEW[r], waits=[vres])
        prev_b_done = st_guard[0]
        em.barrier()
    ar.reset(stage_mk)


def ln_tiles(ctx, xg, xl, lg_col, lb_col):
    em, ar = ctx.em, ctx.ar
    P4a, P4b = ctx.P4a, ctx.P4b
    mk = ar.mark()
    SQ = ar.tiles(2, [128, T], BF16)
    ROWS = ar.tile([1, 3 * T], F32)
    ROWm = ROWS[:, 0:T]
    ROWmsq = ROWS[:, T:2 * T]
    ROWvar = ROWS[:, 2 * T:3 * T]
    ROWsd = ROWS[:, T:2 * T]
    ROWr = ROWS[:, 2 * T:3 * T]
    ROWSB = ar.tile([1, 2 * T], BF16)
    ROWrb = ROWSB[:, 0:T]
    ROWmrb = ROWSB[:, T:2 * T]
    RBB = ar.tile([128, T], BF16)
    MRB = ar.tile([128, T], BF16)
    sq_guard = [None, None]
    last_mm = None
    for dc in range(DC):
        r = dc % 2
        sq = em.v(lambda e, o=SQ[r], i=xg[dc]:
                  e.tensor_tensor(o, i, i, ALU.mult), waits=[sq_guard[r]])
        for ns in range(4):
            em.t(lambda e, o=P4b[0:1, 512 * ns:512 * (ns + 1)], i=xg[dc],
                 ns=ns, st=(dc == 0), sp=(dc == DC - 1):
                 e.matmul(o, ctx.ONESC, i[:, 512 * ns:512 * (ns + 1)],
                          start=st, stop=sp))
        for ns in range(4):
            last_mm = em.t(lambda e, o=P4a[0:1, 512 * ns:512 * (ns + 1)],
                           i=SQ[r], ns=ns, st=(dc == 0), sp=(dc == DC - 1):
                           e.matmul(o, ctx.ONESC,
                                    i[:, 512 * ns:512 * (ns + 1)],
                                    start=st, stop=sp),
                           waits=[sq] if ns == 0 else ())
        sq_guard[r] = last_mm
    v1 = em.v(lambda e: e.tensor_scalar(ROWm, P4b[0:1, :], 1.0 / D, None,
                                        ALU.mult), waits=[last_mm])
    em.v(lambda e: e.tensor_tensor(ROWmsq, ROWm, ROWm, ALU.mult))
    v3 = em.v(lambda e: e.scalar_tensor_tensor(ROWvar, P4a[0:1, :], 1.0 / D,
                                               ROWmsq, ALU.mult, ALU.subtract))
    c1 = em.c(lambda e: e.activation(ROWsd, ROWvar, AF.Sqrt, bias=1e-5),
              waits=[v3])
    cr_ = em.v(lambda e: e.reciprocal(ROWr, ROWsd), waits=[c1])
    v4 = em.v(lambda e: e.tensor_copy(ROWrb, ROWr), waits=[cr_])
    v5 = em.v(lambda e: e.tensor_tensor(ROWmrb, ROWm, ROWr, ALU.mult))
    bc_r = None
    for ns in range(4):
        bc_r = em.t(lambda e, ns=ns: e.matmul(
            P4a[:, 512 * ns:512 * (ns + 1)], ctx.ONESR,
            ROWrb[:, 512 * ns:512 * (ns + 1)], start=True, stop=True),
            waits=[v4, v1] if ns == 0 else ())
    bc_mr = None
    for ns in range(4):
        bc_mr = em.t(lambda e, ns=ns: e.matmul(
            P4b[:, 512 * ns:512 * (ns + 1)], ctx.ONESR,
            ROWmrb[:, 512 * ns:512 * (ns + 1)], start=True, stop=True),
            waits=[v5] if ns == 0 else ())
    e1 = em.c(lambda e: e.activation(RBB, P4a[:, :], AF.Copy), waits=[bc_r])
    e2 = em.c(lambda e: e.activation(MRB, P4b[:, :], AF.Copy), waits=[bc_mr])
    vdone = None
    for dc in range(DC):
        em.v(lambda e, o=xl[dc], a=xg[dc]:
             e.tensor_tensor(o, a, RBB, ALU.mult),
             waits=[e1, e2] if dc == 0 else ())
        em.v(lambda e, o=xl[dc]: e.tensor_tensor(o, o, MRB, ALU.subtract))
        vdone = em.v(lambda e, o=xl[dc], lg=lg_col(dc), lb=lb_col(dc):
                     e.tensor_scalar(o, o, lg, lb, ALU.mult, ALU.add))
    ar.reset(mk)
    return vdone


def ln_x_norm_f32(ctx, src, xn):
    return ln_x_norm(ctx, src, xn)


def stage_sl6(ctx, src_of, dst_of, tag):
    em, ar = ctx.em, ctx.ar
    P4a, P4b = ctx.P4a, ctx.P4b
    P4a_bf = ctx.P4a_bf
    stage_mk = ar.mark()
    IN2 = 512
    HID = 2730
    HIDP = 2816
    MC = 22

    H8ALL = ar.tile([128, 32], BF16)
    em.sdma(H8ALL, ctx.inp["f_h8all"])
    RP = ar.tile([128, 128], BF16)
    em.sdma(RP, ctx.inp["f_rperm"])
    MSK = ar.tile([128, 2, 4, 1024], U8)
    em.sdma(MSK[:, 0], ctx.inp["mask6"].rearrange("c p k -> p c k"))
    em.sdma(MSK[:, 1], ctx.inp["mask6w0"].rearrange("c p k -> p c k"))
    QSKS = ar.tile([128, 8], F32)
    em.sdma(QSKS, ctx.inp["f_qsks_pp"])
    LNG = ar.tile([128, 16], F32)
    em.sdma(LNG, ctx.inp["f_lng_pp"])
    LNB = ar.tile([128, 16], F32)
    em.sdma(LNB, ctx.inp["f_lnb_pp"])
    em.barrier()

    for b in range(B):
        mkb = ar.mark()
        XLOC = ar.tiles(DC, [128, T], F32)
        ln_x_norm(ctx, lambda dc: src_of(b, dc), XLOC)
        em.barrier()

        for l in range(2):
            LIM = SL6_LIMIT
            mkl = ar.mark()
            TQ1 = ar.tile([128, T], BF16)
            TQ2 = ar.tile([128, T], BF16)
            TK1 = ar.tile([128, T], BF16)
            TK2 = ar.tile([128, T], BF16)

            QTT = ar.tiles(4, [128, T], BF16)
            KTT = ar.tiles(4, [128, T], BF16)
            VT6 = ar.tiles(16, [128, IN2], BF16)
            RNQT = ar.tile([128, 16 * 8], F32)
            OH = ar.tile([64, T], BF16)

            mkq = ar.mark()
            TBASE = ar.tile([128, 4 * T], BF16)
            gtb = em.gdma(TBASE, ctx.inp["f_tabs"])
            tdone = None
            for (tt, tb, col) in ((TQ1, 0, 0), (TQ2, 1, 1), (TK1, 2, 2),
                                  (TK2, 3, 3)):
                tdone = em.v(lambda e, o=tt, i_=TBASE[:, tb * T:(tb + 1) * T],
                             s=QSKS[:, 4 * l + col:4 * l + col + 1]:
                             e.tensor_scalar(o, i_, s, None, ALU.mult),
                             waits=[gtb] if tb == 0 else ())
            em.barrier()
            ar.reset(mkq)
            XQ = ar.tiles(DC, [128, T], BF16)
            WQKV, gq = load_w_cast(ctx, ctx.inp["f_wqkv"][l], D, 1536)
            sg = None
            for dc in range(DC):
                if dc < 4:
                    sg = em.v(lambda e, o=XQ[dc], i_=XLOC[dc]:
                              e.tensor_copy(o, i_))
                else:
                    em.v(lambda e, o=XQ[dc][:, 0:2], i_=XQ[dc][:, 0:2]:
                         e.memset(o, 0.0))
                    sg = em.v(lambda e, o=XQ[dc][:, 1:T],
                              i_=XLOC[dc][:, 0:T - 1]:
                              e.tensor_copy(o, i_))
            em.barrier()
            pg = {"a": None, "b": None}
            for which, OUTL, cofs in (("q", QTT, 0), ("k", KTT, 512)):
                for cc in range(4):
                    key = "a" if cc % 2 == 0 else "b"
                    ps = P4a if key == "a" else P4b
                    last = None
                    for ns in range(4):
                        for dc in range(DC):
                            base = dc * 1536 + cofs + 128 * cc
                            last = em.t(
                                lambda e, o=ps[:, 512 * ns:512 * (ns + 1)],
                                l_=WQKV[:, base:base + 128],
                                r_=XQ[dc][:, 512 * ns:512 * (ns + 1)],
                                st=(dc == 0), sp=(dc == DC - 1):
                                e.matmul(o, l_, r_, start=st, stop=sp),
                                waits=([gq, pg[key]]
                                       if (ns == 0 and dc == 0) else ()))
                    pg[key] = em.c(lambda e, o=OUTL[cc], i_=ps:
                                   e.activation(o, i_, AF.Copy),
                                   waits=[last])
            for tc in range(16):
                key = "a" if tc % 2 == 0 else "b"
                ps = (P4a if key == "a" else P4b)[:, 0:512]
                last = None
                for dc in range(DC):
                    base = dc * 1536 + 1024
                    last = em.t(
                        lambda e, o=ps,
                        l_=XQ[dc][:, 128 * tc:128 * (tc + 1)],
                        r_=WQKV[:, base:base + 512],
                        st=(dc == 0), sp=(dc == DC - 1):
                        e.matmul(o, l_, r_, start=st, stop=sp),
                        waits=[pg[key]] if dc == 0 else ())
                pg[key] = em.c(lambda e, o=VT6[tc], i_=ps:
                               e.activation(o, i_, AF.Copy), waits=[last])
            if b == 0 and l == 0:
                dbg_b(ctx, 3, QTT[0], waits=[pg["a"], pg["b"]])
                dbg_b(ctx, 8, VT6[0], waits=[pg["a"], pg["b"]])
            em.barrier()
            ar.reset(mkq)

            if LIM < 2:
                ar.reset(mkl)
                continue
            TMP6 = ar.tiles(2, [128, T], BF16)
            SQ6 = TMP6[0]
            RN8 = ar.tile([8, T], F32)
            RN8B = ar.tile([8, T], BF16)
            RNKB = ar.tiles(4, [128, T], BF16)
            SM6 = ar.tiles(1, [128, 1024], F32) * 2
            ATTN6 = ar.tiles(4, [128, 1024], BF16)
            ATTS6 = ar.tile([128, 8 * 512], BF16)
            NM6 = ar.tile([128, 16], F32)
            RS6 = ar.tile([128, 8], F32)
            RR6 = ar.tile([128, 8], F32)
            RNROW2 = ar.tiles(1, [1, T], BF16) * 2
            rnrow_guard = [None, None]
            rn_guard = None
            for side, OUTL in (("q", QTT), ("k", KTT)):
                lasts = None
                sqg = None
                for j in range(4):
                    sqg = em.v(lambda e, o=SQ6, i_=OUTL[j]:
                               e.tensor_tensor(o, i_, i_, ALU.mult),
                               waits=[lasts])
                    for ns in range(4):
                        lasts = em.t(
                            lambda e, o=P4a[0:8, 512 * ns:512 * (ns + 1)],
                            l_=H8ALL[:, 8 * j:8 * (j + 1)], i_=SQ6, ns=ns,
                            st=(j == 0), sp=(j == 3):
                            e.matmul(o, l_, i_[:, 512 * ns:512 * (ns + 1)],
                                     start=st, stop=sp),
                            waits=[sqg] if ns == 0 else ())
                scl = 1.0 / 64.0 if side == "q" else 1.0
                clamp = 1e-12 / 8.0 if side == "q" else 1e-12
                c1 = em.c(lambda e, o=RN8, i_=P4a[0:8, :],
                          s=scl: e.activation(o, i_, AF.Sqrt, scale=s),
                          waits=[lasts, rn_guard])
                v1 = em.v(lambda e, o=RN8, cl=clamp:
                          e.tensor_scalar(o, o, cl, None, ALU.max),
                          waits=[c1])
                vr = em.v(lambda e, o=RN8:
                          e.reciprocal(o, o), waits=[v1])
                if SL6_SUB < 2:
                    continue
                if side == "q":
                    lastt = None
                    for tc in range(16):
                        lastt = em.t(
                            lambda e, o=ctx.P4b[0:128, 8 * tc:8 * (tc + 1)],
                            i_=RN8[0:8, 128 * tc:128 * (tc + 1)]:
                            e.transpose(o, i_, ctx.IDENTF[0:8, 0:8]),
                            waits=[vr, pg["b"]] if tc == 0 else ())
                    pg["b"] = em.v(lambda e, o=RNQT, i_=ctx.P4b[:, 0:128]:
                                   e.tensor_copy(o, i_), waits=[lastt])
                    rn_guard = pg["b"]
                elif SL6_SUB < 3:
                    continue
                else:
                    vb = em.v(lambda e, o=RN8B, i_=RN8:
                              e.tensor_copy(o, i_), waits=[vr])
                    grn = em.sdma(ctx.RNS, RN8B, waits=[vb])
                    lrn = [None] * 4
                    for j in range(4):
                        lrnA = em.sdma(
                            RNROW2[0], ctx.RNS[2 * j:2 * j + 1, :],
                            waits=[grn, rnrow_guard[0]])
                        lastt = None
                        for ns in range(4):
                            lastt = em.t(
                                lambda e, o=P4b[:, 512 * ns:512 * (ns + 1)],
                                r_=RNROW2[0][:, 512 * ns:512 * (ns + 1)]:
                                e.matmul(o, ctx.INDA, r_, start=True,
                                         stop=False),
                                waits=[lrnA, pg["b"]] if ns == 0 else ())
                        lrnB = em.sdma(
                            RNROW2[0], ctx.RNS[2 * j + 1:2 * j + 2, :],
                            waits=[lastt])
                        for ns in range(4):
                            lastt = em.t(
                                lambda e, o=P4b[:, 512 * ns:512 * (ns + 1)],
                                r_=RNROW2[0][:, 512 * ns:512 * (ns + 1)]:
                                e.matmul(o, ctx.INDB, r_, start=False,
                                         stop=True),
                                waits=[lrnB] if ns == 0 else ())
                        pg["b"] = em.c(lambda e, o=RNKB[j], i_=P4b:
                                       e.activation(o, i_, AF.Copy),
                                       waits=[lastt])
                        rnrow_guard[0] = lastt
            if LIM < 3:
                ar.reset(mkl)
                continue
            rotg = None
            for side, OUTL, T1, T2 in (("q", QTT, TQ1, TQ2),
                                       ("k", KTT, TK1, TK2)):
                for j in range(4):
                    lastr = None
                    for ns in range(4):
                        lastr = em.t(
                            lambda e, o=P4a[:, 512 * ns:512 * (ns + 1)],
                            i_=OUTL[j][:, 512 * ns:512 * (ns + 1)]:
                            e.matmul(o, RP, i_, start=True, stop=True),
                            waits=[pg["a"], rotg] if ns == 0 else ())
                    v1_ = em.v(lambda e, o=TMP6[0], i_=P4a, t2=T2:
                               e.tensor_tensor(o, i_, t2, ALU.mult),
                               waits=[lastr])
                    pg["a"] = v1_
                    v2_ = em.v(lambda e, o=TMP6[1], i_=OUTL[j], t1=T1:
                               e.tensor_tensor(o, i_, t1, ALU.mult))
                    if side == "q":
                        rotg = em.v(lambda e, o=OUTL[j], a=TMP6[1],
                                    bb=TMP6[0]:
                                    e.tensor_tensor(o, a, bb, ALU.add))
                    else:
                        v3_ = em.v(lambda e, o=TMP6[0], a=TMP6[1],
                                   bb=TMP6[0]:
                                   e.tensor_tensor(o, a, bb, ALU.add))
                        rotg = em.v(lambda e, o=OUTL[j], a=TMP6[0],
                                    s=RNKB[j]:
                                    e.tensor_tensor(o, a, s, ALU.mult))
            if b == 0 and l == 0:
                dbg_b(ctx, 4, QTT[0], waits=[rotg])
                dbg_b(ctx, 5, KTT[0], waits=[rotg])
                dbg_b(ctx, 6, RNKB[0], waits=[rotg])
                dbg_f(ctx, 3, RNQT, waits=[rotg])
            em.barrier()

            if LIM < 4:
                ar.reset(mkl)
                continue
            hh_guard = None
            pg = {"a": None, "b": None}
            attn_guard = [None] * 4
            sm_guard = [None, None]
            ev_guard = None
            for h in range(8):
                tile_i, off = h // 2, 64 * (h % 2)
                Qh = QTT[tile_i][off:off + 64, :]
                Kh = KTT[tile_i][off:off + 64, :]
                for w in range(4):
                    kcl = 0 if w > 0 else 512
                    klo = 512 * w - 512 + (512 if w == 0 else 0)
                    nks = 2 if w > 0 else 1
                    softmax_done = [None] * 4
                    for i in range(4):
                        tc = 4 * w + i
                        key = "a" if i % 2 == 0 else "b"
                        ps = P4a if key == "a" else P4b
                        r2 = 0
                        pmem = em.p(lambda e, o=SM6[r2]:
                                    e.memset(o, -3.4e38),
                                    waits=[pg[key], sm_guard[r2]])
                        last = None
                        for ns in range(nks):
                            last = em.t(
                                lambda e,
                                o=ps[:, kcl + 512 * ns:kcl + 512 * (ns + 1)],
                                l_=Qh[:, 128 * tc:128 * (tc + 1)],
                                r_=Kh[:, klo + 512 * ns:klo + 512 * (ns + 1)]:
                                e.matmul(o, l_, r_, start=True, stop=True),
                                waits=[pg[key]] if ns == 0 else ())
                        vcp = em.v(lambda e, o=SM6[r2],
                                   mk=MSK[:, 1 if w == 0 else 0, i],
                                   i_=ps[:, 0:1024]:
                                   e.copy_predicated(o, mk, i_),
                                   waits=[last, pmem])
                        pg[key] = vcp
                        vm = em.v(lambda e, o=NM6[:, tc:tc + 1], i_=SM6[r2]:
                                  e.reduce_max(o, i_, axis=AX.X, negate=True))
                        rncol = RNQT[:, 8 * tc + h:8 * tc + h + 1]
                        cb = em.c(lambda e, o=NM6[:, tc:tc + 1],
                                  i_=NM6[:, tc:tc + 1], s=rncol:
                                  e.activation(o, i_, AF.Identity, scale=s),
                                  waits=[vm])
                        cx = em.c(lambda e, o=ATTN6[i], i_=SM6[r2], ii=i,
                                  s=rncol, tc=tc:
                                  e.activation(o, i_, AF.Exp, scale=s,
                                               bias=NM6[:, tc:tc + 1],
                                               accum_out=RS6[:, ii:ii + 1]),
                                  waits=[cb, attn_guard[i]])
                        sm_guard[r2] = cx
                        vr = em.v(lambda e, o=RR6[:, i:i + 1],
                                  i_=RS6[:, i:i + 1]:
                                  e.reciprocal(o, i_),
                                  waits=[cx])
                        softmax_done[i] = em.v(
                            lambda e, o=ATTN6[i], ii=i:
                            e.tensor_scalar(o, o, RR6[:, ii:ii + 1], None,
                                            ALU.mult), waits=[vr])
                    ngs = 8 if w > 0 else 4
                    for gsc in range(ngs):
                        lastt = None
                        for i in range(4):
                            lastt = em.t(
                                lambda e,
                                o=P4a_bf[:, 128 * i:128 * (i + 1)],
                                i_=ATTN6[i][:, kcl + 128 * gsc:
                                            kcl + 128 * (gsc + 1)]:
                                e.transpose(o, i_, ctx.IDENT),
                                waits=([softmax_done[i], ev_guard, pg["a"]]
                                       if i == 0 else [softmax_done[i]]))
                        ev_guard = em.v(
                            lambda e, o=ATTS6[:, 512 * gsc:512 * (gsc + 1)],
                            i_=P4a_bf[:, 0:512]:
                            e.tensor_copy(o, i_), waits=[lastt])
                    attn_guard = [lastt] * 4
                    lastv = None
                    for gsc in range(ngs):
                        g = 4 * (w - 1) + kcl // 128 + gsc
                        lastv = em.t(
                            lambda e, o=P4b[0:64, 1024:1536],
                            l_=VT6[g][:, 64 * h:64 * h + 64],
                            r_=ATTS6[:, 512 * gsc:512 * (gsc + 1)],
                            st=(gsc == 0), sp=(gsc == ngs - 1):
                            e.matmul(o, l_, r_, start=st, stop=sp),
                            waits=[ev_guard, pg["b"]] if gsc == 0 else ())
                    ce = em.c(lambda e, o=OH[:, 512 * w:512 * (w + 1)],
                              i_=P4b[0:64, 1024:1536]:
                              e.activation(o, i_, AF.Copy),
                              waits=[lastv, hh_guard])
                    pg["a"] = ce
                    pg["b"] = ce
                    ev_guard = None
                sh = em.sdma(ctx.HS[h], OH, waits=[pg["a"]])
                if b == 0 and l == 0 and h == 0:
                    dbg_b(ctx, 7, OH, waits=[pg["a"]])
                hh_guard = sh
            em.barrier()
            ar.reset(mkq)

            if LIM < 5:
                ar.reset(mkl)
                continue
            if DEBUG and b == 0 and l == 0:
                dbg_f(ctx, 8, XLOC[0])
                em.barrier()
            OL = ar.tiles(4, [128, T], BF16)
            WO6, gwo = load_w_cast(ctx, ctx.inp["f_wo"][l], IN2, D)
            hl = None
            for cc in range(4):
                hl = em.sdma(OL[cc], ctx.HS.rearrange("h p t -> (h p) t")
                             [128 * cc:128 * (cc + 1), :])
            if b == 0 and l == 0:
                dbg_b(ctx, 9, OL[0], waits=[hl])
                dbg_b(ctx, 10, OL[3], waits=[hl])
                dbg_b(ctx, 11, WO6[:, 0:T], waits=[gwo])
                if DEBUG:
                    em.sdma(ctx.DBGB[0], ctx.HS.rearrange("h p t -> (h p) t")
                            [0:128, :])
                    dbg_b(ctx, 1, OL[1], waits=[hl])
                    dbg_b(ctx, 2, OL[2], waits=[hl])
                    em.barrier()
            XWT = ar.tiles(2, [128, T], F32)
            pgo = {"a": None, "b": None}
            xwg = [None, None]
            for dc in range(DC):
                key = "a" if dc % 2 == 0 else "b"
                rr = dc % 2
                ps = P4a if key == "a" else P4b
                last = None
                for ns in range(4):
                    for cc in range(4):
                        last = em.t(
                            lambda e, o=ps[:, 512 * ns:512 * (ns + 1)],
                            l_=WO6[:, cc * D + 128 * dc: cc * D + 128 * dc + 128],
                            r_=OL[cc][:, 512 * ns:512 * (ns + 1)],
                            st=(cc == 0), sp=(cc == 3):
                            e.matmul(o, l_, r_, start=st, stop=sp),
                            waits=([pgo[key], gwo, hl]
                                   if (ns == 0 and cc == 0) else ()))
                v1_ = em.v(
                    lambda e, o=XWT[rr], i_=ps, x_=XLOC[dc]:
                    e.scalar_tensor_tensor(o, i_, 1.0, x_,
                                           ALU.mult, ALU.add),
                    waits=[last, xwg[rr]])
                pgo[key] = v1_
                xwg[rr] = em.v(lambda e, o=XLOC[dc], i_=XWT[rr]:
                               e.tensor_copy(o, i_))
                if DEBUG and b == 0 and l == 0 and dc == 0:
                    em.barrier()
                    dbg_f(ctx, 9, XWT[0])
                    dbg_f(ctx, 10, XLOC[0])
                    em.barrier()
            em.barrier()
            if b == 0 and l == 0:
                dbg_f(ctx, 6, XLOC[0])
                dbg_f(ctx, 7, XLOC[4])
                em.barrier()
            ar.reset(mkl)

            if LIM < 6:
                continue
            mkf = ar.mark()
            XL6 = ar.tiles(DC, [128, T], BF16)
            mkg = ar.mark()
            XG6 = ar.tiles(DC, [128, T], BF16)
            for dc in range(DC):
                if dc < 4:
                    em.v(lambda e, o=XG6[dc], i_=XLOC[dc]:
                         e.tensor_copy(o, i_))
                else:
                    em.v(lambda e, o=XG6[dc][:, 0:2], i_=XG6[dc][:, 0:2]:
                         e.memset(o, 0.0))
                    em.v(lambda e, o=XG6[dc][:, 1:T], i_=XLOC[dc][:, 0:T - 1]:
                         e.tensor_copy(o, i_))
            em.barrier()
            ln_tiles(ctx, XG6, XL6,
                     lambda dc, l=l: LNG[:, 8 * l + dc:8 * l + dc + 1],
                     lambda dc, l=l: LNB[:, 8 * l + dc:8 * l + dc + 1])
            em.barrier()
            ar.reset(mkg)
            H2 = ar.tiles(MC, [128, 1024], BF16)
            GT6 = ar.tile([128, 1024], BF16)
            XW2 = ar.tiles(2, [128, 512], F32)
            W1C = ar.tiles(2, [128, DC * 128], BF16)
            W1G = ar.tiles(2, [128, DC * 128], BF16)
            W2C = ar.tiles(2, [128, D], BF16)
            AT6 = ar.tile([128, 1024], F32)
            B1A = ar.tile([128, MC], F32)
            B1G = ar.tile([128, MC], F32)
            B26 = ar.tile([128, 8], F32)
            em.sdma(B1A, ctx.inp["f_b1a_pp"][l])
            em.sdma(B1G, ctx.inp["f_b1g_pp"][l])
            em.sdma(B26, ctx.inp["f_b2_pp"][l])
            em.barrier()
            for th in range(2):
                tlo = 1024 * th
                wg = [None, None]
                evg = [None, None]
                h2g = None
                for mc in range(MC):
                    rw = mc % 2
                    ga = em.gdma(W1C[rw].rearrange("p (kc m) -> p kc m",
                                                   kc=DC),
                                 ctx.inp["f_w1a"][l]
                                 .rearrange("(kc p) m -> p kc m", p=128)
                                 [:, :, 128 * mc:128 * (mc + 1)],
                                 waits=[wg[rw]])
                    gg = em.gdma(W1G[rw].rearrange("p (kc m) -> p kc m",
                                                   kc=DC),
                                 ctx.inp["f_w1g"][l]
                                 .rearrange("(kc p) m -> p kc m", p=128)
                                 [:, :, 128 * mc:128 * (mc + 1)],
                                 waits=[wg[rw]])
                    lasta = None
                    for ns in range(2):
                        for dc in range(DC):
                            lasta = em.t(
                                lambda e, o=P4a[:, 512 * ns:512 * (ns + 1)],
                                l_=W1C[rw][:, 128 * dc:128 * (dc + 1)],
                                r_=XL6[dc][:, tlo + 512 * ns:
                                           tlo + 512 * (ns + 1)],
                                st=(dc == 0), sp=(dc == DC - 1):
                                e.matmul(o, l_, r_, start=st, stop=sp),
                                waits=([ga, evg[0]]
                                       if (ns == 0 and dc == 0) else ()))
                    lastg = None
                    for ns in range(2):
                        for dc in range(DC):
                            lastg = em.t(
                                lambda e, o=P4b[:, 512 * ns:512 * (ns + 1)],
                                l_=W1G[rw][:, 128 * dc:128 * (dc + 1)],
                                r_=XL6[dc][:, tlo + 512 * ns:
                                           tlo + 512 * (ns + 1)],
                                st=(dc == 0), sp=(dc == DC - 1):
                                e.matmul(o, l_, r_, start=st, stop=sp),
                                waits=([gg, evg[1]]
                                       if (ns == 0 and dc == 0) else ()))
                    ea = em.v(lambda e, o=AT6, i_=P4a[:, 0:1024],
                              bb=B1A[:, mc:mc + 1]:
                              e.tensor_scalar(o, i_, bb, None, ALU.add),
                              waits=[lasta, h2g])
                    evg[0] = ea
                    eg = em.c(lambda e, o=GT6, i_=P4b[:, 0:1024],
                              bb=B1G[:, mc:mc + 1]:
                              e.activation(o, i_, AF.Gelu, bias=bb),
                              waits=[lastg, h2g])
                    evg[1] = eg
                    h2g = em.v(lambda e, o=H2[mc], a=AT6:
                               e.tensor_tensor(o, a, GT6, ALU.mult),
                               waits=[eg])
                    if b == 0 and l == 0 and th == 0 and mc in (0, 21):
                        dbg_b(ctx, 1 if mc == 0 else 2, H2[mc], waits=[h2g])
                        if mc == 0:
                            dbg_b(ctx, 0, XL6[0], waits=[h2g])
                            dbg_f(ctx, 1, AT6, waits=[ea])
                    wg[rw] = lasta if lastg is None else lastg
                for sub in range(2):
                    slo = tlo + 512 * sub
                    w2g = [None, None]
                    lasto = None
                    for mc in range(MC):
                        rw = mc % 2
                        g2 = em.gdma(W2C[rw],
                                     ctx.inp["f_w2p"][l]
                                     [128 * mc:128 * (mc + 1), :],
                                     waits=[w2g[rw]])
                        for dc in range(DC):
                            half = P4a if dc < 4 else P4b
                            psb = half[:, 512 * (dc % 4):512 * (dc % 4) + 512]
                            lasto = em.t(
                                lambda e, o=psb,
                                l_=W2C[rw][:, 128 * dc:128 * (dc + 1)],
                                r_=H2[mc][:, 512 * sub:512 * (sub + 1)],
                                st=(mc == 0), sp=(mc == MC - 1):
                                e.matmul(o, l_, r_, start=st, stop=sp),
                                waits=([g2, h2g, evg[0], evg[1]]
                                       if (mc == 0 and dc == 0) else
                                       ([g2] if dc == 0 else ())))
                        w2g[rw] = lasto
                    vr6 = None
                    for dc in range(DC):
                        half = P4a if dc < 4 else P4b
                        psb = half[:, 512 * (dc % 4):512 * (dc % 4) + 512]
                        em.v(lambda e, o=XW2[dc % 2], i_=psb,
                             bb=B26[:, dc:dc + 1], dc=dc, slo=slo:
                             e.scalar_tensor_tensor(
                                 o, i_, bb, XLOC[dc][:, slo:slo + 512],
                                 ALU.add, ALU.add),
                             waits=[lasto] if dc == 0 else ())
                        vr6 = em.v(lambda e, o=XLOC[dc][:, slo:slo + 512],
                                   i_=XW2[dc % 2]: e.tensor_copy(o, i_))
                    evg = [vr6, vr6]
                    if b == 0 and l == 0 and th == 0 and sub == 0:
                        dbg_f(ctx, 0, XLOC[0], waits=[vr6])
                em.barrier()
            ar.reset(mkf)
        em.barrier()
        if b == 0:
            dbg_f(ctx, 4, XLOC[0])
            dbg_f(ctx, 5, XLOC[4])
        em.barrier()
        XF = ar.tiles(2, [128, T], F32)
        XO = ar.tiles(2, [128, T], F32)
        st_guard = [None, None]
        for dc in range(DC):
            r = dc % 2
            ldx = em.sdma(XF[r], src_of(b, dc), waits=[st_guard[r]])
            va = em.v(lambda e, o=XO[r], a=XLOC[dc], bb=XF[r]:
                      e.tensor_tensor(o, a, bb, ALU.add),
                      waits=[ldx, st_guard[r]])
            st_guard[r] = em.sdma(dst_of(b, dc), XO[r], waits=[va])
        em.barrier()
        ar.reset(mkb)
    ar.reset(stage_mk)


def stage_sl7(ctx, src_of, dst_of):
    em, ar = ctx.em, ctx.ar
    P4a, P4b = ctx.P4a, ctx.P4b
    P4a_bf = ctx.P4a_bf
    stage_mk = ar.mark()
    H4 = 4096

    for b in range(B):
        mkb = ar.mark()
        XN = ar.tiles(DC, [128, T], BF16)
        vd = ln_x_norm(ctx, lambda dc: src_of(b, dc), XN)
        for dc in range(DC):
            em.sdma(ctx.XNS[b, 128 * dc:128 * (dc + 1), :], XN[dc],
                    waits=[vd])
        em.barrier()
        ar.reset(mkb)

    for ti, typ in enumerate(("q", "k", "v")):
        mkt = ar.mark()
        W1, g1 = load_w_cast(ctx, ctx.inp[f"b_{typ}_w1"], D, H4)
        W2, g2 = load_w_cast(ctx, ctx.inp[f"b_{typ}_w2"], H4, D)
        B1 = ar.tile([128, 32], F32)
        em.sdma(B1, ctx.inp[f"b_{typ}_b1_pp"])
        B2 = ar.tile([128, 8], F32)
        em.sdma(B2, ctx.inp[f"b_{typ}_b2_pp"])
        em.barrier()
        XT8 = ar.tiles(2, [128, DC, 512], BF16)
        H1S = ar.tile([128, 32 * 512], BF16)
        QEV = ar.tiles(2, [128, 512], BF16)
        VEV = ar.tiles(2, [128, D], BF16)
        ld_guard = [None, None]
        ev_guard = [None, None]
        vev_guard = [None, None]
        h1_guard = None
        for b in range(B):
            for ts in range(4):
                r = (b * 4 + ts) % 2
                lo = 512 * ts
                ldx = em.sdma(
                    XT8[r], ctx.XNS[b].rearrange("(dc p) t -> p dc t", p=128)
                    [:, :, lo:lo + 512], waits=[ld_guard[r]])
                lasth_all = None
                h1ev = [h1_guard, h1_guard]
                for hc in range(32):
                    key = hc % 2
                    ps = (P4a if key == 0 else P4b)[:, 0:512]
                    last = None
                    for dc in range(DC):
                        last = em.t(
                            lambda e, o=ps,
                            l=W1[:, dc * H4 + 128 * hc: dc * H4 + 128 * hc + 128],
                            r_=XT8[r][:, dc, :].squeeze(),
                            st=(dc == 0), sp=(dc == DC - 1):
                            e.matmul(o, l, r_, start=st, stop=sp),
                            waits=(([ldx, g1] if hc < 2 else [])
                                   + [h1ev[key]] if dc == 0 else ()))
                    lasth_all = em.c(
                        lambda e, o=H1S[:, 512 * hc:512 * (hc + 1)], i_=ps,
                        bb=B1[:, hc:hc + 1]:
                        e.activation(o, i_, AF.Gelu, bias=bb),
                        waits=[last])
                    h1ev[key] = lasth_all
                h1_guard = lasth_all
                if typ != "v":
                    for cc in range(DC):
                        key = cc % 2
                        ps = (P4a if key == 0 else P4b)[:, 512:1024]
                        last = None
                        for hc in range(32):
                            last = em.t(
                                lambda e, o=ps,
                                l=W2[:, hc * D + 128 * cc: hc * D + 128 * cc + 128],
                                r_=H1S[:, 512 * hc:512 * (hc + 1)],
                                st=(hc == 0), sp=(hc == 31):
                                e.matmul(o, l, r_, start=st, stop=sp),
                                waits=([lasth_all, g2, ev_guard[key]]
                                       if hc == 0 else ()))
                        rr = cc % 2
                        ev = em.c(lambda e, o=QEV[rr], i_=ps,
                                  bb=B2[:, cc:cc + 1]:
                                  e.activation(o, i_, AF.Identity, bias=bb),
                                  waits=[last, vev_guard[rr]])
                        ev_guard[key] = ev
                        dst = ctx.QS if typ == "q" else ctx.KS
                        vev_guard[rr] = em.sdma(
                            dst[b, 128 * cc:128 * (cc + 1), lo:lo + 512],
                            QEV[rr], waits=[ev])
                else:
                    for tq in range(4):
                        key = tq % 2
                        ps = (P4a if key == 0 else P4b)[:, 1024:2048]
                        last = None
                        for hc in range(32):
                            for ns in range(2):
                                last = em.t(
                                    lambda e, o=ps[:, 512 * ns:512 * (ns + 1)],
                                    l=H1S[:, 512 * hc + 128 * tq:
                                          512 * hc + 128 * tq + 128],
                                    r_=W2[:, hc * D + 512 * ns:
                                          hc * D + 512 * ns + 512],
                                    st=(hc == 0), sp=(hc == 31):
                                    e.matmul(o, l, r_, start=st, stop=sp),
                                    waits=([lasth_all, g2, ev_guard[key]]
                                           if (hc == 0 and ns == 0) else ()))
                        rr = tq % 2
                        ev = em.c(lambda e, o=VEV[rr], i_=ps:
                                  e.activation(o, i_, AF.Copy),
                                  waits=[last, vev_guard[rr]])
                        ev_guard[key] = ev
                        vev_guard[rr] = em.sdma(
                            ctx.VS[b, lo + 128 * tq:lo + 128 * (tq + 1), :],
                            VEV[rr], waits=[ev])
                ld_guard[r] = lasth_all
        em.barrier()
        ar.reset(mkt)

    PROJW, gp = load_w_cast(ctx, ctx.inp["b_proj_w"], D, D)
    PBJ = ar.tile([128, 8], F32)
    em.sdma(PBJ, ctx.inp["b_proj_b_pp"])
    VB2 = ar.tile([128, 8], F32)
    em.sdma(VB2, ctx.inp["b_v_b2_pp"])
    em.barrier()
    for b in range(B):
        mkb = ar.mark()
        QT = ar.tiles(DC, [128, T], BF16)
        KT = ar.tiles(DC, [128, T], BF16)
        VT = ar.tiles(16, [128, D], BF16)
        lq = None
        for dc in range(DC):
            em.sdma(QT[dc], ctx.QS[b, 128 * dc:128 * (dc + 1), :])
            em.sdma(KT[dc], ctx.KS[b, 128 * dc:128 * (dc + 1), :])
        for tc in range(16):
            lq = em.sdma(VT[tc], ctx.VS[b, 128 * tc:128 * (tc + 1), :])
        ATTN = ar.tiles(4, [128, T], BF16)
        ATTS = ar.tile([128, 16 * 512], BF16)
        NM = ar.tile([128, 64], F32)
        RS = ar.tile([128, 16], F32)
        RR = ar.tile([128, 16], F32)
        HH = ar.tile([64, T], BF16)
        em.barrier()

        hh_guard = None
        pg = {"a": None, "b": None}
        attn_guard = [None] * 4
        ev_guard = None
        for h in range(16):
            tile_i, off = h // 2, 64 * (h % 2)
            Qh = QT[tile_i][off:off + 64, :]
            Kh = KT[tile_i][off:off + 64, :]
            for sl in range(4):
                softmax_done = [None] * 4
                for i in range(4):
                    tc = 4 * sl + i
                    key = "a" if i % 2 == 0 else "b"
                    ps = P4a if key == "a" else P4b
                    last = None
                    for ns in range(4):
                        last = em.t(
                            lambda e, o=ps[:, 512 * ns:512 * (ns + 1)],
                            l=Qh[:, 128 * tc:128 * (tc + 1)],
                            r_=Kh[:, 512 * ns:512 * (ns + 1)]:
                            e.matmul(o, l, r_, start=True, stop=True),
                            waits=[pg[key]] if ns == 0 else ())
                    vm = em.v(lambda e, o=NM[:, tc:tc + 1], i_=ps:
                              e.reduce_max(o, i_, axis=AX.X, negate=True),
                              waits=[last])
                    vm2 = em.c(lambda e, o=NM[:, 16 + tc:17 + tc],
                               i_=NM[:, tc:tc + 1]:
                               e.mul(o, i_, 0.125), waits=[vm])
                    cx = em.c(lambda e, o=ATTN[i], i_=ps, tc=tc, ii=i:
                              e.activation(o, i_, AF.Exp, scale=0.125,
                                           bias=NM[:, 16 + tc:17 + tc],
                                           accum_out=RS[:, ii:ii + 1]),
                              waits=[vm2, attn_guard[i]])
                    pg[key] = cx
                    vr = em.v(lambda e, o=RR[:, i:i + 1],
                              i_=RS[:, i:i + 1]:
                              e.reciprocal(o, i_), waits=[cx])
                    softmax_done[i] = em.v(
                        lambda e, o=ATTN[i], i=i:
                        e.tensor_scalar(o, o, RR[:, i:i + 1], None, ALU.mult),
                        waits=[vr])
                for sc in range(16):
                    lastt = None
                    for i in range(4):
                        lastt = em.t(
                            lambda e, o=P4a_bf[:, 128 * i:128 * (i + 1)],
                            i_=ATTN[i][:, 128 * sc:128 * (sc + 1)]:
                            e.transpose(o, i_, ctx.IDENT),
                            waits=([softmax_done[i], ev_guard, pg["a"]]
                                   if i == 0 else [softmax_done[i]]))
                    ev_guard = em.v(
                        lambda e, o=ATTS[:, 512 * sc:512 * (sc + 1)],
                        i_=P4a_bf[:, 0:512]:
                        e.tensor_copy(o, i_), waits=[lastt])
                    attn_guard = [lastt] * 4 if sc == 15 else attn_guard
                lastv = None
                for sc in range(16):
                    lastv = em.t(
                        lambda e, o=P4b[0:64, 512:1024],
                        l=VT[sc][:, 64 * h:64 * h + 64],
                        r_=ATTS[:, 512 * sc:512 * (sc + 1)],
                        st=(sc == 0), sp=(sc == 15):
                        e.matmul(o, l, r_, start=st, stop=sp),
                        waits=([ev_guard, pg["b"], lq] if sc == 0 else ()))
                ce = em.c(lambda e, o=HH[:, 512 * sl:512 * (sl + 1)],
                          i_=P4b[0:64, 512:1024],
                          bb=VB2[off // 64 if False else 0:1, 0:1]:
                          e.activation(o, i_, AF.Copy), waits=[lastv, hh_guard])
                pg["a"] = ce
                pg["b"] = ce
                ev_guard = None
            sh = em.sdma(ctx.HS[h], HH, waits=[pg["a"]])
            hh_guard = sh
        em.barrier()

        ar.reset(mkb)
        OTL = ar.tiles(DC, [128, T], BF16)
        XF = ar.tiles(2, [128, T], F32)
        XNEW = ar.tiles(2, [128, T], F32)
        hl = None
        for cc in range(DC):
            hl = em.sdma(OTL[cc], ctx.HS.rearrange("h p t -> (h p) t")
                         [128 * cc:128 * (cc + 1), :])
        vb = None
        for cc in range(DC):
            vb = em.v(lambda e, o=OTL[cc], bb=VB2[:, cc:cc + 1]:
                      e.tensor_scalar(o, o, bb, None, ALU.add),
                      waits=[hl] if cc == 0 else ())
        st_guard = [None, None]
        xf_guard = [None, None]
        pgo = {"a": None, "b": None}
        for dc in range(DC):
            r = dc % 2
            key = "a" if dc % 2 == 0 else "b"
            ps = P4a if key == "a" else P4b
            ldx = em.sdma(XF[r], src_of(b, dc), waits=[xf_guard[r]])
            last = None
            for ns in range(4):
                for cc in range(DC):
                    last = em.t(
                        lambda e, o=ps[:, 512 * ns:512 * (ns + 1)],
                        l=PROJW[:, cc * D + 128 * dc: cc * D + 128 * dc + 128],
                        r_=OTL[cc][:, 512 * ns:512 * (ns + 1)],
                        st=(cc == 0), sp=(cc == DC - 1):
                        e.matmul(o, l, r_, start=st, stop=sp),
                        waits=([pgo[key], gp, vb]
                               if (ns == 0 and cc == 0) else ()))
            vres = em.v(lambda e, o=XNEW[r], i_=ps,
                        bb=PBJ[:, dc:dc + 1], x=XF[r]:
                        e.scalar_tensor_tensor(o, i_, bb, x, ALU.add, ALU.add),
                        waits=[last, ldx, st_guard[r]])
            pgo[key] = vres
            xf_guard[r] = vres
            st_guard[r] = em.sdma(dst_of(b, dc), XNEW[r], waits=[vres])
        em.barrier()
        ar.reset(mkb)
    ar.reset(stage_mk)


def stage_emit(ctx, src_of):
    em, ar = ctx.em, ctx.ar
    mk = ar.mark()
    TL = ar.tiles(2, [128, T], F32)
    g = [None, None]
    for b in range(B):
        for dc in range(DC):
            r = (b * DC + dc) % 2
            ld = em.sdma(TL[r], src_of(b, dc), waits=[g[r]])
            g[r] = em.sdma(ctx.OUT[b, 128 * dc:128 * (dc + 1), :], TL[r],
                           waits=[ld])
    em.barrier()
    ar.reset(mk)


def build_nc(stages):
    nc = bass.Bass("TRN2", target_bir_lowering=False)
    ctx = Ctx = type("C", (), {})()
    ctx.nc = nc

    inp = {}
    def di(name, shape, dtype=F32):
        inp[name] = nc.dram_tensor(name, list(shape), dtype,
                                   kind="ExternalInput").ap()
    di("Xc", (B, D, T))
    if "sl5" in stages:
        di("L1_attn_a_wq", (D, D)); di("L1_attn_a_wk", (D, D))
        di("L1_attn_a_wv", (D, D)); di("L1_attn_a_wo", (D, D))
        di("L1_attn_a_bq_pp", (128, 8)); di("L1_attn_a_bk_pp", (128, 8))
        di("L1_attn_a_bv_pp", (128, 8)); di("L1_attn_a_bo_pp", (128, 8))
        di("mask5", (16, 128, T), U8)
    if "sl1" in stages:
        di("L0_W_in", (D, 4096))
        di("L0_b_in_64", (64, 64))
        di("L0_W_out", (D, D))
        di("L0_b_out_pp", (128, 8))
        ctx.HS = nc.dram_tensor("HS", [16, 64, T], BF16, kind="Internal").ap()
    if "sl6" in stages or "sl8" in stages:
        di("f_wqkv", (2, D, 1536))
        di("f_wo", (2, 512, D))
        di("f_w1a", (2, D, 2816))
        di("f_w1g", (2, D, 2816))
        di("f_w2p", (2, 2816, D))
        di("f_b1a_pp", (2, 128, 22))
        di("f_b1g_pp", (2, 128, 22))
        di("f_b2_pp", (2, 128, 8))
        di("f_lng_pp", (128, 16))
        di("f_lnb_pp", (128, 16))
        di("f_tabs", (128, 4 * T))
        di("f_qsks_pp", (128, 8))
        di("f_h8all", (128, 32), BF16)
        di("f_sel8", (8, 512), BF16)
        di("f_rperm", (128, 128), BF16)
        di("mask6", (4, 128, 1024), U8)
        di("mask6w0", (4, 128, 1024), U8)
        if "sl1" not in stages and "sl7" not in stages:
            ctx.HS = nc.dram_tensor("HS", [16, 64, T], BF16,
                                    kind="Internal").ap()
        ctx.RNS = nc.dram_tensor("RNS", [8, T], BF16, kind="Internal").ap()
    if "sl7" in stages:
        for typ in ("q", "k", "v"):
            di(f"b_{typ}_w1", (D, 4096))
            di(f"b_{typ}_w2", (4096, D))
            di(f"b_{typ}_b1_pp", (128, 32))
            di(f"b_{typ}_b2_pp", (128, 8))
        di("b_proj_w", (D, D))
        di("b_proj_b_pp", (128, 8))
        ctx.XNS = nc.dram_tensor("XNS", [B, D, T], BF16, kind="Internal").ap()
        ctx.QS = nc.dram_tensor("QS", [B, D, T], BF16, kind="Internal").ap()
        ctx.KS = nc.dram_tensor("KS", [B, D, T], BF16, kind="Internal").ap()
        ctx.VS = nc.dram_tensor("VS", [B, T, D], BF16, kind="Internal").ap()
        if "sl1" not in stages:
            ctx.HS = nc.dram_tensor("HS", [16, 64, T], BF16,
                                    kind="Internal").ap()
    if "sl3" in stages:
        di("router_w_pp", (128, 8))
        di("router_b_11", (1, 1))
    if "sl2" in stages or "sl4" in stages:
        di("phi_wT", (D, 2048))
        di("phi_b_bc", (128, 2048))
        di("expert_w", (64, D, 1024))
        di("expert_b", (64, 1024))
        ctx.CS = nc.dram_tensor("CS", [B, 16, 128, 2048], BF16,
                                kind="Internal").ap()
        ctx.CSUMS = nc.dram_tensor("CSUMS", [B, 128, 16 * 64], F32,
                                   kind="Internal").ap()
        ctx.CCIN = nc.dram_tensor("CCIN", [2048, D], F32, kind="Internal").ap()
        ctx.CCOUT = nc.dram_tensor("CCOUT", [2048, D], F32,
                                   kind="Internal").ap()
    ctx.inp = inp

    ctx.OUT = nc.dram_tensor("OUT", [B, D, T], F32, kind="ExternalOutput").ap()
    ctx.S = nc.dram_tensor("S", [B, D, T], F32, kind="Internal").ap()
    ctx.DBGF = ctx.DBGB = None
    if DEBUG:
        ctx.DBGF = nc.dram_tensor("DBGF", [12, 128, T], F32,
                                  kind="ExternalOutput").ap()
        ctx.DBGB = nc.dram_tensor("DBGB", [12, 128, T], BF16,
                                  kind="ExternalOutput").ap()

    ident = nc.inline_tensor(np.eye(128, dtype=ml_dtypes.bfloat16),
                             name="identc").ap()
    identf = nc.inline_tensor(np.eye(128, dtype=np.float32),
                              name="identfc").ap()
    inda_np = np.zeros((1, 128), dtype=ml_dtypes.bfloat16); inda_np[0, 0:64] = 1
    indb_np = np.zeros((1, 128), dtype=ml_dtypes.bfloat16); indb_np[0, 64:128] = 1
    inda = nc.inline_tensor(inda_np, name="indac").ap()
    indb = nc.inline_tensor(indb_np, name="indbc").ap()
    onesc = nc.inline_tensor(np.ones((128, 1), dtype=ml_dtypes.bfloat16),
                             name="onescc").ap()
    onesr = nc.inline_tensor(np.ones((1, 128), dtype=ml_dtypes.bfloat16),
                             name="onesrc").ap()

    em = Em(nc)
    ctx.em = em
    ar = Arena(nc, "arena", 205 * 1024)
    ctx.ar = ar
    ctx.P4a = nc.alloc_psum_tensor("p4a", [128, 2048], F32).ap()
    ctx.P4b = nc.alloc_psum_tensor("p4b", [128, 2048], F32).ap()
    ctx.P4a_bf = ctx.P4a.bitcast(BF16)
    ctx.P4b_bf = ctx.P4b.bitcast(BF16)

    ctx.IDENT = ar.tile([128, 128], BF16)
    ctx.IDENTF = ar.tile([128, 128], F32)
    ctx.ONESC = ar.tile([128, 1], BF16)
    ctx.ONESR = ar.tile([1, 128], BF16)
    ctx.INDA = ar.tile([1, 128], BF16)
    ctx.INDB = ar.tile([1, 128], BF16)
    em.sdma(ctx.IDENT, ident)
    em.sdma(ctx.IDENTF, identf)
    em.sdma(ctx.INDA, inda)
    em.sdma(ctx.INDB, indb)
    em.sdma(ctx.ONESC, onesc)
    em.sdma(ctx.ONESR, onesr)
    cvals = [1e-5, 1e-4, 1.0, 0.5, -0.5, 2.0, 1.0 / 32.0, 1.0 / 1024.0,
             8.0, 0.125, -2.0, -1.0]
    carr = np.tile(np.asarray(cvals, np.float32)[None, :], (128, 1))
    cbuf = nc.inline_tensor(carr, name="constpool").ap()
    CON = ar.tile([128, len(cvals)], F32)
    em.sdma(CON, cbuf)
    for i, v in enumerate(cvals):
        nc.const_aps.register(F32, float(v), CON[:, i:i + 1])
    ctx.CON = CON
    em.barrier()

    state = {"first": True}
    def src_of(b, dc):
        if state["first"]:
            return inp["Xc"][b, 128 * dc:128 * (dc + 1), :]
        return ctx.S[b, 128 * dc:128 * (dc + 1), :]
    def dst_of(b, dc):
        return ctx.S[b, 128 * dc:128 * (dc + 1), :]

    impl = {
        "sl5": lambda c, s, d: stage_sl5(c, s, d),
        "sl2": lambda c, s, d: stage_sl2(c, s, d, "sl2"),
        "sl4": lambda c, s, d: stage_sl2(c, s, d, "sl4"),
        "sl3": lambda c, s, d: stage_sl3(c, s, d),
        "sl1": lambda c, s, d: stage_sl1(c, s, d),
        "sl7": lambda c, s, d: stage_sl7(c, s, d),
        "sl6": lambda c, s, d: stage_sl6(c, s, d, "sl6"),
        "sl8": lambda c, s, d: stage_sl6(c, s, d, "sl8"),
    }
    for st in stages:
        if st in impl:
            impl[st](ctx, src_of, dst_of)
            state["first"] = False
    if state["first"]:
        stage_emit(ctx, lambda b, dc: inp["Xc"][b, 128 * dc:128 * (dc + 1), :])
    else:
        stage_emit(ctx, lambda b, dc: ctx.S[b, 128 * dc:128 * (dc + 1), :])

    with nc.allow_low_precision(
            reason="bf16 matmul operands sanctioned by problem tolerance"):
        with nc.Block() as blk:
            @blk.sync
            def _(e):
                for f in em.q["sync"]:
                    f(e)
            @blk.tensor
            def _(e):
                for f in em.q["tensor"]:
                    f(e)
            @blk.vector
            def _(e):
                for f in em.q["vector"]:
                    f(e)
            @blk.scalar
            def _(e):
                for f in em.q["scalar"]:
                    f(e)
            @blk.gpsimd
            def _(e):
                for f in em.q["gpsimd"]:
                    f(e)
    return nc


def host_inputs(inputs, stages):
    X = np.asarray(inputs["L0_attn_a_X"], np.float32)
    Xc = np.ascontiguousarray(
        X.reshape(NCORES, B, T, D).transpose(0, 1, 3, 2))
    shared = {}
    if "sl5" in stages:
        shared.update({
            "L1_attn_a_wq": np.ascontiguousarray(inputs["L1_attn_a_wq"], np.float32),
            "L1_attn_a_wk": np.ascontiguousarray(inputs["L1_attn_a_wk"], np.float32),
            "L1_attn_a_wv": np.ascontiguousarray(inputs["L1_attn_a_wv"], np.float32),
            "L1_attn_a_wo": np.ascontiguousarray(inputs["L1_attn_a_wo"], np.float32),
            "L1_attn_a_bq_pp": _pp(inputs["L1_attn_a_bq"]),
            "L1_attn_a_bk_pp": _pp(inputs["L1_attn_a_bk"]),
            "L1_attn_a_bv_pp": _pp(inputs["L1_attn_a_bv"]),
            "L1_attn_a_bo_pp": _pp(inputs["L1_attn_a_bo"]),
        })
        row = np.arange(128)[:, None]
        col = np.arange(T)[None, :]
        m5 = np.stack([(col >= 128 * tc + row + 2).astype(np.uint8)
                       for tc in range(16)])
        shared["mask5"] = m5
    if "sl1" in stages:
        shared["L0_W_in"] = np.ascontiguousarray(inputs["L0_attn_a_W_in"],
                                                 dtype=np.float32)
        shared["L0_b_in_64"] = np.ascontiguousarray(
            np.asarray(inputs["L0_attn_a_b_in"], np.float32)
            .reshape(16, 4, 64).transpose(2, 0, 1).reshape(64, 64))
        shared["L0_W_out"] = np.ascontiguousarray(inputs["L0_attn_a_W_out"],
                                                  dtype=np.float32)
        shared["L0_b_out_pp"] = _pp(inputs["L0_attn_a_b_out"])
    if "sl6" in stages or "sl8" in stages:
        w1 = np.asarray(inputs["L1_ffn_w1"], np.float32)
        w2 = np.asarray(inputs["L1_ffn_w2"], np.float32)
        b1 = np.asarray(inputs["L1_ffn_b1"], np.float32)
        HID = 2730
        w1a = np.zeros((2, D, 2816), np.float32)
        w1g = np.zeros((2, D, 2816), np.float32)
        w1a[:, :, :HID] = w1[:, :, :HID]
        w1g[:, :, :HID] = w1[:, :, HID:]
        w2p = np.zeros((2, 2816, D), np.float32)
        w2p[:, :HID, :] = w2
        shared["f_wqkv"] = np.ascontiguousarray(inputs["L1_ffn_wqkv"],
                                                dtype=np.float32)
        shared["f_wo"] = np.ascontiguousarray(inputs["L1_ffn_wo"],
                                              dtype=np.float32)
        shared["f_w1a"] = w1a
        shared["f_w1g"] = w1g
        shared["f_w2p"] = w2p
        shared["f_b1a_pp"] = np.stack([_pp(b1[l, :HID], pad_to=2816)
                                       for l in range(2)])
        shared["f_b1g_pp"] = np.stack([_pp(b1[l, HID:], pad_to=2816)
                                       for l in range(2)])
        shared["f_b2_pp"] = np.stack([_pp(np.asarray(inputs["L1_ffn_b2"])[l])
                                      for l in range(2)])
        shared["f_lng_pp"] = np.concatenate(
            [_pp(np.asarray(inputs["L1_ffn_ln_g"])[l]) for l in range(2)], axis=1)
        shared["f_lnb_pp"] = np.concatenate(
            [_pp(np.asarray(inputs["L1_ffn_ln_b"])[l]) for l in range(2)], axis=1)
        half_idx = np.arange(0, 64, 2, dtype=np.float64)
        inv_freq = 1.0 / 10000.0 ** (half_idx / 64)
        tt = np.arange(T, dtype=np.float64)
        freqs = np.concatenate([tt[:, None] * inv_freq[None, :]] * 2, -1)
        basep = (half_idx + 0.4 * 64) / (1.4 * 64)
        power = (tt - T // 2) / 256.0
        scale = np.concatenate([basep[None, :] ** power[:, None]] * 2, -1)
        cosf, sinf = np.cos(freqs), np.sin(freqs)
        tabs = [cosf * scale, sinf * scale, cosf / scale, sinf / scale]
        shared["f_tabs"] = np.ascontiguousarray(np.concatenate(
            [np.tile(x.T, (2, 1)) for x in tabs], axis=1).astype(np.float32))
        qs = np.asarray(inputs["L1_ffn_q_scale"], np.float32)
        ks = np.asarray(inputs["L1_ffn_k_scale"], np.float32)
        cols = []
        for l in range(2):
            for vec in (qs[l], np.roll(qs[l], -32), ks[l], np.roll(ks[l], -32)):
                cols.append(np.tile(vec, 2))
        shared["f_qsks_pp"] = np.ascontiguousarray(np.stack(cols, axis=1))
        H8 = np.zeros((128, 32), np.float32)
        SEL = np.zeros((8, 512), np.float32)
        for j in range(4):
            H8[0:64, 8 * j + 2 * j] = 1.0
            H8[64:128, 8 * j + 2 * j + 1] = 1.0
            SEL[2 * j, 128 * j:128 * j + 64] = 1.0
            SEL[2 * j + 1, 128 * j + 64:128 * j + 128] = 1.0
        shared["f_h8all"] = H8.astype(ml_dtypes.bfloat16)
        shared["f_sel8"] = SEL.astype(ml_dtypes.bfloat16)
        Rp = np.zeros((128, 128), np.float32)
        for blk in range(2):
            o = 64 * blk
            for c in range(32):
                Rp[o + c + 32, o + c] = -1.0
            for c in range(32, 64):
                Rp[o + c - 32, o + c] = 1.0
        shared["f_rperm"] = Rp.astype(ml_dtypes.bfloat16)
        row = np.arange(128)[:, None]
        jk = np.arange(1024)[None, :]
        m6 = np.stack([((0 <= 128 * c4 + row - jk + 512)
                        & (128 * c4 + row - jk + 512 <= 512)).astype(np.uint8)
                       for c4 in range(4)])
        shared["mask6"] = m6
        m6w0 = m6.copy()
        m6w0[:, :, 0:512] = 0
        shared["mask6w0"] = m6w0
    if "sl7" in stages:
        for typ in ("q", "k", "v"):
            shared[f"b_{typ}_w1"] = np.ascontiguousarray(
                inputs[f"L1_attn_b_{typ}_w1"], dtype=np.float32)
            shared[f"b_{typ}_w2"] = np.ascontiguousarray(
                inputs[f"L1_attn_b_{typ}_w2"], dtype=np.float32)
            shared[f"b_{typ}_b1_pp"] = _pp(inputs[f"L1_attn_b_{typ}_b1"])
            shared[f"b_{typ}_b2_pp"] = _pp(inputs[f"L1_attn_b_{typ}_b2"])
        shared["b_proj_w"] = np.ascontiguousarray(
            inputs["L1_attn_b_proj_w"], dtype=np.float32)
        shared["b_proj_b_pp"] = _pp(inputs["L1_attn_b_proj_b"])
        shared["b_v_b2_pp"] = _pp(inputs["L1_attn_b_v_b2"])
    if "sl3" in stages:
        shared["router_w_pp"] = _pp(inputs["L0_attn_b_router_w"].reshape(-1))
        shared["router_b_11"] = np.asarray(
            inputs["L0_attn_b_router_b"], np.float32).reshape(1, 1)
    if "sl2" in stages or "sl4" in stages:
        shared["phi_wT"] = np.ascontiguousarray(
            np.asarray(inputs["L0_ffn_phi_w"], np.float32).T)
        shared["phi_b_bc"] = np.ascontiguousarray(np.tile(
            np.asarray(inputs["L0_ffn_phi_b"], np.float32)[None, :], (128, 1)))
        shared["expert_w"] = np.ascontiguousarray(
            inputs["L0_ffn_expert_w"], dtype=np.float32)
        shared["expert_b"] = np.ascontiguousarray(
            inputs["L0_ffn_expert_b"], dtype=np.float32)
    return Xc, shared


def kernel(**inputs):
    stages = list(STAGES)
    Xc, shared = host_inputs(inputs, stages)
    nc = build_nc(stages)
    in_maps = [dict(shared, Xc=np.ascontiguousarray(Xc[c]))
               for c in range(NCORES)]
    res = run_bass_kernel_spmd(nc, in_maps, core_ids=list(range(NCORES)))
    out = np.concatenate(
        [np.asarray(res.results[c]["OUT"]).transpose(0, 2, 1)
         for c in range(NCORES)], axis=0)
    return np.ascontiguousarray(out)
